# version 06
# baseline (speedup 1.0000x reference)
import math
import numpy as np

import concourse.bass as bass
import concourse.mybir as mybir
from concourse.bass_utils import run_bass_kernel_spmd

F32 = mybir.dt.float32
F32R = mybir.dt.float32r
BF16 = mybir.dt.bfloat16
I32 = mybir.dt.int32
AF = mybir.ActivationFunctionType
ALU = mybir.AluOpType

NCORE = 8
B, L, C = 4, 4096, 1024
TPC = B * L // NCORE
PPC = L // NCORE
NCH = C // 128
NTT = TPC // 512
H, H_KV, D = 16, 4, 64
K_SEL = 256
H_I, D_I = 4, 64
ROPE_BASE = 10000.0
SM_SCALE = D ** -0.5
I_L1 = 3072
LN_EPS = 1e-5
NEG_BIG = -30000.0
TWO_PI = 2.0 * math.pi

ENGINES = ("sync", "tensor", "vector", "scalar", "gpsimd")

_prog_cache = {}


class Seq:

    def __init__(self, nc):
        self.nc = nc
        keys = list(ENGINES) + [e + "_dma" for e in ENGINES]
        self.sems = {e: nc.alloc_semaphore(f"sq_{e}") for e in keys}
        self.cnt = {e: 0 for e in keys}

    def collective(self, blk, in_handle, out_handle):
        key = f"coll{len([k for k in self.cnt if k.startswith('coll')])}"
        sem = self.nc.alloc_semaphore("sq_" + key)
        self.sems[key] = sem
        self.cnt[key] = 0
        waits = [(self.sems[e], self.cnt[e]) for e in self.cnt if self.cnt[e] > 0]

        def body(eng):
            for s, val in waits:
                eng.wait_ge(s, val)
            eng.collective_compute(
                "AllGather", mybir.AluOpType.bypass,
                replica_groups=[list(range(NCORE))],
                ins=[in_handle.ap().opt()], outs=[out_handle.ap().opt()],
            ).then_inc(sem)

        blk.gpsimd(body)
        self.cnt[key] = 1

    def step(self, blk, ename):
        waits = [(self.sems[e], self.cnt[e]) for e in self.cnt if self.cnt[e] > 0]

        def deco(f):
            seq = self

            def body(eng):
                for s, val in waits:
                    eng.wait_ge(s, val)
                st = _Step(seq, ename, eng)
                last = f(eng, st)
                if last is not None:
                    last.then_inc(seq.sems[ename], 1)
                    seq.cnt[ename] += 1
                else:
                    assert st.marked, "step emitted nothing trackable"

            getattr(blk, ename)(body)

        return deco


class _Step:
    def __init__(self, seq, ename, eng):
        self.seq = seq
        self.ename = ename
        self.eng = eng
        self.marked = False

    def dma(self, out, in_, **kw):
        key = self.ename + "_dma"
        inst = self.eng.dma_start(out=out, in_=in_, **kw)
        inst.then_inc(self.seq.sems[key], 16)
        self.seq.cnt[key] += 16
        self.marked = True
        return inst

    def chain(self, inst):
        inst.then_inc(self.seq.sems[self.ename], 1)
        self.seq.cnt[self.ename] += 1
        self.eng.wait_ge(self.seq.sems[self.ename], self.seq.cnt[self.ename])
        self.marked = True
        return inst


def make_rot_T(nc, g, rt, st):
    st.chain(g.memset(rt, 0.0))
    last = None
    for blk_ in (0, 64):
        g.affine_select(out=rt[:, blk_:blk_ + 32], in_=rt[:, blk_:blk_ + 32],
                        compare_op=ALU.not_equal, fill=-1.0,
                        base=-(blk_ + 32), pattern=[[-1, 32]], channel_multiplier=1)
        last = g.affine_select(out=rt[:, blk_ + 32:blk_ + 64], in_=rt[:, blk_ + 32:blk_ + 64],
                               compare_op=ALU.not_equal, fill=1.0,
                               base=-blk_, pattern=[[-1, 32]], channel_multiplier=1)
    return last


def make_ident(nc, g, ident, st):
    st.chain(g.memset(ident, 0.0))
    return g.affine_select(out=ident, in_=ident, compare_op=ALU.not_equal, fill=1.0,
                           base=0, pattern=[[-1, ident.shape[1]]], channel_multiplier=1)



def emit_A(nc, blk, sq, io):
    xT = io["xT"]
    win = {}
    for name, cols in [("wq", 1024), ("wk", 256), ("wvv", 256), ("wvg", 256),
                       ("wqi", 256), ("wki", 64), ("wwi", 4), ("wog", 1024)]:
        win[name] = (io[name], cols)
    posoff = io["posoff"]
    bvg = io["bvg"]
    bog = io["bog"]
    kv_loc = io["kv_loc"]
    qrT = io["qrT"]
    qiT = io["qiT"]
    g1T = io["g1T"]
    xnT = io["xnT"]
    wI_tok = io["wI_tok"]

    woff = {}
    off = 0
    for name in ("wq", "wk", "wvv", "wvg", "wqi", "wki", "wwi", "wog"):
        woff[name] = off
        off += win[name][1]
    WCOLS = off

    from contextlib import ExitStack
    with ExitStack() as _ctx:
        wsb = _ctx.enter_context(nc.sbuf_tensor("Awsb", [128, NCH * WCOLS], BF16))
        xtt = _ctx.enter_context(nc.sbuf_tensor("Axtt", [128, NCH * 512], F32))
        xntt = _ctx.enter_context(nc.sbuf_tensor("Axntt", [128, NCH * 512], BF16))
        x2 = _ctx.enter_context(nc.sbuf_tensor("Ax2", [128, 512], F32))
        onesf = _ctx.enter_context(nc.sbuf_tensor("Aonesf", [128, 1], F32))
        identb = _ctx.enter_context(nc.sbuf_tensor("Aidentb", [128, 128], BF16))
        srow = _ctx.enter_context(nc.sbuf_tensor("Asrow", [1, 3 * 512], F32))
        mb = _ctx.enter_context(nc.sbuf_tensor("Amb", [128, 512], F32))
        rb = _ctx.enter_context(nc.sbuf_tensor("Arb", [128, 512], F32))
        rotT = _ctx.enter_context(nc.sbuf_tensor("ArotT", [128, 128], BF16))
        costab = _ctx.enter_context(nc.sbuf_tensor("Acostab", [128, 512], BF16))
        sintab = _ctx.enter_context(nc.sbuf_tensor("Asintab", [128, 512], BF16))
        tsc = _ctx.enter_context(nc.sbuf_tensor("Atsc", [128, 512], F32))
        tsc2 = _ctx.enter_context(nc.sbuf_tensor("Atsc2", [128, 512], F32))
        tin = _ctx.enter_context(nc.sbuf_tensor("Atin", [128, 512], I32))
        ifr = _ctx.enter_context(nc.sbuf_tensor("Aifr", [128, 1], F32))
        pidf = _ctx.enter_context(nc.sbuf_tensor("Apidf", [128, 1], F32))
        epsc = _ctx.enter_context(nc.sbuf_tensor("Aepsc", [1, 1], F32))
        onesr = _ctx.enter_context(nc.sbuf_tensor("Aonesr", [1, 128], F32))
        qsb = _ctx.enter_context(nc.sbuf_tensor("Aqsb", [128, 512], BF16))
        rosb = _ctx.enter_context(nc.sbuf_tensor("Arosb", [128, 512], BF16))
        t1 = _ctx.enter_context(nc.sbuf_tensor("At1", [128, 512], F32))
        gsb = _ctx.enter_context(nc.sbuf_tensor("Agsb", [128, 512], BF16))
        biasb = _ctx.enter_context(nc.sbuf_tensor("Abiasb", [128, 10], F32))
        wisb = _ctx.enter_context(nc.sbuf_tensor("Awisb", [128, 4], F32))
        ktt = _ctx.enter_context(nc.sbuf_tensor("Aktt", [128, 5 * 512], BF16))
        kvsb = _ctx.enter_context(nc.sbuf_tensor("Akvsb", [128, 4 * 576], BF16))
        ps0 = _ctx.enter_context(nc.psum_tensor("Aps0", [128, 512], F32))
        ps1 = _ctx.enter_context(nc.psum_tensor("Aps1", [128, 512], F32))
        ps2 = _ctx.enter_context(nc.psum_tensor("Aps2", [128, 512], F32))
        prow = _ctx.enter_context(nc.psum_tensor("Aprow", [1, 1024], F32))
        pqt = _ctx.enter_context(nc.psum_tensor("Apqt", [128, 4], F32))
        ptr = _ctx.enter_context(nc.psum_tensor("Aptr", [128, 640], BF16))
        wv_ = wsb.ap().rearrange("p (k c) -> p k c", k=NCH)
        xv_ = xtt.ap().rearrange("p (c t) -> p c t", c=NCH)
        xnv = xntt.ap().rearrange("p (c t) -> p c t", c=NCH)
        ktv = ktt.ap().rearrange("p (c t) -> p c t", c=5)

        def W(name, k, c0, c1):
            return wv_[:, k, woff[name] + c0:woff[name] + c1]

        @sq.step(blk, "gpsimd")
        def _(eng, st):
            for name in woff:
                wap, cols = win[name]
                for k in range(NCH):
                    st.dma(wv_[:, k, woff[name]:woff[name] + cols],
                           wap[128 * k:128 * (k + 1), :])
            st.dma(pidf[:, :], posoff[:, :])
            st.dma(biasb[:, 0:1], bvg[0:1, 0:128].rearrange("o p -> p o"))
            st.dma(biasb[:, 1:2], bvg[0:1, 128:256].rearrange("o p -> p o"))
            for cc in range(8):
                st.dma(biasb[:, 2 + cc:3 + cc],
                       bog[0:1, 128 * cc:128 * (cc + 1)].rearrange("o p -> p o"))

        @sq.step(blk, "gpsimd")
        def _(eng, st):
            make_rot_T(nc, eng, rotT[:, :], st)
            make_ident(nc, eng, identb[:, :], st)
            eng.iota(tin[:, 0:1], pattern=[[0, 1]], base=0, channel_multiplier=1)
            return eng.iota(tin[:, 1:2], pattern=[[0, 1]], base=0, channel_multiplier=1)

        @sq.step(blk, "vector")
        def _(eng, st):
            eng.memset(onesf[:, :], 1.0)
            eng.memset(epsc[:, :], LN_EPS)
            eng.memset(onesr[:, :], 1.0)
            st.chain(eng.tensor_copy(out=t1[:, 0:1], in_=tin[:, 0:1]))
            return eng.tensor_copy(out=ifr[:, :], in_=t1[:, 0:1])

        for thr in (32.0, 64.0, 96.0):
            @sq.step(blk, "vector")
            def _(eng, st, thr=thr):
                st.chain(eng.tensor_scalar(out=tsc[:, 0:1], in0=t1[:, 0:1], scalar1=thr,
                                           scalar2=-32.0, op0=ALU.is_ge, op1=ALU.mult))
                return eng.tensor_tensor(out=ifr[:, :], in0=ifr[:, :],
                                         in1=tsc[:, 0:1], op=ALU.add)

        @sq.step(blk, "scalar")
        def _(eng, st):
            return eng.activation(out=ifr[:, :], in_=ifr[:, :], func=AF.Exp,
                                  scale=-math.log(ROPE_BASE) / 32.0)

        @sq.step(blk, "gpsimd")
        def _(eng, st):
            return eng.iota(tin[:, :], pattern=[[1, 512]], base=0, channel_multiplier=0)

        @sq.step(blk, "vector")
        def _(eng, st):
            return eng.tensor_copy(out=tsc[:, :], in_=tin[:, :])

        @sq.step(blk, "vector")
        def _(eng, st):
            return eng.tensor_scalar(out=tsc[:, :], in0=tsc[:, :],
                                     scalar1=pidf[:, 0:1], scalar2=None, op0=ALU.add)

        @sq.step(blk, "vector")
        def _(eng, st):
            st.chain(eng.tensor_scalar(out=tsc[:, :], in0=tsc[:, :], scalar1=ifr[:, 0:1],
                                       scalar2=None, op0=ALU.mult))
            return eng.tensor_scalar(out=tsc2[:, :], in0=tsc[:, :],
                                     scalar1=1.0 / TWO_PI, scalar2=None, op0=ALU.mult)

        @sq.step(blk, "vector")
        def _(eng, st):
            return eng.tensor_copy(out=tin[:, :], in_=tsc2[:, :])

        @sq.step(blk, "vector")
        def _(eng, st):
            return eng.tensor_copy(out=tsc[:, :], in_=tin[:, :])

        @sq.step(blk, "vector")
        def _(eng, st):
            st.chain(eng.tensor_tensor(out=tsc2[:, :], in0=tsc2[:, :], in1=tsc[:, :],
                                       op=ALU.subtract))
            st.chain(eng.tensor_scalar(out=t1[:, :], in0=tsc2[:, :], scalar1=0.5,
                                       scalar2=None, op0=ALU.is_gt))
            st.chain(eng.tensor_tensor(out=tsc2[:, :], in0=tsc2[:, :], in1=t1[:, :],
                                       op=ALU.subtract))
            st.chain(eng.tensor_scalar(out=t1[:, :], in0=tsc2[:, :], scalar1=-0.5,
                                       scalar2=None, op0=ALU.is_lt))
            st.chain(eng.tensor_tensor(out=tsc2[:, :], in0=tsc2[:, :], in1=t1[:, :],
                                       op=ALU.add))
            return eng.tensor_scalar(out=tsc[:, :], in0=tsc2[:, :],
                                     scalar1=TWO_PI, scalar2=None, op0=ALU.mult)

        @sq.step(blk, "scalar")
        def _(eng, st):
            return eng.activation(out=sintab[:, :], in_=tsc[:, :], func=AF.Sin)

        @sq.step(blk, "vector")
        def _(eng, st):
            st.chain(eng.tensor_scalar(out=tsc2[:, :], in0=tsc[:, :], scalar1=-1.0,
                                       scalar2=math.pi / 2.0, op0=ALU.mult, op1=ALU.add))
            st.chain(eng.tensor_scalar(out=tsc[:, :], in0=tsc2[:, :], scalar1=math.pi,
                                       scalar2=-TWO_PI, op0=ALU.is_gt, op1=ALU.mult))
            return eng.tensor_tensor(out=tsc2[:, :], in0=tsc2[:, :], in1=tsc[:, :],
                                     op=ALU.add)

        @sq.step(blk, "scalar")
        def _(eng, st):
            return eng.activation(out=costab[:, :], in_=tsc2[:, :], func=AF.Sin)

        for tt in range(NTT):
            @sq.step(blk, "sync")
            def _(eng, st, tt=tt):
                for c in range(NCH):
                    st.dma(xv_[:, c], xT[128 * c:128 * (c + 1), 512 * tt:512 * (tt + 1)])

            for c in range(NCH):
                @sq.step(blk, "vector")
                def _(eng, st, c=c):
                    return eng.tensor_tensor(out=x2[:, :], in0=xv_[:, c],
                                             in1=xv_[:, c], op=ALU.mult)

                @sq.step(blk, "tensor")
                def _(eng, st, c=c):
                    eng.matmul(prow[:, 0:512], onesf[:, :],
                               xv_[:, c],
                               start=(c == 0), stop=(c == NCH - 1))
                    return eng.matmul(prow[:, 512:1024], onesf[:, :],
                                      x2[:, :],
                                      start=(c == 0), stop=(c == NCH - 1))

            @sq.step(blk, "vector")
            def _(eng, st):
                eng.tensor_scalar(out=srow[:, 0:512], in0=prow[:, 0:512],
                                  scalar1=1.0 / C, scalar2=None, op0=ALU.mult)
                st.chain(eng.tensor_scalar(out=srow[:, 512:1024], in0=prow[:, 512:1024],
                                           scalar1=1.0 / C, scalar2=None, op0=ALU.mult))
                st.chain(eng.tensor_tensor(out=srow[:, 1024:1536], in0=srow[:, 0:512],
                                           in1=srow[:, 0:512], op=ALU.mult))
                return eng.tensor_tensor(out=srow[:, 512:1024], in0=srow[:, 512:1024],
                                         in1=srow[:, 1024:1536], op=ALU.subtract)

            @sq.step(blk, "scalar")
            def _(eng, st):
                return eng.activation(out=srow[:, 1024:1536], in_=srow[:, 512:1024],
                                      func=AF.Sqrt, bias=epsc[0:1, 0:1])

            @sq.step(blk, "vector")
            def _(eng, st):
                return eng.reciprocal(out=srow[:, 512:1024], in_=srow[:, 1024:1536])

            @sq.step(blk, "tensor")
            def _(eng, st):
                eng.matmul(ps0[:, :], onesr[:, :],
                           srow[:, 0:512], start=True, stop=True)
                return eng.matmul(ps1[:, :], onesr[:, :],
                                  srow[:, 512:1024], start=True, stop=True)

            @sq.step(blk, "vector")
            def _(eng, st):
                eng.tensor_copy(out=mb[:, :], in_=ps0[:, :])
                return eng.tensor_copy(out=rb[:, :], in_=ps1[:, :])

            for c in range(NCH):
                @sq.step(blk, "vector")
                def _(eng, st, c=c):
                    st.chain(eng.tensor_tensor(out=x2[:, :], in0=xv_[:, c], in1=mb[:, :],
                                               op=ALU.subtract))
                    return eng.tensor_tensor(out=xnv[:, c], in0=x2[:, :], in1=rb[:, :],
                                             op=ALU.mult)

            @sq.step(blk, "sync")
            def _(eng, st, tt=tt):
                for c in range(NCH):
                    st.dma(xnT[128 * c:128 * (c + 1), 512 * tt:512 * (tt + 1)],
                           xnv[:, c])

            def proj_step(name, oc, psb, c0=None, c1=None):
                c0 = 128 * oc if c0 is None else c0
                c1 = 128 * (oc + 1) if c1 is None else c1

                @sq.step(blk, "tensor")
                def _(eng, st):
                    last = None
                    for k in range(NCH):
                        last = eng.matmul(psb[0:c1 - c0, 0:512], W(name, k, c0, c1),
                                          xnv[:, k], start=(k == 0), stop=(k == NCH - 1))
                    return last

            def rope_steps(oc, dst_ap, dma_to=None, tt=tt):
                @sq.step(blk, "scalar")
                def _(eng, st):
                    return eng.activation(out=qsb[:, :], in_=ps0[:, :], func=AF.Copy)

                @sq.step(blk, "tensor")
                def _(eng, st):
                    return eng.matmul(ps1[:, :], rotT[:, :], qsb[:, :],
                                      start=True, stop=True)

                @sq.step(blk, "scalar")
                def _(eng, st):
                    return eng.activation(out=rosb[:, :], in_=ps1[:, :], func=AF.Copy)

                @sq.step(blk, "vector")
                def _(eng, st):
                    eng.tensor_tensor(out=t1[:, :], in0=qsb[:, :], in1=costab[:, :],
                                      op=ALU.mult)
                    st.chain(eng.tensor_tensor(out=x2[:, :], in0=rosb[:, :], in1=sintab[:, :],
                                               op=ALU.mult))
                    return eng.tensor_tensor(out=dst_ap, in0=t1[:, :], in1=x2[:, :],
                                             op=ALU.add)

                if dma_to is not None:
                    @sq.step(blk, "sync")
                    def _(eng, st):
                        st.dma(dma_to, dst_ap)

            for oc in range(NCH):
                proj_step("wq", oc, ps0)
                rope_steps(oc, qsb[:, :],
                           dma_to=qrT[128 * oc:128 * (oc + 1), 512 * tt:512 * (tt + 1)])

            for oc in range(2):
                proj_step("wk", oc, ps0)
                rope_steps(oc, ktv[:, oc])

            for oc in range(2):
                proj_step("wvv", oc, ps0)
                proj_step("wvg", oc, ps1)

                @sq.step(blk, "scalar")
                def _(eng, st, oc=oc):
                    return eng.activation(out=gsb[:, :], in_=ps1[:, :],
                                          func=AF.Sigmoid, bias=biasb[:, oc:oc + 1])

                @sq.step(blk, "vector")
                def _(eng, st, oc=oc):
                    return eng.tensor_tensor(out=ktv[:, 2 + oc], in0=ps0[:, :],
                                             in1=gsb[:, :], op=ALU.mult)

            proj_step("wki", 0, ps0, 0, 64)

            @sq.step(blk, "vector")
            def _(eng, st):
                st.chain(eng.memset(ktv[:, 4], 0.0))
                return eng.tensor_copy(out=ktv[:, 4][0:64, :], in_=ps0[0:64, 0:512])

            for oc in range(2):
                proj_step("wqi", oc, ps2)

                @sq.step(blk, "scalar")
                def _(eng, st):
                    return eng.activation(out=gsb[:, :], in_=ps2[:, :], func=AF.Copy)

                @sq.step(blk, "sync")
                def _(eng, st, oc=oc, tt=tt):
                    st.dma(qiT[128 * oc:128 * (oc + 1), 512 * tt:512 * (tt + 1)],
                           gsb[:, :])

            for oc in range(NCH):
                proj_step("wog", oc, ps2)

                @sq.step(blk, "scalar")
                def _(eng, st, oc=oc):
                    return eng.activation(out=gsb[:, :], in_=ps2[:, :],
                                          func=AF.Sigmoid, bias=biasb[:, 2 + oc:3 + oc])

                @sq.step(blk, "sync")
                def _(eng, st, oc=oc, tt=tt):
                    st.dma(g1T[128 * oc:128 * (oc + 1), 512 * tt:512 * (tt + 1)],
                           gsb[:, :])

            for qs in range(4):
                @sq.step(blk, "tensor")
                def _(eng, st, qs=qs):
                    last = None
                    for k in range(NCH):
                        last = eng.matmul(pqt[:, :], xnv[:, k][:, 128 * qs:128 * (qs + 1)],
                                          W("wwi", k, 0, 4), start=(k == 0), stop=(k == NCH - 1))
                    return last

                @sq.step(blk, "scalar")
                def _(eng, st):
                    return eng.activation(out=wisb[:, :], in_=pqt[:, :], func=AF.Sigmoid)

                @sq.step(blk, "sync")
                def _(eng, st, qs=qs, tt=tt):
                    st.dma(wI_tok[512 * tt + 128 * qs:512 * tt + 128 * (qs + 1), :],
                           wisb[:, :])

            for ts in range(4):
                @sq.step(blk, "tensor")
                def _(eng, st, ts=ts):
                    last = None
                    for fc in range(5):
                        last = eng.transpose(ptr[:, 128 * fc:128 * (fc + 1)],
                                             ktv[:, fc][:, 128 * ts:128 * (ts + 1)],
                                             identb[:, :])
                    return last

                @sq.step(blk, "vector")
                def _(eng, st, ts=ts):
                    for fc in range(4):
                        eng.tensor_copy(out=kvsb[:, 576 * ts + 128 * fc:576 * ts + 128 * (fc + 1)],
                                        in_=ptr[:, 128 * fc:128 * (fc + 1)])
                    return eng.tensor_copy(out=kvsb[:, 576 * ts + 512:576 * ts + 576],
                                           in_=ptr[:, 512:576])

            @sq.step(blk, "gpsimd")
            def _(eng, st, tt=tt):
                for ts in range(4):
                    st.dma(kv_loc[512 * tt + 128 * ts:512 * tt + 128 * (ts + 1), :],
                           kvsb[:, 576 * ts:576 * (ts + 1)])



def emit_B(nc, blk, sq, io, n_batches=4, bisect_iters=34):
    qrT = io["qrT"]
    qiT = io["qiT"]
    g1T = io["g1T"]
    xT = io["xT"]
    wI_tok = io["wI_tok"]
    kv_all = io["kv_all"]
    bi_in = io["bi_in"]
    wo = io["wo"]
    xAT = io["xAT"]

    from contextlib import ExitStack
    with ExitStack() as _ctx:
        wosb = _ctx.enter_context(nc.sbuf_tensor("Bwosb", [128, NCH * 1024], BF16))
        identb = _ctx.enter_context(nc.sbuf_tensor("Bidentb", [128, 128], BF16))
        onesr = _ctx.enter_context(nc.sbuf_tensor("Bonesr", [1, 128], F32))
        bib = _ctx.enter_context(nc.sbuf_tensor("Bbib", [128, 4], F32))
        kt64 = _ctx.enter_context(nc.sbuf_tensor("Bkt64", [64, 4 * L], BF16))
        ki64 = _ctx.enter_context(nc.sbuf_tensor("Bki64", [64, L], BF16))
        vtok = _ctx.enter_context(nc.sbuf_tensor("Bvtok", [128, 32 * 260], BF16))
        qr64 = _ctx.enter_context(nc.sbuf_tensor("Bqr64", [64, H * 512], BF16))
        qi64 = _ctx.enter_context(nc.sbuf_tensor("Bqi64", [64, 4 * 512], BF16))
        wib = _ctx.enter_context(nc.sbuf_tensor("Bwib", [128, 4 * 4], F32))
        ktok = _ctx.enter_context(nc.sbuf_tensor("Bktok", [128, 384], BF16))
        mtsb = _ctx.enter_context(nc.sbuf_tensor("Bmtsb", [128, 32 * 512], BF16))
        isb = _ctx.enter_context(nc.sbuf_tensor("Bisb", [128, L], F32))
        scr = _ctx.enter_context(nc.sbuf_tensor("Bscr", [128, L], BF16))
        bst = _ctx.enter_context(nc.sbuf_tensor("Bbst", [128, 8], F32))
        sx0 = _ctx.enter_context(nc.sbuf_tensor("Bsx0", [128, 512], F32))
        sx1 = _ctx.enter_context(nc.sbuf_tensor("Bsx1", [128, 512], F32))
        sx2 = _ctx.enter_context(nc.sbuf_tensor("Bsx2", [128, 512], F32))
        sx3 = _ctx.enter_context(nc.sbuf_tensor("Bsx3", [128, 512], F32))
        pt0 = _ctx.enter_context(nc.sbuf_tensor("Bpt0", [128, 512], BF16))
        pt1 = _ctx.enter_context(nc.sbuf_tensor("Bpt1", [128, 512], BF16))
        pt2 = _ctx.enter_context(nc.sbuf_tensor("Bpt2", [128, 512], BF16))
        pt3 = _ctx.enter_context(nc.sbuf_tensor("Bpt3", [128, 512], BF16))
        drw = _ctx.enter_context(nc.sbuf_tensor("Bdrw", [1, 512], F32))
        drb = _ctx.enter_context(nc.sbuf_tensor("Bdrb", [64, 512], F32))
        att = _ctx.enter_context(nc.sbuf_tensor("Batt", [128, NCH * 512], BF16))
        g1b = _ctx.enter_context(nc.sbuf_tensor("Bg1b", [128, NCH * 512], BF16))
        xbu = _ctx.enter_context(nc.sbuf_tensor("Bxbu", [128, 512], F32))
        xao = _ctx.enter_context(nc.sbuf_tensor("Bxao", [128, 512], F32))
        psS0 = _ctx.enter_context(nc.psum_tensor("BpsS0", [128, 512], F32))
        psS1 = _ctx.enter_context(nc.psum_tensor("BpsS1", [128, 512], F32))
        psS2 = _ctx.enter_context(nc.psum_tensor("BpsS2", [128, 512], F32))
        psS3 = _ctx.enter_context(nc.psum_tensor("BpsS3", [128, 512], F32))
        psO = _ctx.enter_context(nc.psum_tensor("BpsO", [65, 512], F32))
        psB = _ctx.enter_context(nc.psum_tensor("BpsB", [64, 512], F32))
        ptr = _ctx.enter_context(nc.psum_tensor("Bptr", [128, 512], BF16))
        psY = _ctx.enter_context(nc.psum_tensor("BpsY", [128, 512], F32))
        psS = (psS0, psS1, psS2, psS3)
        sxs = (sx0, sx1, sx2, sx3)
        pts = (pt0, pt1, pt2, pt3)
        vtv = vtok.ap().rearrange("p (c t) -> p c t", c=32)
        mtv = mtsb.ap().rearrange("p (c t) -> p c t", c=32)
        attv = att.ap().rearrange("p (c t) -> p c t", c=NCH)
        g1v = g1b.ap().rearrange("p (c t) -> p c t", c=NCH)
        wov = wosb.ap().rearrange("p (c t) -> p c t", c=NCH)
        ktg = kt64.ap().rearrange("p (g t) -> p g t", g=4)
        qrh = qr64.ap().rearrange("p (h t) -> p h t", h=H)
        qih = qi64.ap().rearrange("p (h t) -> p h t", h=4)

        @sq.step(blk, "gpsimd")
        def _(eng, st):
            for k in range(NCH):
                st.dma(wov[:, k], wo[128 * k:128 * (k + 1), :])

        @sq.step(blk, "gpsimd")
        def _(eng, st):
            return make_ident(nc, eng, identb[:, :], st)

        @sq.step(blk, "vector")
        def _(eng, st):
            eng.memset(onesr[:, :], 1.0)
            return eng.memset(
                vtok.ap().rearrange("p (c g o) -> p c g o", c=32, g=4)[:, :, :, 64:65], 1.0)

        @sq.step(blk, "sync")
        def _(eng, st):
            st.dma(drw[0:1, 0:4], bi_in[0:1, :])

        @sq.step(blk, "tensor")
        def _(eng, st):
            return eng.matmul(psS0[:, 0:4], onesr[:, :],
                              drw[0:1, 0:4], start=True, stop=True)

        @sq.step(blk, "vector")
        def _(eng, st):
            return eng.tensor_copy(out=bib[:, :], in_=psS0[:, 0:4])

        for b in range(n_batches):
            @sq.step(blk, "sync")
            def _(eng, st, b=b):
                for h in range(H):
                    st.dma(qrh[0:64, h, :],
                           qrT[64 * h:64 * (h + 1), 512 * b:512 * (b + 1)])
                for h in range(4):
                    st.dma(qih[0:64, h, :],
                           qiT[64 * h:64 * (h + 1), 512 * b:512 * (b + 1)])
                for qt in range(4):
                    st.dma(wib[:, 4 * qt:4 * (qt + 1)],
                           wI_tok[512 * b + 128 * qt:512 * b + 128 * (qt + 1), :])
                for c in range(NCH):
                    st.dma(g1v[:, c], g1T[128 * c:128 * (c + 1), 512 * b:512 * (b + 1)])

            @sq.step(blk, "gpsimd")
            def _(eng, st, b=b):
                for j in range(32):
                    r0 = 2048 * (j // 4) + 512 * b + 128 * (j % 4)
                    for g in range(4):
                        st.dma(vtv[:, j][:, 65 * g:65 * g + 64],
                               kv_all[r0:r0 + 128, 256 + 64 * g:256 + 64 * (g + 1)])

            for j in range(32):
                @sq.step(blk, "gpsimd")
                def _(eng, st, j=j, b=b):
                    r0 = 2048 * (j // 4) + 512 * b + 128 * (j % 4)
                    st.dma(ktok[:, 0:256], kv_all[r0:r0 + 128, 0:256])
                    st.dma(ktok[:, 256:320], kv_all[r0:r0 + 128, 512:576])

                @sq.step(blk, "tensor")
                def _(eng, st, j=j):
                    last = None
                    for fc in range(3):
                        last = eng.transpose(ptr[:, 128 * fc:128 * (fc + 1)],
                                             ktok[:, 128 * fc:128 * (fc + 1)],
                                             identb[:, :])
                    return last

                @sq.step(blk, "vector")
                def _(eng, st, j=j):
                    sl = slice(128 * j, 128 * (j + 1))
                    eng.tensor_copy(out=ktg[0:64, 0, sl], in_=ptr[0:64, 0:128])
                    st.chain(eng.tensor_copy(out=ktg[0:64, 1, sl], in_=ptr[64:128, 0:128]))
                    eng.tensor_copy(out=ktg[0:64, 2, sl], in_=ptr[0:64, 128:256])
                    st.chain(eng.tensor_copy(out=ktg[0:64, 3, sl], in_=ptr[64:128, 128:256]))
                    return eng.tensor_copy(out=ki64[0:64, sl], in_=ptr[0:64, 256:384])

            for qt in range(4):
                for h in range(H_I):
                    for kc in range(NCH):
                        @sq.step(blk, "tensor")
                        def _(eng, st, h=h, kc=kc, qt=qt):
                            return eng.matmul(
                                psS[kc % 2][:, :],
                                qih[0:64, h, 128 * qt:128 * (qt + 1)],
                                ki64[0:64, 512 * kc:512 * (kc + 1)],
                                start=True, stop=True)

                        @sq.step(blk, "scalar")
                        def _(eng, st, h=h, kc=kc):
                            return eng.activation(
                                out=scr[:, 512 * kc:512 * (kc + 1)],
                                in_=psS[kc % 2][:, :], func=AF.Sigmoid,
                                bias=bib[:, h:h + 1])

                    @sq.step(blk, "vector")
                    def _(eng, st, h=h, qt=qt):
                        if h == 0:
                            return eng.tensor_scalar(
                                out=isb[:, :], in0=scr[:, :],
                                scalar1=wib[:, 4 * qt + h:4 * qt + h + 1],
                                scalar2=None, op0=ALU.mult)
                        return eng.scalar_tensor_tensor(
                            out=isb[:, :], in0=scr[:, :],
                            scalar=wib[:, 4 * qt + h:4 * qt + h + 1],
                            in1=isb[:, :], op0=ALU.mult, op1=ALU.add)

                @sq.step(blk, "vector")
                def _(eng, st, qt=qt):
                    st.chain(eng.memset(bst[:, 0:2], 0.0))
                    st.chain(eng.memset(bst[:, 1:2], 4.0))
                    for it in range(bisect_iters):
                        st.chain(eng.tensor_tensor(out=bst[:, 2:3], in0=bst[:, 0:1],
                                                   in1=bst[:, 1:2], op=ALU.add))
                        st.chain(eng.tensor_scalar(out=bst[:, 2:3], in0=bst[:, 2:3],
                                                   scalar1=0.5, scalar2=None,
                                                   op0=ALU.mult))
                        st.chain(eng.tensor_scalar(out=scr[:, :], in0=isb[:, :],
                                                   scalar1=bst[:, 2:3], scalar2=0.0,
                                                   op0=ALU.is_gt, op1=ALU.add,
                                                   accum_out=bst[:, 3:4]))
                        st.chain(eng.tensor_scalar(out=bst[:, 4:5], in0=bst[:, 3:4],
                                                   scalar1=float(K_SEL), scalar2=None,
                                                   op0=ALU.is_ge))
                        st.chain(eng.tensor_tensor(out=bst[:, 5:6], in0=bst[:, 2:3],
                                                   in1=bst[:, 0:1], op=ALU.subtract))
                        st.chain(eng.scalar_tensor_tensor(out=bst[:, 0:1], in0=bst[:, 5:6],
                                                          scalar=bst[:, 4:5], in1=bst[:, 0:1],
                                                          op0=ALU.mult, op1=ALU.add))
                        st.chain(eng.tensor_tensor(out=bst[:, 5:6], in0=bst[:, 1:2],
                                                   in1=bst[:, 2:3], op=ALU.subtract))
                        st.chain(eng.scalar_tensor_tensor(out=bst[:, 1:2], in0=bst[:, 5:6],
                                                          scalar=bst[:, 4:5], in1=bst[:, 2:3],
                                                          op0=ALU.mult, op1=ALU.add))
                    return eng.tensor_scalar(out=scr[:, :], in0=isb[:, :],
                                             scalar1=bst[:, 0:1], scalar2=NEG_BIG,
                                             op0=ALU.is_le, op1=ALU.mult)

                for jg in range(8):
                    @sq.step(blk, "tensor")
                    def _(eng, st, jg=jg):
                        last = None
                        for u in range(4):
                            kc = 4 * jg + u
                            last = eng.transpose(ptr[:, 128 * u:128 * (u + 1)],
                                                 scr[:, 128 * kc:128 * (kc + 1)],
                                                 identb[:, :])
                        return last

                    @sq.step(blk, "vector")
                    def _(eng, st, jg=jg, qt=qt):
                        last = None
                        for u in range(4):
                            kc = 4 * jg + u
                            last = eng.tensor_copy(
                                out=mtv[:, kc][:, 128 * qt:128 * (qt + 1)],
                                in_=ptr[:, 128 * u:128 * (u + 1)])
                        return last

            for h in range(H):
                g = h // 4
                for jg in range(8):
                    @sq.step(blk, "tensor")
                    def _(eng, st, h=h, g=g, jg=jg):
                        last = None
                        for u in range(4):
                            kc = 4 * jg + u
                            last = eng.matmul(psS[u][:, :],
                                              ktg[0:64, g, 128 * kc:128 * (kc + 1)],
                                              qrh[0:64, h, :],
                                              start=True, stop=True)
                        return last

                    @sq.step(blk, "vector")
                    def _(eng, st, jg=jg):
                        last = None
                        for u in range(4):
                            kc = 4 * jg + u
                            last = eng.scalar_tensor_tensor(
                                out=sxs[u][:, :], in0=psS[u][:, :],
                                scalar=SM_SCALE, in1=mtv[:, kc],
                                op0=ALU.mult, op1=ALU.add)
                        return last

                    @sq.step(blk, "scalar")
                    def _(eng, st, jg=jg):
                        last = None
                        for u in range(4):
                            last = eng.activation(out=pts[u][:, :], in_=sxs[u][:, :],
                                                  func=AF.Exp)
                        return last

                    @sq.step(blk, "tensor")
                    def _(eng, st, g=g, jg=jg):
                        last = None
                        for u in range(4):
                            kc = 4 * jg + u
                            last = eng.matmul(psO[0:65, :],
                                              vtv[:, kc][:, 65 * g:65 * (g + 1)],
                                              pts[u][:, :],
                                              start=(kc == 0), stop=(kc == 31))
                        return last

                @sq.step(blk, "vector")
                def _(eng, st):
                    return eng.reciprocal(out=drw[0:1, :], in_=psO[64:65, :])

                @sq.step(blk, "tensor")
                def _(eng, st):
                    return eng.matmul(psB[0:64, :], onesr[0:1, 0:64], drw[0:1, :],
                                      start=True, stop=True)

                @sq.step(blk, "vector")
                def _(eng, st):
                    return eng.tensor_copy(out=drb[0:64, :], in_=psB[0:64, :])

                @sq.step(blk, "vector")
                def _(eng, st, h=h):
                    dst = attv[:, h // 2][64 * (h % 2):64 * (h % 2) + 64, :]
                    return eng.tensor_tensor(out=dst, in0=psO[0:64, :],
                                             in1=drb[0:64, :], op=ALU.mult)

            for c in range(NCH):
                @sq.step(blk, "vector")
                def _(eng, st, c=c):
                    return eng.tensor_tensor(out=attv[:, c], in0=attv[:, c],
                                             in1=g1v[:, c], op=ALU.mult)

            for oc in range(NCH):
                @sq.step(blk, "tensor")
                def _(eng, st, oc=oc):
                    last = None
                    for k in range(NCH):
                        last = eng.matmul(psY[:, :],
                                          wov[:, k][:, 128 * oc:128 * (oc + 1)],
                                          attv[:, k], start=(k == 0), stop=(k == NCH - 1))
                    return last

                @sq.step(blk, "sync")
                def _(eng, st, oc=oc, b=b):
                    st.dma(xbu[:, :], xT[128 * oc:128 * (oc + 1), 512 * b:512 * (b + 1)])

                @sq.step(blk, "vector")
                def _(eng, st):
                    return eng.tensor_tensor(out=xao[:, :], in0=psY[:, :],
                                             in1=xbu[:, :], op=ALU.add)

                @sq.step(blk, "sync")
                def _(eng, st, oc=oc, b=b):
                    st.dma(xAT[128 * oc:128 * (oc + 1), 512 * b:512 * (b + 1)],
                           xao[:, :])



def emit_C(nc, blk, sq, io, n_experts=16):
    xnT = io["xnT"]
    xAT = io["xAT"]
    gwT = io["gwT"]
    ebias = io["ebias"]
    wgT = io["wgT"]
    wuT = io["wuT"]
    wdT = io["wdT"]
    sgT = io["sgT"]
    suT = io["suT"]
    sdT = io["sdT"]
    x1T = io["x1T"]
    xn1T = io["xn1T"]
    xn1F = io["xn1F"]
    xmax = io["xmax_in"]

    from contextlib import ExitStack
    with ExitStack() as _ctx:
        xnsb = _ctx.enter_context(nc.sbuf_tensor("Cxnsb", [128, NCH * TPC], BF16))
        ysb = _ctx.enter_context(nc.sbuf_tensor("Cysb", [128, NCH * TPC], F32))
        wge = _ctx.enter_context(nc.sbuf_tensor("Cwge", [128, NCH * 1024], BF16))
        wue = _ctx.enter_context(nc.sbuf_tensor("Cwue", [128, NCH * 1024], BF16))
        wde = _ctx.enter_context(nc.sbuf_tensor("Cwde", [128, NCH * 1024], BF16))
        gwsb = _ctx.enter_context(nc.sbuf_tensor("Cgwsb", [128, NCH * 16], BF16))
        ebb = _ctx.enter_context(nc.sbuf_tensor("Cebb", [128, 16], F32))
        cmb = _ctx.enter_context(nc.sbuf_tensor("Ccmb", [128, 16 * 16], F32))
        ctr_ = _ctx.enter_context(nc.sbuf_tensor("Cctr_", [16, TPC], F32))
        crow = _ctx.enter_context(nc.sbuf_tensor("Ccrow", [1, TPC], F32))
        cbf = _ctx.enter_context(nc.sbuf_tensor("Ccbf", [128, 512], F32))
        hsl = _ctx.enter_context(nc.sbuf_tensor("Chsl", [128, 512], BF16))
        hsb = _ctx.enter_context(nc.sbuf_tensor("Chsb", [128, NCH * 512], BF16))
        r16 = _ctx.enter_context(nc.sbuf_tensor("Cr16", [128, 16], F32))
        r16b = _ctx.enter_context(nc.sbuf_tensor("Cr16b", [128, 16], F32))
        r16c = _ctx.enter_context(nc.sbuf_tensor("Cr16c", [128, 16], F32))
        r8 = _ctx.enter_context(nc.sbuf_tensor("Cr8", [128, 8], F32))
        rcol = _ctx.enter_context(nc.sbuf_tensor("Crcol", [128, 2], F32))
        onesr = _ctx.enter_context(nc.sbuf_tensor("Conesr", [1, 128], F32))
        onesf = _ctx.enter_context(nc.sbuf_tensor("Conesf", [128, 1], F32))
        identf = _ctx.enter_context(nc.sbuf_tensor("Cidentf", [128, 128], F32))
        epsc = _ctx.enter_context(nc.sbuf_tensor("Cepsc", [1, 1], F32))
        xbu = _ctx.enter_context(nc.sbuf_tensor("Cxbu", [128, 512], F32))
        x1b = _ctx.enter_context(nc.sbuf_tensor("Cx1b", [128, NCH * 512], F32))
        x2 = _ctx.enter_context(nc.sbuf_tensor("Cx2", [128, 512], F32))
        srow = _ctx.enter_context(nc.sbuf_tensor("Csrow", [1, 3 * 512], F32))
        mb = _ctx.enter_context(nc.sbuf_tensor("Cmb", [128, 512], F32))
        rb = _ctx.enter_context(nc.sbuf_tensor("Crb", [128, 512], F32))
        mxc = _ctx.enter_context(nc.sbuf_tensor("Cmxc", [128, 2], F32))
        mrow = _ctx.enter_context(nc.sbuf_tensor("Cmrow", [1, 128], F32))
        psG = _ctx.enter_context(nc.psum_tensor("CpsG", [128, 512], F32))
        psU = _ctx.enter_context(nc.psum_tensor("CpsU", [128, 512], F32))
        psD = _ctx.enter_context(nc.psum_tensor("CpsD", [128, 512], F32))
        psR = _ctx.enter_context(nc.psum_tensor("CpsR", [128, 16], F32))
        psB = _ctx.enter_context(nc.psum_tensor("CpsB", [128, 512], F32))
        psT = _ctx.enter_context(nc.psum_tensor("CpsT", [128, 128], F32))
        prow = _ctx.enter_context(nc.psum_tensor("Cprow", [1, 1024], F32))
        xnv = xnsb.ap().rearrange("p (c t) -> p c t", c=NCH)
        yv = ysb.ap().rearrange("p (c t) -> p c t", c=NCH)
        x1v = x1b.ap().rearrange("p (c t) -> p c t", c=NCH)
        wgv = wge.ap().rearrange("p (c t) -> p c t", c=NCH)
        wuv = wue.ap().rearrange("p (c t) -> p c t", c=NCH)
        wdv = wde.ap().rearrange("p (c t) -> p c t", c=NCH)
        gwv = gwsb.ap().rearrange("p (c t) -> p c t", c=NCH)
        hsv = hsb.ap().rearrange("p (c t) -> p c t", c=NCH)

        @sq.step(blk, "gpsimd")
        def _(eng, st):
            for c in range(NCH):
                st.dma(xnv[:, c], xnT[128 * c:128 * (c + 1), :])
                st.dma(gwv[:, c], gwT[128 * c:128 * (c + 1), :])
            st.dma(srow[0:1, 0:16], ebias[0:1, :])

        @sq.step(blk, "gpsimd")
        def _(eng, st):
            return make_ident(nc, eng, identf[:, :], st)

        @sq.step(blk, "vector")
        def _(eng, st):
            eng.memset(onesr[:, :], 1.0)
            eng.memset(onesf[:, :], 1.0)
            return eng.memset(epsc[:, :], LN_EPS)

        @sq.step(blk, "tensor")
        def _(eng, st):
            return eng.matmul(psR[:, :], onesr[:, :], srow[0:1, 0:16],
                              start=True, stop=True)

        @sq.step(blk, "vector")
        def _(eng, st):
            return eng.tensor_copy(out=ebb[:, :], in_=psR[:, :])

        for qt in range(16):
            @sq.step(blk, "tensor")
            def _(eng, st, qt=qt):
                last = None
                for k in range(NCH):
                    last = eng.matmul(psR[:, :],
                                      xnv[:, k][:, 128 * qt:128 * (qt + 1)],
                                      gwv[:, k], start=(k == 0), stop=(k == NCH - 1))
                return last

            @sq.step(blk, "scalar")
            def _(eng, st):
                return eng.activation(out=r16[:, :], in_=psR[:, :], func=AF.Sigmoid)

            @sq.step(blk, "vector")
            def _(eng, st, qt=qt):
                st.chain(eng.tensor_tensor(out=r16b[:, :], in0=r16[:, :],
                                           in1=ebb[:, :], op=ALU.add))
                st.chain(eng.max(out=r8[:, :], in_=r16b[:, :]))
                st.chain(eng.match_replace(out=r16c[:, :], in_to_replace=r8[:, :],
                                           in_values=r16b[:, :], imm_value=-1e30))
                st.chain(eng.tensor_scalar(out=r16c[:, :], in0=r16c[:, :],
                                           scalar1=-1e29, scalar2=None, op0=ALU.is_lt))
                st.chain(eng.tensor_tensor(out=r16c[:, :], in0=r16c[:, :],
                                           in1=r16[:, :], op=ALU.mult))
                st.chain(eng.reduce_sum(out=rcol[:, 0:1], in_=r16c[:, :],
                                        axis=mybir.AxisListType.X))
                st.chain(eng.reciprocal(out=rcol[:, 1:2], in_=rcol[:, 0:1]))
                return eng.tensor_scalar(out=cmb[:, 16 * qt:16 * (qt + 1)],
                                         in0=r16c[:, :], scalar1=rcol[:, 1:2],
                                         scalar2=2.826, op0=ALU.mult, op1=ALU.mult)

            @sq.step(blk, "tensor")
            def _(eng, st, qt=qt):
                return eng.transpose(psT[0:16, :].bitcast(F32),
                                     cmb[:, 16 * qt:16 * (qt + 1)], identf[:, :])

            @sq.step(blk, "vector")
            def _(eng, st, qt=qt):
                return eng.tensor_copy(out=ctr_[0:16, 128 * qt:128 * (qt + 1)],
                                       in_=psT[0:16, 0:128].bitcast(F32))

        for e in range(n_experts + 1):
            shared = (e == n_experts)

            @sq.step(blk, "gpsimd")
            def _(eng, st, e=e, shared=shared):
                for c in range(NCH):
                    if shared:
                        st.dma(wgv[:, c], sgT[128 * c:128 * (c + 1), :])
                        st.dma(wuv[:, c], suT[128 * c:128 * (c + 1), :])
                        st.dma(wdv[:, c], sdT[128 * c:128 * (c + 1), :])
                    else:
                        st.dma(wgv[:, c], wgT[e, 128 * c:128 * (c + 1), :])
                        st.dma(wuv[:, c], wuT[e, 128 * c:128 * (c + 1), :])
                        st.dma(wdv[:, c], wdT[e, 128 * c:128 * (c + 1), :])

            if not shared:
                @sq.step(blk, "sync")
                def _(eng, st, e=e):
                    st.dma(crow[0:1, :], ctr_[e:e + 1, :])

            for tt in range(NTT):
                if not shared:
                    @sq.step(blk, "tensor")
                    def _(eng, st, e=e, tt=tt):
                        return eng.matmul(psB[:, :], onesr[:, :],
                                          crow[0:1, 512 * tt:512 * (tt + 1)],
                                          start=True, stop=True)

                    @sq.step(blk, "vector")
                    def _(eng, st):
                        return eng.tensor_copy(out=cbf[:, :], in_=psB[:, :])

                for ic in range(NCH):
                    @sq.step(blk, "tensor")
                    def _(eng, st, ic=ic, tt=tt):
                        for k in range(NCH):
                            eng.matmul(psG[:, :],
                                       wgv[:, k][:, 128 * ic:128 * (ic + 1)],
                                       xnv[:, k][:, 512 * tt:512 * (tt + 1)],
                                       start=(k == 0), stop=(k == NCH - 1))
                        last = None
                        for k in range(NCH):
                            last = eng.matmul(psU[:, :],
                                              wuv[:, k][:, 128 * ic:128 * (ic + 1)],
                                              xnv[:, k][:, 512 * tt:512 * (tt + 1)],
                                              start=(k == 0), stop=(k == NCH - 1))
                        return last

                    @sq.step(blk, "scalar")
                    def _(eng, st):
                        return eng.activation(out=hsl[:, :], in_=psG[:, :], func=AF.Silu)

                    @sq.step(blk, "vector")
                    def _(eng, st, ic=ic, shared=shared):
                        if shared:
                            return eng.tensor_tensor(out=hsv[:, ic], in0=hsl[:, :],
                                                     in1=psU[:, :], op=ALU.mult)
                        st.chain(eng.tensor_tensor(out=hsv[:, ic], in0=hsl[:, :],
                                                   in1=psU[:, :], op=ALU.mult))
                        return eng.tensor_tensor(out=hsv[:, ic], in0=hsv[:, ic],
                                                 in1=cbf[:, :], op=ALU.mult)

                for dc in range(NCH):
                    @sq.step(blk, "tensor")
                    def _(eng, st, dc=dc, tt=tt):
                        last = None
                        for k in range(NCH):
                            last = eng.matmul(psD[:, :],
                                              wdv[:, k][:, 128 * dc:128 * (dc + 1)],
                                              hsv[:, k], start=(k == 0),
                                              stop=(k == NCH - 1))
                        return last

                    @sq.step(blk, "vector")
                    def _(eng, st, e=e, dc=dc, tt=tt):
                        dst = yv[:, dc][:, 512 * tt:512 * (tt + 1)]
                        if e == 0:
                            return eng.tensor_copy(out=dst, in_=psD[:, :])
                        return eng.tensor_tensor(out=dst, in0=dst, in1=psD[:, :],
                                                 op=ALU.add)

        @sq.step(blk, "vector")
        def _(eng, st):
            return eng.memset(mxc[:, 0:1], 0.0)

        for tt in range(NTT):
            @sq.step(blk, "sync")
            def _(eng, st, tt=tt):
                for c in range(NCH):
                    st.dma(x1v[:, c],
                           xAT[128 * c:128 * (c + 1), 512 * tt:512 * (tt + 1)])

            for c in range(NCH):
                @sq.step(blk, "vector")
                def _(eng, st, c=c, tt=tt):
                    return eng.tensor_tensor(out=x1v[:, c], in0=x1v[:, c],
                                             in1=yv[:, c][:, 512 * tt:512 * (tt + 1)],
                                             op=ALU.add)

            @sq.step(blk, "sync")
            def _(eng, st, tt=tt):
                for c in range(NCH):
                    st.dma(x1T[128 * c:128 * (c + 1), 512 * tt:512 * (tt + 1)],
                           x1v[:, c])

            for c in range(NCH):
                @sq.step(blk, "vector")
                def _(eng, st, c=c):
                    return eng.tensor_tensor(out=x2[:, :], in0=x1v[:, c],
                                             in1=x1v[:, c], op=ALU.mult)

                @sq.step(blk, "tensor")
                def _(eng, st, c=c):
                    eng.matmul(prow[:, 0:512], onesf[:, :], x1v[:, c],
                               start=(c == 0), stop=(c == NCH - 1))
                    return eng.matmul(prow[:, 512:1024], onesf[:, :], x2[:, :],
                                      start=(c == 0), stop=(c == NCH - 1))

            @sq.step(blk, "vector")
            def _(eng, st):
                eng.tensor_scalar(out=srow[:, 0:512], in0=prow[:, 0:512],
                                  scalar1=1.0 / C, scalar2=None, op0=ALU.mult)
                st.chain(eng.tensor_scalar(out=srow[:, 512:1024], in0=prow[:, 512:1024],
                                           scalar1=1.0 / C, scalar2=None, op0=ALU.mult))
                st.chain(eng.tensor_tensor(out=srow[:, 1024:1536], in0=srow[:, 0:512],
                                           in1=srow[:, 0:512], op=ALU.mult))
                return eng.tensor_tensor(out=srow[:, 512:1024], in0=srow[:, 512:1024],
                                         in1=srow[:, 1024:1536], op=ALU.subtract)

            @sq.step(blk, "scalar")
            def _(eng, st):
                return eng.activation(out=srow[:, 1024:1536], in_=srow[:, 512:1024],
                                      func=AF.Sqrt, bias=epsc[0:1, 0:1])

            @sq.step(blk, "vector")
            def _(eng, st):
                return eng.reciprocal(out=srow[:, 512:1024], in_=srow[:, 1024:1536])

            @sq.step(blk, "tensor")
            def _(eng, st):
                eng.matmul(psG[:, :], onesr[:, :], srow[:, 0:512],
                           start=True, stop=True)
                return eng.matmul(psU[:, :], onesr[:, :], srow[:, 512:1024],
                                  start=True, stop=True)

            @sq.step(blk, "vector")
            def _(eng, st):
                eng.tensor_copy(out=mb[:, :], in_=psG[:, :])
                return eng.tensor_copy(out=rb[:, :], in_=psU[:, :])

            for c in range(NCH):
                @sq.step(blk, "vector")
                def _(eng, st, c=c):
                    st.chain(eng.tensor_tensor(out=x2[:, :], in0=x1v[:, c],
                                               in1=mb[:, :], op=ALU.subtract))
                    st.chain(eng.tensor_tensor(out=x2[:, :], in0=x2[:, :],
                                               in1=rb[:, :], op=ALU.mult))
                    st.chain(eng.tensor_copy(out=hsl[:, :], in_=x2[:, :]))
                    st.chain(eng.reduce_max(out=mxc[:, 1:2], in_=x2[:, :],
                                            axis=mybir.AxisListType.X,
                                            apply_absolute_value=True))
                    return eng.tensor_tensor(out=mxc[:, 0:1], in0=mxc[:, 0:1],
                                             in1=mxc[:, 1:2], op=ALU.max)

                @sq.step(blk, "sync")
                def _(eng, st, c=c, tt=tt):
                    st.dma(xn1T[128 * c:128 * (c + 1), 512 * tt:512 * (tt + 1)],
                           hsl[:, :])
                    st.dma(xn1F[128 * c:128 * (c + 1), 512 * tt:512 * (tt + 1)],
                           x2[:, :])

        @sq.step(blk, "tensor")
        def _(eng, st):
            return eng.transpose(psT[0:1, 0:128].bitcast(F32), mxc[:, 0:1],
                                 identf[:, :])

        @sq.step(blk, "vector")
        def _(eng, st):
            st.chain(eng.tensor_copy(out=mrow[0:1, :], in_=psT[0:1, 0:128].bitcast(F32)))
            return eng.reduce_max(out=srow[0:1, 0:1], in_=mrow[0:1, :],
                                  axis=mybir.AxisListType.X)

        @sq.step(blk, "sync")
        def _(eng, st):
            st.dma(xmax[0:1, 0:1], srow[0:1, 0:1])



def emit_D(nc, blk, sq, io):
    xn1T = io["xn1T"]
    x1T = io["x1T"]
    wkv = io["wkv"]
    bkv = io["bkv"]
    wq1 = io["wq1"]
    bq1 = io["bq1"]
    wp1 = io["wp1"]
    bp1 = io["bp1"]
    xBT = io["xBT"]

    from contextlib import ExitStack
    with ExitStack() as _ctx:
        xnsb = _ctx.enter_context(nc.sbuf_tensor("Dxnsb", [128, NCH * TPC], BF16))
        wkvb = _ctx.enter_context(nc.sbuf_tensor("Dwkvb", [128, NCH * 2048], BF16))
        wqb = _ctx.enter_context(nc.sbuf_tensor("Dwqb", [128, NCH * 1024], BF16))
        wpb = _ctx.enter_context(nc.sbuf_tensor("Dwpb", [128, NCH * 1024], BF16))
        bkvb = _ctx.enter_context(nc.sbuf_tensor("Dbkvb", [128, 2048], F32))
        bqb = _ctx.enter_context(nc.sbuf_tensor("Dbqb", [128, 1024], F32))
        bpc = _ctx.enter_context(nc.sbuf_tensor("Dbpc", [128, NCH], F32))
        brow = _ctx.enter_context(nc.sbuf_tensor("Dbrow", [1, 2048], F32))
        onesr = _ctx.enter_context(nc.sbuf_tensor("Donesr", [1, 128], F32))
        identb = _ctx.enter_context(nc.sbuf_tensor("Didentb", [128, 128], BF16))
        qtok = _ctx.enter_context(nc.sbuf_tensor("Dqtok", [128, 4 * 1024], BF16))
        ktok = _ctx.enter_context(nc.sbuf_tensor("Dktok2", [128, 4 * 1024], BF16))
        vtok = _ctx.enter_context(nc.sbuf_tensor("Dvtok2", [128, 4 * 1024], BF16))
        atile = _ctx.enter_context(nc.sbuf_tensor("Datile", [128, 4 * 16 * 4], F32))
        amx = _ctx.enter_context(nc.sbuf_tensor("Damx", [128, 4 * 16], F32))
        tmp1 = _ctx.enter_context(nc.sbuf_tensor("Dtmp1", [128, 1024], F32))
        avt = _ctx.enter_context(nc.sbuf_tensor("Davt", [128, 4 * 1024], BF16))
        avT = _ctx.enter_context(nc.sbuf_tensor("DavT", [128, NCH * TPC], BF16))
        xbu = _ctx.enter_context(nc.sbuf_tensor("Dxbu", [128, 512], F32))
        xao = _ctx.enter_context(nc.sbuf_tensor("Dxao", [128, 512], F32))
        psA = _ctx.enter_context(nc.psum_tensor("DpsA", [128, 512], F32))
        psB2 = _ctx.enter_context(nc.psum_tensor("DpsB2", [128, 512], F32))
        psC2 = _ctx.enter_context(nc.psum_tensor("DpsC2", [128, 512], F32))
        psD2 = _ctx.enter_context(nc.psum_tensor("DpsD2", [128, 512], F32))
        ptr = _ctx.enter_context(nc.psum_tensor("Dptr", [128, 512], BF16))
        xnv = xnsb.ap().rearrange("p (c t) -> p c t", c=NCH)
        wkvv = wkvb.ap().rearrange("p (c t) -> p c t", c=NCH)
        wqv = wqb.ap().rearrange("p (c t) -> p c t", c=NCH)
        wpv = wpb.ap().rearrange("p (c t) -> p c t", c=NCH)
        qtv = qtok.ap().rearrange("p (b f) -> p b f", b=4)
        ktv = ktok.ap().rearrange("p (b f) -> p b f", b=4)
        vtv = vtok.ap().rearrange("p (b f) -> p b f", b=4)
        avv = avt.ap().rearrange("p (b f) -> p b f", b=4)
        avTv = avT.ap().rearrange("p (c t) -> p c t", c=NCH)
        av_ = atile.ap().rearrange("p (s h t) -> p s h t", s=4, h=16)
        amv = amx.ap().rearrange("p (s h) -> p s h", s=4)

        @sq.step(blk, "gpsimd")
        def _(eng, st):
            for c in range(NCH):
                st.dma(xnv[:, c], xn1T[128 * c:128 * (c + 1), :])
                st.dma(wkvv[:, c], wkv[128 * c:128 * (c + 1), :])
                st.dma(wqv[:, c], wq1[128 * c:128 * (c + 1), :])
                st.dma(wpv[:, c], wp1[128 * c:128 * (c + 1), :])
            for c in range(NCH):
                st.dma(bpc[:, c:c + 1], bp1[0:1, 128 * c:128 * (c + 1)].rearrange("o p -> p o"))

        @sq.step(blk, "gpsimd")
        def _(eng, st):
            return make_ident(nc, eng, identb[:, :], st)

        @sq.step(blk, "sync")
        def _(eng, st):
            st.dma(brow[0:1, :], bkv[0:1, :])

        @sq.step(blk, "vector")
        def _(eng, st):
            return eng.memset(onesr[:, :], 1.0)

        for i in range(4):
            @sq.step(blk, "tensor")
            def _(eng, st, i=i):
                return eng.matmul(psA[:, :], onesr[:, :],
                                  brow[0:1, 512 * i:512 * (i + 1)],
                                  start=True, stop=True)

            @sq.step(blk, "vector")
            def _(eng, st, i=i):
                return eng.tensor_copy(out=bkvb[:, 512 * i:512 * (i + 1)], in_=psA[:, :])

        @sq.step(blk, "sync")
        def _(eng, st):
            st.dma(brow[0:1, 0:1024], bq1[0:1, :])

        for i in range(2):
            @sq.step(blk, "tensor")
            def _(eng, st, i=i):
                return eng.matmul(psA[:, :], onesr[:, :],
                                  brow[0:1, 512 * i:512 * (i + 1)],
                                  start=True, stop=True)

            @sq.step(blk, "vector")
            def _(eng, st, i=i):
                return eng.tensor_copy(out=bqb[:, 512 * i:512 * (i + 1)], in_=psA[:, :])

        for jb in range(4):
            for b in range(4):
                tsl = slice(512 * b + 128 * jb, 512 * b + 128 * (jb + 1))
                for fg in range(4):
                    @sq.step(blk, "tensor")
                    def _(eng, st, fg=fg, tsl=tsl):
                        last = None
                        for k in range(NCH):
                            last = eng.matmul(psA[:, :],
                                              xnv[:, k][:, tsl],
                                              wkvv[:, k][:, 512 * fg:512 * (fg + 1)],
                                              start=(k == 0), stop=(k == NCH - 1))
                        return last

                    @sq.step(blk, "vector")
                    def _(eng, st, fg=fg, b=b):
                        dst = ktv[:, b][:, 512 * fg:512 * (fg + 1)] if fg < 2 else \
                            vtv[:, b][:, 512 * (fg - 2):512 * (fg - 1)]
                        return eng.tensor_tensor(out=dst, in0=psA[:, :],
                                                 in1=bkvb[:, 512 * fg:512 * (fg + 1)],
                                                 op=ALU.add)

                for fg in range(2):
                    @sq.step(blk, "tensor")
                    def _(eng, st, fg=fg, tsl=tsl):
                        last = None
                        for k in range(NCH):
                            last = eng.matmul(psA[:, :],
                                              xnv[:, k][:, tsl],
                                              wqv[:, k][:, 512 * fg:512 * (fg + 1)],
                                              start=(k == 0), stop=(k == NCH - 1))
                        return last

                    @sq.step(blk, "vector")
                    def _(eng, st, fg=fg, b=b):
                        return eng.tensor_tensor(out=qtv[:, b][:, 512 * fg:512 * (fg + 1)],
                                                 in0=psA[:, :],
                                                 in1=bqb[:, 512 * fg:512 * (fg + 1)],
                                                 op=ALU.add)

            for s in range(4):
                for t in range(4):
                    @sq.step(blk, "vector")
                    def _(eng, st, s=s, t=t):
                        st.chain(eng.tensor_tensor(out=tmp1[:, :], in0=qtv[:, s],
                                                   in1=ktv[:, t], op=ALU.mult))
                        return eng.reduce_sum(
                            out=av_[:, s, :, t].unsqueeze(-1),
                            in_=tmp1.ap().rearrange("p (h d) -> p h d", h=16),
                            axis=mybir.AxisListType.X)

            @sq.step(blk, "vector")
            def _(eng, st):
                return eng.reduce_max(out=amx[:, :], in_=atile[:, :].rearrange(
                    "p (s h t) -> p (s h) t", s=4, h=16), axis=mybir.AxisListType.X)

            @sq.step(blk, "vector")
            def _(eng, st):
                return eng.tensor_tensor(
                    out=atile[:, :].rearrange("p (s h t) -> p (s h) t", s=4, h=16),
                    in0=atile[:, :].rearrange("p (s h t) -> p (s h) t", s=4, h=16),
                    in1=amx[:, :].unsqueeze(-1).to_broadcast([128, 64, 4]),
                    op=ALU.subtract)

            @sq.step(blk, "scalar")
            def _(eng, st):
                return eng.activation(out=atile[:, :], in_=atile[:, :], func=AF.Exp,
                                      scale=1.0 / 8.0)

            @sq.step(blk, "vector")
            def _(eng, st):
                return eng.reduce_sum(out=amx[:, :], in_=atile[:, :].rearrange(
                    "p (s h t) -> p (s h) t", s=4, h=16), axis=mybir.AxisListType.X)

            @sq.step(blk, "vector")
            def _(eng, st):
                st.chain(eng.reciprocal(out=amx[:, :], in_=amx[:, :]))
                return eng.tensor_tensor(
                    out=atile[:, :].rearrange("p (s h t) -> p (s h) t", s=4, h=16),
                    in0=atile[:, :].rearrange("p (s h t) -> p (s h) t", s=4, h=16),
                    in1=amx[:, :].unsqueeze(-1).to_broadcast([128, 64, 4]),
                    op=ALU.mult)

            for s in range(4):
                for t in range(4):
                    @sq.step(blk, "vector")
                    def _(eng, st, s=s, t=t):
                        pview = av_[:, s, :, t].unsqueeze(-1).to_broadcast([128, 16, 64])
                        if t == 0:
                            return eng.tensor_tensor(
                                out=avv[:, s].rearrange("p (h d) -> p h d", h=16),
                                in0=vtv[:, t].rearrange("p (h d) -> p h d", h=16),
                                in1=pview, op=ALU.mult)
                        st.chain(eng.tensor_tensor(
                            out=tmp1.ap().rearrange("p (h d) -> p h d", h=16),
                            in0=vtv[:, t].rearrange("p (h d) -> p h d", h=16),
                            in1=pview, op=ALU.mult))
                        return eng.tensor_tensor(out=avv[:, s], in0=avv[:, s],
                                                 in1=tmp1[:, :], op=ALU.add)

            for s in range(4):
                for fg in range(2):
                    @sq.step(blk, "tensor")
                    def _(eng, st, s=s, fg=fg):
                        last = None
                        for u in range(4):
                            fc = 4 * fg + u
                            last = eng.transpose(ptr[:, 128 * u:128 * (u + 1)],
                                                 avv[:, s][:, 128 * fc:128 * (fc + 1)],
                                                 identb[:, :])
                        return last

                    @sq.step(blk, "vector")
                    def _(eng, st, s=s, fg=fg, jb=jb):
                        last = None
                        for u in range(4):
                            fc = 4 * fg + u
                            last = eng.tensor_copy(
                                out=avTv[:, fc][:, 512 * s + 128 * jb:512 * s + 128 * (jb + 1)],
                                in_=ptr[:, 128 * u:128 * (u + 1)])
                        return last

        for oc in range(NCH):
            for tt in range(NTT):
                @sq.step(blk, "tensor")
                def _(eng, st, oc=oc, tt=tt):
                    last = None
                    for k in range(NCH):
                        last = eng.matmul(psD2[:, :],
                                          wpv[:, k][:, 128 * oc:128 * (oc + 1)],
                                          avTv[:, k][:, 512 * tt:512 * (tt + 1)],
                                          start=(k == 0), stop=(k == NCH - 1))
                    return last

                @sq.step(blk, "sync")
                def _(eng, st, oc=oc, tt=tt):
                    st.dma(xbu[:, :], x1T[128 * oc:128 * (oc + 1), 512 * tt:512 * (tt + 1)])

                @sq.step(blk, "vector")
                def _(eng, st, oc=oc):
                    st.chain(eng.tensor_scalar(out=xao[:, :], in0=psD2[:, :],
                                               scalar1=bpc[:, oc:oc + 1], scalar2=None,
                                               op0=ALU.add))
                    return eng.tensor_tensor(out=xao[:, :], in0=xao[:, :],
                                             in1=xbu[:, :], op=ALU.add)

                @sq.step(blk, "sync")
                def _(eng, st, oc=oc, tt=tt):
                    st.dma(xBT[128 * oc:128 * (oc + 1), 512 * tt:512 * (tt + 1)],
                           xao[:, :])



def emit_round_inplace(eng, st, r, ti, tf, td):
    st.chain(eng.tensor_copy(out=ti, in_=r))
    st.chain(eng.tensor_copy(out=tf, in_=ti))
    st.chain(eng.tensor_tensor(out=td, in0=r, in1=tf, op=ALU.subtract))
    st.chain(eng.tensor_scalar(out=r, in0=td, scalar1=0.5, scalar2=None, op0=ALU.is_gt))
    st.chain(eng.tensor_tensor(out=tf, in0=tf, in1=r, op=ALU.add))
    st.chain(eng.tensor_scalar(out=r, in0=td, scalar1=-0.5, scalar2=None, op0=ALU.is_lt))
    return eng.tensor_tensor(out=r, in0=tf, in1=r, op=ALU.subtract)


def emit_qdq(eng, st, out_ap, x_ap, inv_col, s_col, r, ti, tf, td):
    st.chain(eng.tensor_scalar(out=r, in0=x_ap, scalar1=inv_col, scalar2=None,
                               op0=ALU.mult))
    st.chain(emit_round_inplace(eng, st, r, ti, tf, td))
    return eng.tensor_scalar(out=out_ap, in0=r, scalar1=s_col, scalar2=None,
                             op0=ALU.mult)


def emit_scale_from_max(sq, blk, nc, onesr, m_in_row, n_in, qmax, srow, scol, icol, ps):
    @sq.step(blk, "vector")
    def _(eng, st):
        st.chain(eng.reduce_max(out=srow[0:1, 0:1], in_=m_in_row[0:1, 0:n_in],
                                axis=mybir.AxisListType.X))
        st.chain(eng.tensor_scalar(out=srow[0:1, 1:2], in0=srow[0:1, 0:1],
                                   scalar1=1.0 / qmax, scalar2=1e-12,
                                   op0=ALU.mult, op1=ALU.max))
        return eng.reciprocal(out=srow[0:1, 2:3], in_=srow[0:1, 1:2])

    @sq.step(blk, "tensor")
    def _(eng, st):
        eng.matmul(ps[:, 0:1], onesr[:, :], srow[0:1, 1:2], start=True, stop=True)
        return eng.matmul(ps[:, 1:2], onesr[:, :], srow[0:1, 2:3], start=True, stop=True)

    @sq.step(blk, "vector")
    def _(eng, st):
        st.chain(eng.tensor_copy(out=scol, in_=ps[:, 0:1]))
        return eng.tensor_copy(out=icol, in_=ps[:, 1:2])


def emit_colmax_to_scalar(sq, blk, nc, mcol, identf, psT, mrow, out11):
    @sq.step(blk, "tensor")
    def _(eng, st):
        return eng.transpose(psT[0:1, 0:128], mcol, identf[:, :])

    @sq.step(blk, "vector")
    def _(eng, st):
        st.chain(eng.tensor_copy(out=mrow[0:1, :], in_=psT[0:1, 0:128]))
        return eng.reduce_max(out=out11, in_=mrow[0:1, :], axis=mybir.AxisListType.X)


def emit_dequant_chunk(eng, st, w_ap, wq_ap, nblk, sw, swr, r, ti, tf, td):
    wv = w_ap.rearrange("p (b i) -> p b i", i=32)
    st.chain(eng.reduce_max(out=sw, in_=wv, axis=mybir.AxisListType.X,
                            apply_absolute_value=True))
    st.chain(eng.tensor_scalar(out=sw, in0=sw, scalar1=1.0 / 7.0, scalar2=1e-12,
                               op0=ALU.mult, op1=ALU.max))
    st.chain(eng.reciprocal(out=swr, in_=sw))
    rv = r.rearrange("p (b i) -> p b i", i=32)
    st.chain(eng.tensor_tensor(out=rv, in0=wv,
                               in1=swr.unsqueeze(-1).to_broadcast(list(sw.shape) + [32]),
                               op=ALU.mult))
    st.chain(emit_round_inplace(eng, st, r, ti, tf, td))
    return eng.tensor_tensor(out=wq_ap.rearrange("p (b i) -> p b i", i=32), in0=rv,
                             in1=sw.unsqueeze(-1).to_broadcast(list(sw.shape) + [32]),
                             op=ALU.mult)



def emit_E1(nc, blk, sq, io):
    xn1F = io["xn1F"]
    wgn = io["wgn"]
    wun = io["wun"]
    upD = io["upD"]
    gateD = io["gateD"]

    NIC = I_L1 // 128

    from contextlib import ExitStack
    with ExitStack() as _ctx:
        xqsb = _ctx.enter_context(nc.sbuf_tensor("Exqsb", [128, NCH * TPC], BF16))
        wgT = _ctx.enter_context(nc.sbuf_tensor("EwgT", [128, NCH * I_L1], BF16))
        wuT = _ctx.enter_context(nc.sbuf_tensor("EwuT", [128, NCH * I_L1], BF16))
        wch = _ctx.enter_context(nc.sbuf_tensor("Ewch", [128, 1024], F32))
        wqc = _ctx.enter_context(nc.sbuf_tensor("Ewqc", [128, 1024], BF16))
        sw = _ctx.enter_context(nc.sbuf_tensor("Esw", [128, 32], F32))
        swr = _ctx.enter_context(nc.sbuf_tensor("Eswr", [128, 32], F32))
        rr = _ctx.enter_context(nc.sbuf_tensor("Err", [128, 2048], F32))
        ti = _ctx.enter_context(nc.sbuf_tensor("Eti", [128, 2048], I32))
        tf = _ctx.enter_context(nc.sbuf_tensor("Etf", [128, 2048], F32))
        td = _ctx.enter_context(nc.sbuf_tensor("Etd", [128, 2048], F32))
        xch = _ctx.enter_context(nc.sbuf_tensor("Exch", [128, 2048], F32))
        urow = _ctx.enter_context(nc.sbuf_tensor("Eurow", [128, 512], BF16))
        onesr = _ctx.enter_context(nc.sbuf_tensor("Eonesr", [1, 128], F32))
        identf = _ctx.enter_context(nc.sbuf_tensor("Eidentf", [128, 128], F32))
        identb = _ctx.enter_context(nc.sbuf_tensor("Eidentb", [128, 128], BF16))
        srow = _ctx.enter_context(nc.sbuf_tensor("Esrow", [1, 8], F32))
        mrow = _ctx.enter_context(nc.sbuf_tensor("Emrow", [1, 128], F32))
        scol = _ctx.enter_context(nc.sbuf_tensor("Escol", [128, 1], F32))
        icol = _ctx.enter_context(nc.sbuf_tensor("Eicol", [128, 1], F32))
        mxu = _ctx.enter_context(nc.sbuf_tensor("Emxu", [128, 3], F32))
        mxg = _ctx.enter_context(nc.sbuf_tensor("Emxg", [128, 3], F32))
        psA = _ctx.enter_context(nc.psum_tensor("EpsA", [128, 512], F32))
        psW = _ctx.enter_context(nc.psum_tensor("EpsW", [128, 512], BF16))
        psT = _ctx.enter_context(nc.psum_tensor("EpsT", [128, 128], F32))
        xqv = xqsb.ap().rearrange("p (c t) -> p c t", c=NCH)
        wgv = wgT.ap().rearrange("p (c t) -> p c t", c=NCH)
        wuv = wuT.ap().rearrange("p (c t) -> p c t", c=NCH)

        @sq.step(blk, "gpsimd")
        def _(eng, st):
            st.dma(srow[0:1, :], io["xmax_g"].rearrange("r o -> o r")[0:1, :])

        @sq.step(blk, "gpsimd")
        def _(eng, st):
            st.chain(make_ident(nc, eng, identf[:, :], st))
            return make_ident(nc, eng, identb[:, :], st)

        @sq.step(blk, "vector")
        def _(eng, st):
            eng.memset(onesr[:, :], 1.0)
            eng.memset(mxu[:, 0:1], 0.0)
            return eng.memset(mxg[:, 0:1], 0.0)

        emit_scale_from_max(sq, blk, nc, onesr, srow, 8, 32767.0, srow, scol[:, 0:1], icol[:, 0:1], psT)

        for c in range(NCH):
            @sq.step(blk, "sync")
            def _(eng, st, c=c):
                st.dma(xch[:, :], xn1F[128 * c:128 * (c + 1), :])

            @sq.step(blk, "vector")
            def _(eng, st, c=c):
                return emit_qdq(eng, st, xqv[:, c], xch[:, :], icol[:, 0:1],
                                scol[:, 0:1], rr[:, :], ti[:, :], tf[:, :], td[:, :])

        for which, wn, wTv in (("g", wgn, wgv), ("u", wun, wuv)):
            for oc in range(NIC):
                @sq.step(blk, "sync")
                def _(eng, st, wn=wn, oc=oc):
                    st.dma(wch[:, :], wn[128 * oc:128 * (oc + 1), :])

                @sq.step(blk, "vector")
                def _(eng, st):
                    return emit_dequant_chunk(eng, st, wch[:, :], wqc[:, :], 32,
                                              sw[:, :], swr[:, :], rr[:, 0:1024],
                                              ti[:, 0:1024], tf[:, 0:1024], td[:, 0:1024])

                for hc in range(NCH):
                    @sq.step(blk, "tensor")
                    def _(eng, st, hc=hc):
                        return eng.transpose(psW[:, 0:128],
                                             wqc[:, 128 * hc:128 * (hc + 1)],
                                             identb[:, :])

                    @sq.step(blk, "vector")
                    def _(eng, st, hc=hc, oc=oc, wTv=wTv):
                        return eng.tensor_copy(out=wTv[:, hc][:, 128 * oc:128 * (oc + 1)],
                                               in_=psW[:, 0:128])

        for which, wTv, outD, mx in (("g", wgv, gateD, mxg), ("u", wuv, upD, mxu)):
            for ic in range(NIC):
                for tt in range(NTT):
                    @sq.step(blk, "tensor")
                    def _(eng, st, ic=ic, tt=tt, wTv=wTv):
                        last = None
                        for k in range(NCH):
                            last = eng.matmul(psA[:, :],
                                              wTv[:, k][:, 128 * ic:128 * (ic + 1)],
                                              xqv[:, k][:, 512 * tt:512 * (tt + 1)],
                                              start=(k == 0), stop=(k == NCH - 1))
                        return last

                    @sq.step(blk, "vector")
                    def _(eng, st, mx=mx):
                        st.chain(eng.tensor_copy(out=urow[:, :], in_=psA[:, :]))
                        st.chain(eng.reduce_max(out=mx[:, 1:2], in_=psA[:, :],
                                                axis=mybir.AxisListType.X,
                                                apply_absolute_value=True))
                        return eng.tensor_tensor(out=mx[:, 0:1], in0=mx[:, 0:1],
                                                 in1=mx[:, 1:2], op=ALU.max)

                    @sq.step(blk, "sync")
                    def _(eng, st, ic=ic, tt=tt, outD=outD):
                        st.dma(outD[128 * ic:128 * (ic + 1), 512 * tt:512 * (tt + 1)],
                               urow[:, :])

        emit_colmax_to_scalar(sq, blk, nc, mxg[:, 0:1], identf, psT, mrow, srow[0:1, 3:4])
        emit_colmax_to_scalar(sq, blk, nc, mxu[:, 0:1], identf, psT, mrow, srow[0:1, 4:5])

        @sq.step(blk, "sync")
        def _(eng, st):
            st.dma(io["m1_in"][0:1, 0:2], srow[0:1, 3:5])



def emit_E2(nc, blk, sq, io):
    upD = io["upD"]
    gateD = io["gateD"]
    upqD = io["upqD"]
    g2D = io["g2D"]

    NIC = I_L1 // 128

    from contextlib import ExitStack
    with ExitStack() as _ctx:
        ch = _ctx.enter_context(nc.sbuf_tensor("Fch", [128, TPC], F32))
        ch2 = _ctx.enter_context(nc.sbuf_tensor("Fch2", [128, TPC], F32))
        chb = _ctx.enter_context(nc.sbuf_tensor("Fchb", [128, TPC], BF16))
        rr = _ctx.enter_context(nc.sbuf_tensor("Frr", [128, TPC], F32))
        ti = _ctx.enter_context(nc.sbuf_tensor("Fti", [128, TPC], I32))
        tf = _ctx.enter_context(nc.sbuf_tensor("Ftf", [128, TPC], F32))
        td = _ctx.enter_context(nc.sbuf_tensor("Ftd", [128, TPC], F32))
        onesr = _ctx.enter_context(nc.sbuf_tensor("Fonesr", [1, 128], F32))
        identf = _ctx.enter_context(nc.sbuf_tensor("Fidentf", [128, 128], F32))
        srow = _ctx.enter_context(nc.sbuf_tensor("Fsrow", [1, 8], F32))
        mrow = _ctx.enter_context(nc.sbuf_tensor("Fmrow", [1, 128], F32))
        sgc = _ctx.enter_context(nc.sbuf_tensor("Fsgc", [128, 2], F32))
        suc = _ctx.enter_context(nc.sbuf_tensor("Fsuc", [128, 2], F32))
        mxc = _ctx.enter_context(nc.sbuf_tensor("Fmxc", [128, 2], F32))
        psT = _ctx.enter_context(nc.psum_tensor("FpsT", [128, 128], F32))

        @sq.step(blk, "gpsimd")
        def _(eng, st):
            st.dma(srow[0:1, 0:8], io["m1_g"].rearrange("c two -> two c")[0:1, :],
                   allow_slow_non_contiguous=True)

        @sq.step(blk, "gpsimd")
        def _(eng, st):
            return make_ident(nc, eng, identf[:, :], st)

        @sq.step(blk, "vector")
        def _(eng, st):
            eng.memset(onesr[:, :], 1.0)
            return eng.memset(mxc[:, 0:1], 0.0)

        emit_scale_from_max(sq, blk, nc, onesr, srow, 8, 32767.0, srow,
                            sgc[:, 0:1], sgc[:, 1:2], psT)

        @sq.step(blk, "gpsimd")
        def _(eng, st):
            st.dma(srow[0:1, 0:8], io["m1_g"].rearrange("c two -> two c")[1:2, :],
                   allow_slow_non_contiguous=True)

        emit_scale_from_max(sq, blk, nc, onesr, srow, 8, 32767.0, srow,
                            suc[:, 0:1], suc[:, 1:2], psT)

        for ic in range(NIC):
            @sq.step(blk, "sync")
            def _(eng, st, ic=ic):
                st.dma(chb[:, :], gateD[128 * ic:128 * (ic + 1), :])

            @sq.step(blk, "vector")
            def _(eng, st):
                st.chain(eng.tensor_copy(out=ch[:, :], in_=chb[:, :]))
                return emit_qdq(eng, st, ch[:, :], ch[:, :], sgc[:, 1:2],
                                sgc[:, 0:1], rr[:, :], ti[:, :], tf[:, :], td[:, :])

            @sq.step(blk, "scalar")
            def _(eng, st):
                return eng.activation(out=ch2[:, :], in_=ch[:, :], func=AF.Sigmoid)

            @sq.step(blk, "vector")
            def _(eng, st):
                st.chain(eng.tensor_scalar(out=rr[:, :], in0=ch2[:, :],
                                           scalar1=65536.0, scalar2=None, op0=ALU.mult))
                st.chain(emit_round_inplace(eng, st, rr[:, :], ti[:, :], tf[:, :], td[:, :]))
                st.chain(eng.tensor_scalar(out=rr[:, :], in0=rr[:, :], scalar1=65535.0,
                                           scalar2=1.0 / 65536.0, op0=ALU.min, op1=ALU.mult))
                st.chain(eng.tensor_tensor(out=ch[:, :], in0=ch[:, :], in1=rr[:, :],
                                           op=ALU.mult))
                st.chain(eng.tensor_copy(out=chb[:, :], in_=ch[:, :]))
                st.chain(eng.reduce_max(out=mxc[:, 1:2], in_=ch[:, :],
                                        axis=mybir.AxisListType.X,
                                        apply_absolute_value=True))
                return eng.tensor_tensor(out=mxc[:, 0:1], in0=mxc[:, 0:1],
                                         in1=mxc[:, 1:2], op=ALU.max)

            @sq.step(blk, "sync")
            def _(eng, st, ic=ic):
                st.dma(g2D[128 * ic:128 * (ic + 1), :], chb[:, :])

            @sq.step(blk, "sync")
            def _(eng, st, ic=ic):
                st.dma(chb[:, :], upD[128 * ic:128 * (ic + 1), :])

            @sq.step(blk, "vector")
            def _(eng, st):
                st.chain(eng.tensor_copy(out=ch[:, :], in_=chb[:, :]))
                st.chain(emit_qdq(eng, st, ch[:, :], ch[:, :], suc[:, 1:2],
                                  suc[:, 0:1], rr[:, :], ti[:, :], tf[:, :], td[:, :]))
                return eng.tensor_copy(out=chb[:, :], in_=ch[:, :])

            @sq.step(blk, "sync")
            def _(eng, st, ic=ic):
                st.dma(upqD[128 * ic:128 * (ic + 1), :], chb[:, :])

        emit_colmax_to_scalar(sq, blk, nc, mxc[:, 0:1], identf, psT, mrow, srow[0:1, 3:4])

        @sq.step(blk, "sync")
        def _(eng, st):
            st.dma(io["mg2_in"][0:1, 0:1], srow[0:1, 3:4])



def emit_E3(nc, blk, sq, io):
    upqD = io["upqD"]
    g2D = io["g2D"]
    oD = io["oD"]

    NIC = I_L1 // 128

    from contextlib import ExitStack
    with ExitStack() as _ctx:
        ch = _ctx.enter_context(nc.sbuf_tensor("Gch", [128, TPC], F32))
        chb = _ctx.enter_context(nc.sbuf_tensor("Gchb", [128, TPC], BF16))
        chb2 = _ctx.enter_context(nc.sbuf_tensor("Gchb2", [128, TPC], BF16))
        rr = _ctx.enter_context(nc.sbuf_tensor("Grr", [128, TPC], F32))
        ti = _ctx.enter_context(nc.sbuf_tensor("Gti", [128, TPC], I32))
        tf = _ctx.enter_context(nc.sbuf_tensor("Gtf", [128, TPC], F32))
        td = _ctx.enter_context(nc.sbuf_tensor("Gtd", [128, TPC], F32))
        onesr = _ctx.enter_context(nc.sbuf_tensor("Gonesr", [1, 128], F32))
        identf = _ctx.enter_context(nc.sbuf_tensor("Gidentf", [128, 128], F32))
        srow = _ctx.enter_context(nc.sbuf_tensor("Gsrow", [1, 8], F32))
        mrow = _ctx.enter_context(nc.sbuf_tensor("Gmrow", [1, 128], F32))
        sgc = _ctx.enter_context(nc.sbuf_tensor("Gsgc", [128, 2], F32))
        mxc = _ctx.enter_context(nc.sbuf_tensor("Gmxc", [128, 2], F32))
        psT = _ctx.enter_context(nc.psum_tensor("GpsT", [128, 128], F32))

        @sq.step(blk, "gpsimd")
        def _(eng, st):
            st.dma(srow[0:1, 0:8], io["mg2_g"].rearrange("r o -> o r")[0:1, :])

        @sq.step(blk, "gpsimd")
        def _(eng, st):
            return make_ident(nc, eng, identf[:, :], st)

        @sq.step(blk, "vector")
        def _(eng, st):
            eng.memset(onesr[:, :], 1.0)
            return eng.memset(mxc[:, 0:1], 0.0)

        emit_scale_from_max(sq, blk, nc, onesr, srow, 8, 32767.0, srow,
                            sgc[:, 0:1], sgc[:, 1:2], psT)

        for ic in range(NIC):
            @sq.step(blk, "sync")
            def _(eng, st, ic=ic):
                st.dma(chb[:, :], g2D[128 * ic:128 * (ic + 1), :])
                st.dma(chb2[:, :], upqD[128 * ic:128 * (ic + 1), :])

            @sq.step(blk, "vector")
            def _(eng, st):
                st.chain(eng.tensor_copy(out=ch[:, :], in_=chb[:, :]))
                st.chain(emit_qdq(eng, st, ch[:, :], ch[:, :], sgc[:, 1:2],
                                  sgc[:, 0:1], rr[:, :], ti[:, :], tf[:, :], td[:, :]))
                st.chain(eng.tensor_tensor(out=ch[:, :], in0=ch[:, :], in1=chb2[:, :],
                                           op=ALU.mult))
                st.chain(eng.tensor_copy(out=chb[:, :], in_=ch[:, :]))
                st.chain(eng.reduce_max(out=mxc[:, 1:2], in_=ch[:, :],
                                        axis=mybir.AxisListType.X,
                                        apply_absolute_value=True))
                return eng.tensor_tensor(out=mxc[:, 0:1], in0=mxc[:, 0:1],
                                         in1=mxc[:, 1:2], op=ALU.max)

            @sq.step(blk, "sync")
            def _(eng, st, ic=ic):
                st.dma(oD[128 * ic:128 * (ic + 1), :], chb[:, :])

        emit_colmax_to_scalar(sq, blk, nc, mxc[:, 0:1], identf, psT, mrow, srow[0:1, 3:4])

        @sq.step(blk, "sync")
        def _(eng, st):
            st.dma(io["mo_in"][0:1, 0:1], srow[0:1, 3:4])



def emit_E4(nc, blk, sq, io):
    oD = io["oD"]
    wdn = io["wdn"]
    xBT = io["xBT"]
    xoutT = io["xoutT"]

    NIC = I_L1 // 128

    from contextlib import ExitStack
    with ExitStack() as _ctx:
        oq = _ctx.enter_context(nc.sbuf_tensor("Hoq", [128, NIC * TPC // 4], BF16))
        wdT = _ctx.enter_context(nc.sbuf_tensor("HwdT", [128, NIC * 1024], BF16))
        wch = _ctx.enter_context(nc.sbuf_tensor("Hwch", [128, I_L1], F32))
        wqc = _ctx.enter_context(nc.sbuf_tensor("Hwqc", [128, I_L1], BF16))
        sw = _ctx.enter_context(nc.sbuf_tensor("Hsw", [128, 96], F32))
        swr = _ctx.enter_context(nc.sbuf_tensor("Hswr", [128, 96], F32))
        rr = _ctx.enter_context(nc.sbuf_tensor("Hrr", [128, I_L1], F32))
        ti = _ctx.enter_context(nc.sbuf_tensor("Hti", [128, I_L1], I32))
        tf = _ctx.enter_context(nc.sbuf_tensor("Htf", [128, I_L1], F32))
        td = _ctx.enter_context(nc.sbuf_tensor("Htd", [128, I_L1], F32))
        chb = _ctx.enter_context(nc.sbuf_tensor("Hchb", [128, TPC], BF16))
        ch = _ctx.enter_context(nc.sbuf_tensor("Hch", [128, TPC], F32))
        onesr = _ctx.enter_context(nc.sbuf_tensor("Honesr", [1, 128], F32))
        identb = _ctx.enter_context(nc.sbuf_tensor("Hidentb", [128, 128], BF16))
        srow = _ctx.enter_context(nc.sbuf_tensor("Hsrow", [1, 8], F32))
        scol = _ctx.enter_context(nc.sbuf_tensor("Hscol", [128, 1], F32))
        icol = _ctx.enter_context(nc.sbuf_tensor("Hicol", [128, 1], F32))
        xbu = _ctx.enter_context(nc.sbuf_tensor("Hxbu", [128, 512], F32))
        xao = _ctx.enter_context(nc.sbuf_tensor("Hxao", [128, 512], F32))
        psA = _ctx.enter_context(nc.psum_tensor("HpsA", [128, 512], F32))
        psW = _ctx.enter_context(nc.psum_tensor("HpsW", [128, 128], BF16))
        psT = _ctx.enter_context(nc.psum_tensor("HpsT", [128, 128], F32))
        oqv = oq.ap().rearrange("p (c t) -> p c t", c=NIC)
        wdv = wdT.ap().rearrange("p (c t) -> p c t", c=NIC)

        @sq.step(blk, "gpsimd")
        def _(eng, st):
            st.dma(srow[0:1, 0:8], io["mo_g"].rearrange("r o -> o r")[0:1, :])

        @sq.step(blk, "gpsimd")
        def _(eng, st):
            return make_ident(nc, eng, identb[:, :], st)

        @sq.step(blk, "vector")
        def _(eng, st):
            return eng.memset(onesr[:, :], 1.0)

        emit_scale_from_max(sq, blk, nc, onesr, srow, 8, 32767.0, srow, scol[:, 0:1], icol[:, 0:1], psT)

        for hc in range(NCH):
            @sq.step(blk, "sync")
            def _(eng, st, hc=hc):
                st.dma(wch[:, :], wdn[128 * hc:128 * (hc + 1), :])

            @sq.step(blk, "vector")
            def _(eng, st):
                return emit_dequant_chunk(eng, st, wch[:, :], wqc[:, :], 96,
                                          sw[:, :], swr[:, :], rr[:, :],
                                          ti[:, :], tf[:, :], td[:, :])

            for ic in range(NIC):
                @sq.step(blk, "tensor")
                def _(eng, st, ic=ic):
                    return eng.transpose(psW[:, 0:128],
                                         wqc[:, 128 * ic:128 * (ic + 1)],
                                         identb[:, :])

                @sq.step(blk, "vector")
                def _(eng, st, ic=ic, hc=hc):
                    return eng.tensor_copy(out=wdv[:, ic][:, 128 * hc:128 * (hc + 1)],
                                           in_=psW[:, 0:128])

        for tt in range(NTT):
            for ic in range(NIC):
                @sq.step(blk, "sync")
                def _(eng, st, ic=ic, tt=tt):
                    st.dma(chb[:, 0:512], oD[128 * ic:128 * (ic + 1),
                                             512 * tt:512 * (tt + 1)])

                @sq.step(blk, "vector")
                def _(eng, st, ic=ic):
                    st.chain(eng.tensor_copy(out=ch[:, 0:512], in_=chb[:, 0:512]))
                    return emit_qdq(eng, st, oqv[:, ic], ch[:, 0:512], icol[:, 0:1],
                                    scol[:, 0:1], rr[:, 0:512], ti[:, 0:512],
                                    tf[:, 0:512], td[:, 0:512])

            for oc in range(NCH):
                @sq.step(blk, "tensor")
                def _(eng, st, oc=oc):
                    last = None
                    for k in range(NIC):
                        last = eng.matmul(psA[:, :],
                                          wdv[:, k][:, 128 * oc:128 * (oc + 1)],
                                          oqv[:, k], start=(k == 0), stop=(k == NIC - 1))
                    return last

                @sq.step(blk, "sync")
                def _(eng, st, oc=oc, tt=tt):
                    st.dma(xbu[:, :], xBT[128 * oc:128 * (oc + 1), 512 * tt:512 * (tt + 1)])

                @sq.step(blk, "vector")
                def _(eng, st):
                    return eng.tensor_tensor(out=xao[:, :], in0=psA[:, :],
                                             in1=xbu[:, :], op=ALU.add)

                @sq.step(blk, "sync")
                def _(eng, st, oc=oc, tt=tt):
                    st.dma(xoutT[128 * oc:128 * (oc + 1), 512 * tt:512 * (tt + 1)],
                           xao[:, :])



_EXT = [
    ("xT", [C, TPC], F32), ("posoff", [128, 1], F32),
    ("wq", [C, 1024], F32), ("wk", [C, 256], F32), ("wvv", [C, 256], F32),
    ("wvg", [C, 256], F32), ("wqi", [C, 256], F32), ("wki", [C, 64], F32),
    ("wwi", [C, 4], F32), ("wog", [C, 1024], F32),
    ("bvg", [1, 256], F32), ("bog", [1, 1024], F32),
    ("bi_in", [1, 4], F32), ("wo", [1024, 1024], F32),
    ("gwT", [C, 16], F32), ("ebias", [1, 16], F32),
    ("wgT", [16, C, 1024], F32), ("wuT", [16, C, 1024], F32),
    ("wdT", [16, 1024, C], F32),
    ("sgT", [C, 1024], F32), ("suT", [C, 1024], F32), ("sdT", [1024, C], F32),
    ("wkv", [C, 2048], F32), ("bkv", [1, 2048], F32),
    ("wq1", [C, 1024], F32), ("bq1", [1, 1024], F32),
    ("wp1", [C, 1024], F32), ("bp1", [1, 1024], F32),
    ("wgn", [I_L1, C], F32), ("wun", [I_L1, C], F32), ("wdn", [C, I_L1], F32),
]
_INT = [
    ("kv_loc", [TPC, 576], BF16), ("kv_all", [B * L, 576], BF16),
    ("qrT", [C, TPC], BF16), ("qiT", [256, TPC], BF16),
    ("g1T", [C, TPC], BF16), ("xnT", [C, TPC], BF16), ("wI_tok", [TPC, 4], F32),
    ("xAT", [C, TPC], F32),
    ("x1T", [C, TPC], F32), ("xn1T", [C, TPC], BF16), ("xn1F", [C, TPC], F32),
    ("xmax_in", [1, 1], F32), ("xmax_g", [NCORE, 1], F32),
    ("xBT", [C, TPC], F32),
    ("upD", [I_L1, TPC], BF16), ("gateD", [I_L1, TPC], BF16),
    ("m1_in", [1, 2], F32), ("m1_g", [NCORE, 2], F32),
    ("upqD", [I_L1, TPC], BF16), ("g2D", [I_L1, TPC], BF16),
    ("mg2_in", [1, 1], F32), ("mg2_g", [NCORE, 1], F32),
    ("oD", [I_L1, TPC], BF16), ("mo_in", [1, 1], F32), ("mo_g", [NCORE, 1], F32),
]


def build_full():
    nc = bass.Bass("TRN2", target_bir_lowering=False)
    io, ioh = {}, {}
    for name, shape, dt in _EXT:
        io[name] = nc.dram_tensor(name, shape, dt, kind="ExternalInput").ap()
    io["xoutT"] = nc.dram_tensor("xoutT", [C, TPC], F32, kind="ExternalOutput").ap()
    for name, shape, dt in _INT:
        h = nc.dram_tensor(name, shape, dt)
        ioh[name] = h
        io[name] = h.ap()
    with nc.Block() as blk:
        sq = Seq(nc)
        emit_A(nc, blk, sq, io)
        sq.collective(blk, ioh["kv_loc"], ioh["kv_all"])
        emit_B(nc, blk, sq, io)
        emit_C(nc, blk, sq, io)
        sq.collective(blk, ioh["xmax_in"], ioh["xmax_g"])
        emit_D(nc, blk, sq, io)
        emit_E1(nc, blk, sq, io)
        sq.collective(blk, ioh["m1_in"], ioh["m1_g"])
        emit_E2(nc, blk, sq, io)
        sq.collective(blk, ioh["mg2_in"], ioh["mg2_g"])
        emit_E3(nc, blk, sq, io)
        sq.collective(blk, ioh["mo_in"], ioh["mo_g"])
        emit_E4(nc, blk, sq, io)

        @sq.step(blk, "sync")
        def _(eng, st):
            return eng.nop()

    return nc


def _shard_x_T(x):
    out = []
    for c in range(NCORE):
        sl = x[:, 512 * c:512 * (c + 1), :]
        sl = sl.reshape(TPC, C).T
        out.append(np.ascontiguousarray(sl, dtype=np.float32))
    return out


def kernel(**inputs):
    inputs = {k: np.asarray(v) for k, v in inputs.items()}
    if "full" not in _prog_cache:
        _prog_cache["full"] = build_full()
    nc = _prog_cache["full"]
    f32 = lambda a: np.ascontiguousarray(np.asarray(a, np.float32))
    tr = lambda a, ax: np.ascontiguousarray(
        np.transpose(np.asarray(a, np.float32), ax))
    base = {
        "wq": f32(inputs["L0_attn_Wq"]),
        "wk": f32(inputs["L0_attn_Wk"]),
        "wvv": f32(inputs["L0_attn_Wv"]),
        "wvg": f32(inputs["L0_attn_Wvg"]),
        "wqi": f32(inputs["L0_attn_Wqi"]),
        "wki": f32(inputs["L0_attn_Wki"]),
        "wwi": f32(inputs["L0_attn_Wwi"]),
        "wog": f32(inputs["L0_attn_Wog"]),
        "bvg": f32(inputs["L0_attn_bvg"]).reshape(1, 256),
        "bog": f32(inputs["L0_attn_bog"]).reshape(1, 1024),
        "bi_in": f32(inputs["L0_attn_bi"]).reshape(1, 4),
        "wo": f32(inputs["L0_attn_Wo"]),
        "gwT": tr(inputs["L0_ffn_gate_w"], (1, 0)),
        "ebias": f32(inputs["L0_ffn_expert_bias"]).reshape(1, 16),
        "wgT": tr(inputs["L0_ffn_w_gate"], (0, 2, 1)),
        "wuT": tr(inputs["L0_ffn_w_up"], (0, 2, 1)),
        "wdT": tr(inputs["L0_ffn_w_down"], (0, 2, 1)),
        "sgT": tr(inputs["L0_ffn_shared_gate_w"], (1, 0)),
        "suT": tr(inputs["L0_ffn_shared_up_w"], (1, 0)),
        "sdT": tr(inputs["L0_ffn_shared_down_w"], (1, 0)),
        "wkv": f32(inputs["L1_attn_Wkv"]),
        "bkv": f32(inputs["L1_attn_bkv"]).reshape(1, 2048),
        "wq1": f32(inputs["L1_attn_Wq"]),
        "bq1": f32(inputs["L1_attn_bq"]).reshape(1, 1024),
        "wp1": f32(inputs["L1_attn_Wp"]),
        "bp1": f32(inputs["L1_attn_bp"]).reshape(1, 1024),
        "wgn": f32(inputs["L1_ffn_w_gate"]),
        "wun": f32(inputs["L1_ffn_w_up"]),
        "wdn": f32(inputs["L1_ffn_w_down"]),
    }
    xs = _shard_x_T(inputs["L0_attn_hidden_states"])
    in_maps = [dict(base, xT=xs[c],
                    posoff=np.full((128, 1), 512.0 * c, np.float32))
               for c in range(NCORE)]
    res = run_bass_kernel_spmd(nc, in_maps, core_ids=list(range(NCORE)))
    out = np.empty((B, L, C), np.float32)
    for c in range(NCORE):
        xo = np.asarray(res.results[c]["xoutT"])
        out[:, 512 * c:512 * (c + 1), :] = xo.T.reshape(B, 512, C)
    return out


# version 07
# speedup vs baseline: 1.3255x; 1.3255x over previous
import math
import numpy as np

import concourse.bass as bass
import concourse.mybir as mybir
from concourse.bass_utils import run_bass_kernel_spmd

F32 = mybir.dt.float32
F32R = mybir.dt.float32r
BF16 = mybir.dt.bfloat16
I32 = mybir.dt.int32
AF = mybir.ActivationFunctionType
ALU = mybir.AluOpType

NCORE = 8
B, L, C = 4, 4096, 1024
TPC = B * L // NCORE
PPC = L // NCORE
NCH = C // 128
NTT = TPC // 512
H, H_KV, D = 16, 4, 64
K_SEL = 256
H_I, D_I = 4, 64
ROPE_BASE = 10000.0
SM_SCALE = D ** -0.5
I_L1 = 3072
LN_EPS = 1e-5
NEG_BIG = -30000.0
TWO_PI = 2.0 * math.pi

ENGINES = ("sync", "tensor", "vector", "scalar", "gpsimd")

_prog_cache = {}


class Seq:

    def __init__(self, nc):
        self.nc = nc
        keys = list(ENGINES) + [e + "_dma" for e in ENGINES]
        self.sems = {e: nc.alloc_semaphore(f"sq_{e}") for e in keys}
        self.cnt = {e: 0 for e in keys}
        self.waited = {}

    def _emit_waits(self, eng, ename, pairs):
        cache = self.waited.setdefault(ename, {})
        for key, val in pairs:
            if val > cache.get(key, 0):
                eng.wait_ge(self.sems[key], val)
                cache[key] = val

    def collective(self, blk, in_handle, out_handle):
        key = f"coll{len([k for k in self.cnt if k.startswith('coll')])}"
        sem = self.nc.alloc_semaphore("sq_" + key)
        self.sems[key] = sem
        self.cnt[key] = 0
        waits = [(self.sems[e], self.cnt[e]) for e in self.cnt if self.cnt[e] > 0]

        def body(eng):
            for s, val in waits:
                eng.wait_ge(s, val)
            eng.collective_compute(
                "AllGather", mybir.AluOpType.bypass,
                replica_groups=[list(range(NCORE))],
                ins=[in_handle.ap().opt()], outs=[out_handle.ap().opt()],
            ).then_inc(sem)

        blk.gpsimd(body)
        self.cnt[key] = 1

    def step(self, blk, ename):
        waits = [(e, self.cnt[e]) for e in self.cnt if self.cnt[e] > 0]

        def deco(f):
            seq = self

            def body(eng):
                seq._emit_waits(eng, ename, waits)
                st = _Step(seq, ename, eng)
                last = f(eng, st)
                if last is not None:
                    last.then_inc(seq.sems[ename], 1)
                    seq.cnt[ename] += 1
                else:
                    assert st.marked, "step emitted nothing trackable"

            getattr(blk, ename)(body)

        return deco


class _Step:
    def __init__(self, seq, ename, eng):
        self.seq = seq
        self.ename = ename
        self.eng = eng
        self.marked = False

    def dma(self, out, in_, **kw):
        key = self.ename + "_dma"
        inst = self.eng.dma_start(out=out, in_=in_, **kw)
        inst.then_inc(self.seq.sems[key], 16)
        self.seq.cnt[key] += 16
        self.marked = True
        return inst

    def chain(self, inst):
        inst.then_inc(self.seq.sems[self.ename], 1)
        self.seq.cnt[self.ename] += 1
        self.eng.wait_ge(self.seq.sems[self.ename], self.seq.cnt[self.ename])
        self.seq.waited.setdefault(self.ename, {})[self.ename] = \
            self.seq.cnt[self.ename]
        self.marked = True
        return inst


def make_rot_T(nc, g, rt, st):
    st.chain(g.memset(rt, 0.0))
    last = None
    for blk_ in (0, 64):
        g.affine_select(out=rt[:, blk_:blk_ + 32], in_=rt[:, blk_:blk_ + 32],
                        compare_op=ALU.not_equal, fill=-1.0,
                        base=-(blk_ + 32), pattern=[[-1, 32]], channel_multiplier=1)
        last = g.affine_select(out=rt[:, blk_ + 32:blk_ + 64], in_=rt[:, blk_ + 32:blk_ + 64],
                               compare_op=ALU.not_equal, fill=1.0,
                               base=-blk_, pattern=[[-1, 32]], channel_multiplier=1)
    return last


def make_ident(nc, g, ident, st):
    st.chain(g.memset(ident, 0.0))
    return g.affine_select(out=ident, in_=ident, compare_op=ALU.not_equal, fill=1.0,
                           base=0, pattern=[[-1, ident.shape[1]]], channel_multiplier=1)



def emit_A(nc, blk, sq, io):
    xT = io["xT"]
    win = {}
    for name, cols in [("wq", 1024), ("wk", 256), ("wvv", 256), ("wvg", 256),
                       ("wqi", 256), ("wki", 64), ("wwi", 4), ("wog", 1024)]:
        win[name] = (io[name], cols)
    posoff = io["posoff"]
    bvg = io["bvg"]
    bog = io["bog"]
    kv_loc = io["kv_loc"]
    qrT = io["qrT"]
    qiT = io["qiT"]
    g1T = io["g1T"]
    xnT = io["xnT"]
    wI_tok = io["wI_tok"]

    woff = {}
    off = 0
    for name in ("wq", "wk", "wvv", "wvg", "wqi", "wki", "wwi", "wog"):
        woff[name] = off
        off += win[name][1]
    WCOLS = off

    from contextlib import ExitStack
    with ExitStack() as _ctx:
        wsb = _ctx.enter_context(nc.sbuf_tensor("Awsb", [128, NCH * WCOLS], BF16))
        xtt = _ctx.enter_context(nc.sbuf_tensor("Axtt", [128, NCH * 512], F32))
        xntt = _ctx.enter_context(nc.sbuf_tensor("Axntt", [128, NCH * 512], BF16))
        x2 = _ctx.enter_context(nc.sbuf_tensor("Ax2", [128, 512], F32))
        onesf = _ctx.enter_context(nc.sbuf_tensor("Aonesf", [128, 1], F32))
        identb = _ctx.enter_context(nc.sbuf_tensor("Aidentb", [128, 128], BF16))
        srow = _ctx.enter_context(nc.sbuf_tensor("Asrow", [1, 3 * 512], F32))
        mb = _ctx.enter_context(nc.sbuf_tensor("Amb", [128, 512], F32))
        rb = _ctx.enter_context(nc.sbuf_tensor("Arb", [128, 512], F32))
        rotT = _ctx.enter_context(nc.sbuf_tensor("ArotT", [128, 128], BF16))
        costab = _ctx.enter_context(nc.sbuf_tensor("Acostab", [128, 512], BF16))
        sintab = _ctx.enter_context(nc.sbuf_tensor("Asintab", [128, 512], BF16))
        tsc = _ctx.enter_context(nc.sbuf_tensor("Atsc", [128, 512], F32))
        tsc2 = _ctx.enter_context(nc.sbuf_tensor("Atsc2", [128, 512], F32))
        tin = _ctx.enter_context(nc.sbuf_tensor("Atin", [128, 512], I32))
        ifr = _ctx.enter_context(nc.sbuf_tensor("Aifr", [128, 1], F32))
        pidf = _ctx.enter_context(nc.sbuf_tensor("Apidf", [128, 1], F32))
        epsc = _ctx.enter_context(nc.sbuf_tensor("Aepsc", [1, 1], F32))
        onesr = _ctx.enter_context(nc.sbuf_tensor("Aonesr", [1, 128], F32))
        qsb = _ctx.enter_context(nc.sbuf_tensor("Aqsb", [128, 512], BF16))
        rosb = _ctx.enter_context(nc.sbuf_tensor("Arosb", [128, 512], BF16))
        t1 = _ctx.enter_context(nc.sbuf_tensor("At1", [128, 512], F32))
        gsb = _ctx.enter_context(nc.sbuf_tensor("Agsb", [128, 512], BF16))
        biasb = _ctx.enter_context(nc.sbuf_tensor("Abiasb", [128, 10], F32))
        wisb = _ctx.enter_context(nc.sbuf_tensor("Awisb", [128, 4], F32))
        ktt = _ctx.enter_context(nc.sbuf_tensor("Aktt", [128, 5 * 512], BF16))
        kvsb = _ctx.enter_context(nc.sbuf_tensor("Akvsb", [128, 4 * 576], BF16))
        ps0 = _ctx.enter_context(nc.psum_tensor("Aps0", [128, 512], F32))
        ps1 = _ctx.enter_context(nc.psum_tensor("Aps1", [128, 512], F32))
        ps2 = _ctx.enter_context(nc.psum_tensor("Aps2", [128, 512], F32))
        prow = _ctx.enter_context(nc.psum_tensor("Aprow", [1, 1024], F32))
        pqt = _ctx.enter_context(nc.psum_tensor("Apqt", [128, 4], F32))
        ptr = _ctx.enter_context(nc.psum_tensor("Aptr", [128, 640], BF16))
        wv_ = wsb.ap().rearrange("p (k c) -> p k c", k=NCH)
        xv_ = xtt.ap().rearrange("p (c t) -> p c t", c=NCH)
        xnv = xntt.ap().rearrange("p (c t) -> p c t", c=NCH)
        ktv = ktt.ap().rearrange("p (c t) -> p c t", c=5)

        def W(name, k, c0, c1):
            return wv_[:, k, woff[name] + c0:woff[name] + c1]

        @sq.step(blk, "gpsimd")
        def _(eng, st):
            for name in woff:
                wap, cols = win[name]
                for k in range(NCH):
                    st.dma(wv_[:, k, woff[name]:woff[name] + cols],
                           wap[128 * k:128 * (k + 1), :])
            st.dma(pidf[:, :], posoff[:, :])
            st.dma(biasb[:, 0:1], bvg[0:1, 0:128].rearrange("o p -> p o"))
            st.dma(biasb[:, 1:2], bvg[0:1, 128:256].rearrange("o p -> p o"))
            for cc in range(8):
                st.dma(biasb[:, 2 + cc:3 + cc],
                       bog[0:1, 128 * cc:128 * (cc + 1)].rearrange("o p -> p o"))

        @sq.step(blk, "gpsimd")
        def _(eng, st):
            make_rot_T(nc, eng, rotT[:, :], st)
            make_ident(nc, eng, identb[:, :], st)
            eng.iota(tin[:, 0:1], pattern=[[0, 1]], base=0, channel_multiplier=1)
            return eng.iota(tin[:, 1:2], pattern=[[0, 1]], base=0, channel_multiplier=1)

        @sq.step(blk, "vector")
        def _(eng, st):
            eng.memset(onesf[:, :], 1.0)
            eng.memset(epsc[:, :], LN_EPS)
            eng.memset(onesr[:, :], 1.0)
            st.chain(eng.tensor_copy(out=t1[:, 0:1], in_=tin[:, 0:1]))
            return eng.tensor_copy(out=ifr[:, :], in_=t1[:, 0:1])

        for thr in (32.0, 64.0, 96.0):
            @sq.step(blk, "vector")
            def _(eng, st, thr=thr):
                st.chain(eng.tensor_scalar(out=tsc[:, 0:1], in0=t1[:, 0:1], scalar1=thr,
                                           scalar2=-32.0, op0=ALU.is_ge, op1=ALU.mult))
                return eng.tensor_tensor(out=ifr[:, :], in0=ifr[:, :],
                                         in1=tsc[:, 0:1], op=ALU.add)

        @sq.step(blk, "scalar")
        def _(eng, st):
            return eng.activation(out=ifr[:, :], in_=ifr[:, :], func=AF.Exp,
                                  scale=-math.log(ROPE_BASE) / 32.0)

        @sq.step(blk, "gpsimd")
        def _(eng, st):
            return eng.iota(tin[:, :], pattern=[[1, 512]], base=0, channel_multiplier=0)

        @sq.step(blk, "vector")
        def _(eng, st):
            return eng.tensor_copy(out=tsc[:, :], in_=tin[:, :])

        @sq.step(blk, "vector")
        def _(eng, st):
            return eng.tensor_scalar(out=tsc[:, :], in0=tsc[:, :],
                                     scalar1=pidf[:, 0:1], scalar2=None, op0=ALU.add)

        @sq.step(blk, "vector")
        def _(eng, st):
            st.chain(eng.tensor_scalar(out=tsc[:, :], in0=tsc[:, :], scalar1=ifr[:, 0:1],
                                       scalar2=None, op0=ALU.mult))
            return eng.tensor_scalar(out=tsc2[:, :], in0=tsc[:, :],
                                     scalar1=1.0 / TWO_PI, scalar2=None, op0=ALU.mult)

        @sq.step(blk, "vector")
        def _(eng, st):
            return eng.tensor_copy(out=tin[:, :], in_=tsc2[:, :])

        @sq.step(blk, "vector")
        def _(eng, st):
            return eng.tensor_copy(out=tsc[:, :], in_=tin[:, :])

        @sq.step(blk, "vector")
        def _(eng, st):
            st.chain(eng.tensor_tensor(out=tsc2[:, :], in0=tsc2[:, :], in1=tsc[:, :],
                                       op=ALU.subtract))
            st.chain(eng.tensor_scalar(out=t1[:, :], in0=tsc2[:, :], scalar1=0.5,
                                       scalar2=None, op0=ALU.is_gt))
            st.chain(eng.tensor_tensor(out=tsc2[:, :], in0=tsc2[:, :], in1=t1[:, :],
                                       op=ALU.subtract))
            st.chain(eng.tensor_scalar(out=t1[:, :], in0=tsc2[:, :], scalar1=-0.5,
                                       scalar2=None, op0=ALU.is_lt))
            st.chain(eng.tensor_tensor(out=tsc2[:, :], in0=tsc2[:, :], in1=t1[:, :],
                                       op=ALU.add))
            return eng.tensor_scalar(out=tsc[:, :], in0=tsc2[:, :],
                                     scalar1=TWO_PI, scalar2=None, op0=ALU.mult)

        @sq.step(blk, "scalar")
        def _(eng, st):
            return eng.activation(out=sintab[:, :], in_=tsc[:, :], func=AF.Sin)

        @sq.step(blk, "vector")
        def _(eng, st):
            st.chain(eng.tensor_scalar(out=tsc2[:, :], in0=tsc[:, :], scalar1=-1.0,
                                       scalar2=math.pi / 2.0, op0=ALU.mult, op1=ALU.add))
            st.chain(eng.tensor_scalar(out=tsc[:, :], in0=tsc2[:, :], scalar1=math.pi,
                                       scalar2=-TWO_PI, op0=ALU.is_gt, op1=ALU.mult))
            return eng.tensor_tensor(out=tsc2[:, :], in0=tsc2[:, :], in1=tsc[:, :],
                                     op=ALU.add)

        @sq.step(blk, "scalar")
        def _(eng, st):
            return eng.activation(out=costab[:, :], in_=tsc2[:, :], func=AF.Sin)

        for tt in range(NTT):
            @sq.step(blk, "sync")
            def _(eng, st, tt=tt):
                for c in range(NCH):
                    st.dma(xv_[:, c], xT[128 * c:128 * (c + 1), 512 * tt:512 * (tt + 1)])

            for c in range(NCH):
                @sq.step(blk, "vector")
                def _(eng, st, c=c):
                    return eng.tensor_tensor(out=x2[:, :], in0=xv_[:, c],
                                             in1=xv_[:, c], op=ALU.mult)

                @sq.step(blk, "tensor")
                def _(eng, st, c=c):
                    eng.matmul(prow[:, 0:512], onesf[:, :],
                               xv_[:, c],
                               start=(c == 0), stop=(c == NCH - 1))
                    return eng.matmul(prow[:, 512:1024], onesf[:, :],
                                      x2[:, :],
                                      start=(c == 0), stop=(c == NCH - 1))

            @sq.step(blk, "vector")
            def _(eng, st):
                eng.tensor_scalar(out=srow[:, 0:512], in0=prow[:, 0:512],
                                  scalar1=1.0 / C, scalar2=None, op0=ALU.mult)
                st.chain(eng.tensor_scalar(out=srow[:, 512:1024], in0=prow[:, 512:1024],
                                           scalar1=1.0 / C, scalar2=None, op0=ALU.mult))
                st.chain(eng.tensor_tensor(out=srow[:, 1024:1536], in0=srow[:, 0:512],
                                           in1=srow[:, 0:512], op=ALU.mult))
                return eng.tensor_tensor(out=srow[:, 512:1024], in0=srow[:, 512:1024],
                                         in1=srow[:, 1024:1536], op=ALU.subtract)

            @sq.step(blk, "scalar")
            def _(eng, st):
                return eng.activation(out=srow[:, 1024:1536], in_=srow[:, 512:1024],
                                      func=AF.Sqrt, bias=epsc[0:1, 0:1])

            @sq.step(blk, "vector")
            def _(eng, st):
                return eng.reciprocal(out=srow[:, 512:1024], in_=srow[:, 1024:1536])

            @sq.step(blk, "tensor")
            def _(eng, st):
                eng.matmul(ps0[:, :], onesr[:, :],
                           srow[:, 0:512], start=True, stop=True)
                return eng.matmul(ps1[:, :], onesr[:, :],
                                  srow[:, 512:1024], start=True, stop=True)

            @sq.step(blk, "vector")
            def _(eng, st):
                eng.tensor_copy(out=mb[:, :], in_=ps0[:, :])
                return eng.tensor_copy(out=rb[:, :], in_=ps1[:, :])

            for c in range(NCH):
                @sq.step(blk, "vector")
                def _(eng, st, c=c):
                    st.chain(eng.tensor_tensor(out=x2[:, :], in0=xv_[:, c], in1=mb[:, :],
                                               op=ALU.subtract))
                    return eng.tensor_tensor(out=xnv[:, c], in0=x2[:, :], in1=rb[:, :],
                                             op=ALU.mult)

            @sq.step(blk, "sync")
            def _(eng, st, tt=tt):
                for c in range(NCH):
                    st.dma(xnT[128 * c:128 * (c + 1), 512 * tt:512 * (tt + 1)],
                           xnv[:, c])

            def proj_step(name, oc, psb, c0=None, c1=None):
                c0 = 128 * oc if c0 is None else c0
                c1 = 128 * (oc + 1) if c1 is None else c1

                @sq.step(blk, "tensor")
                def _(eng, st):
                    last = None
                    for k in range(NCH):
                        last = eng.matmul(psb[0:c1 - c0, 0:512], W(name, k, c0, c1),
                                          xnv[:, k], start=(k == 0), stop=(k == NCH - 1))
                    return last

            def rope_steps(oc, dst_ap, dma_to=None, tt=tt):
                @sq.step(blk, "scalar")
                def _(eng, st):
                    return eng.activation(out=qsb[:, :], in_=ps0[:, :], func=AF.Copy)

                @sq.step(blk, "tensor")
                def _(eng, st):
                    return eng.matmul(ps1[:, :], rotT[:, :], qsb[:, :],
                                      start=True, stop=True)

                @sq.step(blk, "scalar")
                def _(eng, st):
                    return eng.activation(out=rosb[:, :], in_=ps1[:, :], func=AF.Copy)

                @sq.step(blk, "vector")
                def _(eng, st):
                    eng.tensor_tensor(out=t1[:, :], in0=qsb[:, :], in1=costab[:, :],
                                      op=ALU.mult)
                    st.chain(eng.tensor_tensor(out=x2[:, :], in0=rosb[:, :], in1=sintab[:, :],
                                               op=ALU.mult))
                    return eng.tensor_tensor(out=dst_ap, in0=t1[:, :], in1=x2[:, :],
                                             op=ALU.add)

                if dma_to is not None:
                    @sq.step(blk, "sync")
                    def _(eng, st):
                        st.dma(dma_to, dst_ap)

            for oc in range(NCH):
                proj_step("wq", oc, ps0)
                rope_steps(oc, qsb[:, :],
                           dma_to=qrT[128 * oc:128 * (oc + 1), 512 * tt:512 * (tt + 1)])

            for oc in range(2):
                proj_step("wk", oc, ps0)
                rope_steps(oc, ktv[:, oc])

            for oc in range(2):
                proj_step("wvv", oc, ps0)
                proj_step("wvg", oc, ps1)

                @sq.step(blk, "scalar")
                def _(eng, st, oc=oc):
                    return eng.activation(out=gsb[:, :], in_=ps1[:, :],
                                          func=AF.Sigmoid, bias=biasb[:, oc:oc + 1])

                @sq.step(blk, "vector")
                def _(eng, st, oc=oc):
                    return eng.tensor_tensor(out=ktv[:, 2 + oc], in0=ps0[:, :],
                                             in1=gsb[:, :], op=ALU.mult)

            proj_step("wki", 0, ps0, 0, 64)

            @sq.step(blk, "vector")
            def _(eng, st):
                st.chain(eng.memset(ktv[:, 4], 0.0))
                return eng.tensor_copy(out=ktv[:, 4][0:64, :], in_=ps0[0:64, 0:512])

            for oc in range(2):
                proj_step("wqi", oc, ps2)

                @sq.step(blk, "scalar")
                def _(eng, st):
                    return eng.activation(out=gsb[:, :], in_=ps2[:, :], func=AF.Copy)

                @sq.step(blk, "sync")
                def _(eng, st, oc=oc, tt=tt):
                    st.dma(qiT[128 * oc:128 * (oc + 1), 512 * tt:512 * (tt + 1)],
                           gsb[:, :])

            for oc in range(NCH):
                proj_step("wog", oc, ps2)

                @sq.step(blk, "scalar")
                def _(eng, st, oc=oc):
                    return eng.activation(out=gsb[:, :], in_=ps2[:, :],
                                          func=AF.Sigmoid, bias=biasb[:, 2 + oc:3 + oc])

                @sq.step(blk, "sync")
                def _(eng, st, oc=oc, tt=tt):
                    st.dma(g1T[128 * oc:128 * (oc + 1), 512 * tt:512 * (tt + 1)],
                           gsb[:, :])

            for qs in range(4):
                @sq.step(blk, "tensor")
                def _(eng, st, qs=qs):
                    last = None
                    for k in range(NCH):
                        last = eng.matmul(pqt[:, :], xnv[:, k][:, 128 * qs:128 * (qs + 1)],
                                          W("wwi", k, 0, 4), start=(k == 0), stop=(k == NCH - 1))
                    return last

                @sq.step(blk, "scalar")
                def _(eng, st):
                    return eng.activation(out=wisb[:, :], in_=pqt[:, :], func=AF.Sigmoid)

                @sq.step(blk, "sync")
                def _(eng, st, qs=qs, tt=tt):
                    st.dma(wI_tok[512 * tt + 128 * qs:512 * tt + 128 * (qs + 1), :],
                           wisb[:, :])

            for ts in range(4):
                @sq.step(blk, "tensor")
                def _(eng, st, ts=ts):
                    last = None
                    for fc in range(5):
                        last = eng.transpose(ptr[:, 128 * fc:128 * (fc + 1)],
                                             ktv[:, fc][:, 128 * ts:128 * (ts + 1)],
                                             identb[:, :])
                    return last

                @sq.step(blk, "vector")
                def _(eng, st, ts=ts):
                    for fc in range(4):
                        eng.tensor_copy(out=kvsb[:, 576 * ts + 128 * fc:576 * ts + 128 * (fc + 1)],
                                        in_=ptr[:, 128 * fc:128 * (fc + 1)])
                    return eng.tensor_copy(out=kvsb[:, 576 * ts + 512:576 * ts + 576],
                                           in_=ptr[:, 512:576])

            @sq.step(blk, "gpsimd")
            def _(eng, st, tt=tt):
                for ts in range(4):
                    st.dma(kv_loc[512 * tt + 128 * ts:512 * tt + 128 * (ts + 1), :],
                           kvsb[:, 576 * ts:576 * (ts + 1)])



def emit_B(nc, blk, sq, io, n_batches=4, bisect_iters=34):
    qrT = io["qrT"]
    qiT = io["qiT"]
    g1T = io["g1T"]
    xT = io["xT"]
    wI_tok = io["wI_tok"]
    kv_all = io["kv_all"]
    bi_in = io["bi_in"]
    wo = io["wo"]
    xAT = io["xAT"]

    from contextlib import ExitStack
    with ExitStack() as _ctx:
        wosb = _ctx.enter_context(nc.sbuf_tensor("Bwosb", [128, NCH * 1024], BF16))
        identb = _ctx.enter_context(nc.sbuf_tensor("Bidentb", [128, 128], BF16))
        onesr = _ctx.enter_context(nc.sbuf_tensor("Bonesr", [1, 128], F32))
        bib = _ctx.enter_context(nc.sbuf_tensor("Bbib", [128, 4], F32))
        kt64 = _ctx.enter_context(nc.sbuf_tensor("Bkt64", [64, 4 * L], BF16))
        ki64 = _ctx.enter_context(nc.sbuf_tensor("Bki64", [64, L], BF16))
        vtok = _ctx.enter_context(nc.sbuf_tensor("Bvtok", [128, 32 * 260], BF16))
        qr64 = _ctx.enter_context(nc.sbuf_tensor("Bqr64", [64, H * 512], BF16))
        qi64 = _ctx.enter_context(nc.sbuf_tensor("Bqi64", [64, 4 * 512], BF16))
        wib = _ctx.enter_context(nc.sbuf_tensor("Bwib", [128, 4 * 4], F32))
        ktok = _ctx.enter_context(nc.sbuf_tensor("Bktok", [128, 384], BF16))
        mtsb = _ctx.enter_context(nc.sbuf_tensor("Bmtsb", [128, 32 * 512], BF16))
        isb = _ctx.enter_context(nc.sbuf_tensor("Bisb", [128, L], F32))
        scr = _ctx.enter_context(nc.sbuf_tensor("Bscr", [128, L], BF16))
        bst = _ctx.enter_context(nc.sbuf_tensor("Bbst", [128, 8], F32))
        sx0 = _ctx.enter_context(nc.sbuf_tensor("Bsx0", [128, 512], F32))
        sx1 = _ctx.enter_context(nc.sbuf_tensor("Bsx1", [128, 512], F32))
        sx2 = _ctx.enter_context(nc.sbuf_tensor("Bsx2", [128, 512], F32))
        sx3 = _ctx.enter_context(nc.sbuf_tensor("Bsx3", [128, 512], F32))
        pt0 = _ctx.enter_context(nc.sbuf_tensor("Bpt0", [128, 512], BF16))
        pt1 = _ctx.enter_context(nc.sbuf_tensor("Bpt1", [128, 512], BF16))
        pt2 = _ctx.enter_context(nc.sbuf_tensor("Bpt2", [128, 512], BF16))
        pt3 = _ctx.enter_context(nc.sbuf_tensor("Bpt3", [128, 512], BF16))
        drw = _ctx.enter_context(nc.sbuf_tensor("Bdrw", [1, 512], F32))
        drb = _ctx.enter_context(nc.sbuf_tensor("Bdrb", [64, 512], F32))
        att = _ctx.enter_context(nc.sbuf_tensor("Batt", [128, NCH * 512], BF16))
        g1b = _ctx.enter_context(nc.sbuf_tensor("Bg1b", [128, NCH * 512], BF16))
        xbu = _ctx.enter_context(nc.sbuf_tensor("Bxbu", [128, 512], F32))
        xao = _ctx.enter_context(nc.sbuf_tensor("Bxao", [128, 512], F32))
        psS0 = _ctx.enter_context(nc.psum_tensor("BpsS0", [128, 512], F32))
        psS1 = _ctx.enter_context(nc.psum_tensor("BpsS1", [128, 512], F32))
        psS2 = _ctx.enter_context(nc.psum_tensor("BpsS2", [128, 512], F32))
        psS3 = _ctx.enter_context(nc.psum_tensor("BpsS3", [128, 512], F32))
        psO = _ctx.enter_context(nc.psum_tensor("BpsO", [65, 512], F32))
        psB = _ctx.enter_context(nc.psum_tensor("BpsB", [64, 512], F32))
        ptr = _ctx.enter_context(nc.psum_tensor("Bptr", [128, 512], BF16))
        psY = _ctx.enter_context(nc.psum_tensor("BpsY", [128, 512], F32))
        psS = (psS0, psS1, psS2, psS3)
        sxs = (sx0, sx1, sx2, sx3)
        pts = (pt0, pt1, pt2, pt3)
        vtv = vtok.ap().rearrange("p (c t) -> p c t", c=32)
        mtv = mtsb.ap().rearrange("p (c t) -> p c t", c=32)
        attv = att.ap().rearrange("p (c t) -> p c t", c=NCH)
        g1v = g1b.ap().rearrange("p (c t) -> p c t", c=NCH)
        wov = wosb.ap().rearrange("p (c t) -> p c t", c=NCH)
        ktg = kt64.ap().rearrange("p (g t) -> p g t", g=4)
        qrh = qr64.ap().rearrange("p (h t) -> p h t", h=H)
        qih = qi64.ap().rearrange("p (h t) -> p h t", h=4)

        @sq.step(blk, "gpsimd")
        def _(eng, st):
            for k in range(NCH):
                st.dma(wov[:, k], wo[128 * k:128 * (k + 1), :])

        @sq.step(blk, "gpsimd")
        def _(eng, st):
            return make_ident(nc, eng, identb[:, :], st)

        @sq.step(blk, "vector")
        def _(eng, st):
            eng.memset(onesr[:, :], 1.0)
            return eng.memset(
                vtok.ap().rearrange("p (c g o) -> p c g o", c=32, g=4)[:, :, :, 64:65], 1.0)

        @sq.step(blk, "sync")
        def _(eng, st):
            st.dma(drw[0:1, 0:4], bi_in[0:1, :])

        @sq.step(blk, "tensor")
        def _(eng, st):
            return eng.matmul(psS0[:, 0:4], onesr[:, :],
                              drw[0:1, 0:4], start=True, stop=True)

        @sq.step(blk, "vector")
        def _(eng, st):
            return eng.tensor_copy(out=bib[:, :], in_=psS0[:, 0:4])

        for b in range(n_batches):
            @sq.step(blk, "sync")
            def _(eng, st, b=b):
                for h in range(H):
                    st.dma(qrh[0:64, h, :],
                           qrT[64 * h:64 * (h + 1), 512 * b:512 * (b + 1)])
                for h in range(4):
                    st.dma(qih[0:64, h, :],
                           qiT[64 * h:64 * (h + 1), 512 * b:512 * (b + 1)])
                for qt in range(4):
                    st.dma(wib[:, 4 * qt:4 * (qt + 1)],
                           wI_tok[512 * b + 128 * qt:512 * b + 128 * (qt + 1), :])
                for c in range(NCH):
                    st.dma(g1v[:, c], g1T[128 * c:128 * (c + 1), 512 * b:512 * (b + 1)])

            @sq.step(blk, "gpsimd")
            def _(eng, st, b=b):
                for j in range(32):
                    r0 = 2048 * (j // 4) + 512 * b + 128 * (j % 4)
                    for g in range(4):
                        st.dma(vtv[:, j][:, 65 * g:65 * g + 64],
                               kv_all[r0:r0 + 128, 256 + 64 * g:256 + 64 * (g + 1)])

            for j in range(32):
                @sq.step(blk, "gpsimd")
                def _(eng, st, j=j, b=b):
                    r0 = 2048 * (j // 4) + 512 * b + 128 * (j % 4)
                    st.dma(ktok[:, 0:256], kv_all[r0:r0 + 128, 0:256])
                    st.dma(ktok[:, 256:320], kv_all[r0:r0 + 128, 512:576])

                @sq.step(blk, "tensor")
                def _(eng, st, j=j):
                    last = None
                    for fc in range(3):
                        last = eng.transpose(ptr[:, 128 * fc:128 * (fc + 1)],
                                             ktok[:, 128 * fc:128 * (fc + 1)],
                                             identb[:, :])
                    return last

                @sq.step(blk, "vector")
                def _(eng, st, j=j):
                    sl = slice(128 * j, 128 * (j + 1))
                    eng.tensor_copy(out=ktg[0:64, 0, sl], in_=ptr[0:64, 0:128])
                    st.chain(eng.tensor_copy(out=ktg[0:64, 1, sl], in_=ptr[64:128, 0:128]))
                    eng.tensor_copy(out=ktg[0:64, 2, sl], in_=ptr[0:64, 128:256])
                    st.chain(eng.tensor_copy(out=ktg[0:64, 3, sl], in_=ptr[64:128, 128:256]))
                    return eng.tensor_copy(out=ki64[0:64, sl], in_=ptr[0:64, 256:384])

            for qt in range(4):
                for h in range(H_I):
                    for kc in range(NCH):
                        @sq.step(blk, "tensor")
                        def _(eng, st, h=h, kc=kc, qt=qt):
                            return eng.matmul(
                                psS[kc % 2][:, :],
                                qih[0:64, h, 128 * qt:128 * (qt + 1)],
                                ki64[0:64, 512 * kc:512 * (kc + 1)],
                                start=True, stop=True)

                        @sq.step(blk, "scalar")
                        def _(eng, st, h=h, kc=kc):
                            return eng.activation(
                                out=scr[:, 512 * kc:512 * (kc + 1)],
                                in_=psS[kc % 2][:, :], func=AF.Sigmoid,
                                bias=bib[:, h:h + 1])

                    @sq.step(blk, "vector")
                    def _(eng, st, h=h, qt=qt):
                        if h == 0:
                            return eng.tensor_scalar(
                                out=isb[:, :], in0=scr[:, :],
                                scalar1=wib[:, 4 * qt + h:4 * qt + h + 1],
                                scalar2=None, op0=ALU.mult)
                        return eng.scalar_tensor_tensor(
                            out=isb[:, :], in0=scr[:, :],
                            scalar=wib[:, 4 * qt + h:4 * qt + h + 1],
                            in1=isb[:, :], op0=ALU.mult, op1=ALU.add)

                @sq.step(blk, "vector")
                def _(eng, st, qt=qt):
                    st.chain(eng.memset(bst[:, 0:2], 0.0))
                    st.chain(eng.memset(bst[:, 1:2], 4.0))
                    for it in range(bisect_iters):
                        st.chain(eng.tensor_tensor(out=bst[:, 2:3], in0=bst[:, 0:1],
                                                   in1=bst[:, 1:2], op=ALU.add))
                        st.chain(eng.tensor_scalar(out=bst[:, 2:3], in0=bst[:, 2:3],
                                                   scalar1=0.5, scalar2=None,
                                                   op0=ALU.mult))
                        st.chain(eng.tensor_scalar(out=scr[:, :], in0=isb[:, :],
                                                   scalar1=bst[:, 2:3], scalar2=0.0,
                                                   op0=ALU.is_gt, op1=ALU.add,
                                                   accum_out=bst[:, 3:4]))
                        st.chain(eng.tensor_scalar(out=bst[:, 4:5], in0=bst[:, 3:4],
                                                   scalar1=float(K_SEL), scalar2=None,
                                                   op0=ALU.is_ge))
                        st.chain(eng.tensor_tensor(out=bst[:, 5:6], in0=bst[:, 2:3],
                                                   in1=bst[:, 0:1], op=ALU.subtract))
                        st.chain(eng.scalar_tensor_tensor(out=bst[:, 0:1], in0=bst[:, 5:6],
                                                          scalar=bst[:, 4:5], in1=bst[:, 0:1],
                                                          op0=ALU.mult, op1=ALU.add))
                        st.chain(eng.tensor_tensor(out=bst[:, 5:6], in0=bst[:, 1:2],
                                                   in1=bst[:, 2:3], op=ALU.subtract))
                        st.chain(eng.scalar_tensor_tensor(out=bst[:, 1:2], in0=bst[:, 5:6],
                                                          scalar=bst[:, 4:5], in1=bst[:, 2:3],
                                                          op0=ALU.mult, op1=ALU.add))
                    return eng.tensor_scalar(out=scr[:, :], in0=isb[:, :],
                                             scalar1=bst[:, 0:1], scalar2=NEG_BIG,
                                             op0=ALU.is_le, op1=ALU.mult)

                for jg in range(8):
                    @sq.step(blk, "tensor")
                    def _(eng, st, jg=jg):
                        last = None
                        for u in range(4):
                            kc = 4 * jg + u
                            last = eng.transpose(ptr[:, 128 * u:128 * (u + 1)],
                                                 scr[:, 128 * kc:128 * (kc + 1)],
                                                 identb[:, :])
                        return last

                    @sq.step(blk, "vector")
                    def _(eng, st, jg=jg, qt=qt):
                        last = None
                        for u in range(4):
                            kc = 4 * jg + u
                            last = eng.tensor_copy(
                                out=mtv[:, kc][:, 128 * qt:128 * (qt + 1)],
                                in_=ptr[:, 128 * u:128 * (u + 1)])
                        return last

            for h in range(H):
                g = h // 4
                for jg in range(8):
                    @sq.step(blk, "tensor")
                    def _(eng, st, h=h, g=g, jg=jg):
                        last = None
                        for u in range(4):
                            kc = 4 * jg + u
                            last = eng.matmul(psS[u][:, :],
                                              ktg[0:64, g, 128 * kc:128 * (kc + 1)],
                                              qrh[0:64, h, :],
                                              start=True, stop=True)
                        return last

                    @sq.step(blk, "vector")
                    def _(eng, st, jg=jg):
                        last = None
                        for u in range(4):
                            kc = 4 * jg + u
                            last = eng.scalar_tensor_tensor(
                                out=sxs[u][:, :], in0=psS[u][:, :],
                                scalar=SM_SCALE, in1=mtv[:, kc],
                                op0=ALU.mult, op1=ALU.add)
                        return last

                    @sq.step(blk, "scalar")
                    def _(eng, st, jg=jg):
                        last = None
                        for u in range(4):
                            last = eng.activation(out=pts[u][:, :], in_=sxs[u][:, :],
                                                  func=AF.Exp)
                        return last

                    @sq.step(blk, "tensor")
                    def _(eng, st, g=g, jg=jg):
                        last = None
                        for u in range(4):
                            kc = 4 * jg + u
                            last = eng.matmul(psO[0:65, :],
                                              vtv[:, kc][:, 65 * g:65 * (g + 1)],
                                              pts[u][:, :],
                                              start=(kc == 0), stop=(kc == 31))
                        return last

                @sq.step(blk, "vector")
                def _(eng, st):
                    return eng.reciprocal(out=drw[0:1, :], in_=psO[64:65, :])

                @sq.step(blk, "tensor")
                def _(eng, st):
                    return eng.matmul(psB[0:64, :], onesr[0:1, 0:64], drw[0:1, :],
                                      start=True, stop=True)

                @sq.step(blk, "vector")
                def _(eng, st):
                    return eng.tensor_copy(out=drb[0:64, :], in_=psB[0:64, :])

                @sq.step(blk, "vector")
                def _(eng, st, h=h):
                    dst = attv[:, h // 2][64 * (h % 2):64 * (h % 2) + 64, :]
                    return eng.tensor_tensor(out=dst, in0=psO[0:64, :],
                                             in1=drb[0:64, :], op=ALU.mult)

            for c in range(NCH):
                @sq.step(blk, "vector")
                def _(eng, st, c=c):
                    return eng.tensor_tensor(out=attv[:, c], in0=attv[:, c],
                                             in1=g1v[:, c], op=ALU.mult)

            for oc in range(NCH):
                @sq.step(blk, "tensor")
                def _(eng, st, oc=oc):
                    last = None
                    for k in range(NCH):
                        last = eng.matmul(psY[:, :],
                                          wov[:, k][:, 128 * oc:128 * (oc + 1)],
                                          attv[:, k], start=(k == 0), stop=(k == NCH - 1))
                    return last

                @sq.step(blk, "sync")
                def _(eng, st, oc=oc, b=b):
                    st.dma(xbu[:, :], xT[128 * oc:128 * (oc + 1), 512 * b:512 * (b + 1)])

                @sq.step(blk, "vector")
                def _(eng, st):
                    return eng.tensor_tensor(out=xao[:, :], in0=psY[:, :],
                                             in1=xbu[:, :], op=ALU.add)

                @sq.step(blk, "sync")
                def _(eng, st, oc=oc, b=b):
                    st.dma(xAT[128 * oc:128 * (oc + 1), 512 * b:512 * (b + 1)],
                           xao[:, :])



def emit_C(nc, blk, sq, io, n_experts=16):
    xnT = io["xnT"]
    xAT = io["xAT"]
    gwT = io["gwT"]
    ebias = io["ebias"]
    wgT = io["wgT"]
    wuT = io["wuT"]
    wdT = io["wdT"]
    sgT = io["sgT"]
    suT = io["suT"]
    sdT = io["sdT"]
    x1T = io["x1T"]
    xn1T = io["xn1T"]
    xn1F = io["xn1F"]
    xmax = io["xmax_in"]

    ctrD = io["ctrD"]

    from contextlib import ExitStack
    with ExitStack() as _ctx:
        x1f = _ctx.enter_context(nc.sbuf_tensor("Cx1f", [128, NCH * TPC], F32))
        gwsb = _ctx.enter_context(nc.sbuf_tensor("Cgwsb", [128, NCH * 16], BF16))
        ebb = _ctx.enter_context(nc.sbuf_tensor("Cebb", [128, 16], F32))
        cmb = _ctx.enter_context(nc.sbuf_tensor("Ccmb", [128, 16 * 16], F32))
        ctst = _ctx.enter_context(nc.sbuf_tensor("Cctst", [16, 128], F32))
        r16 = _ctx.enter_context(nc.sbuf_tensor("Cr16", [128, 16], F32))
        r16b = _ctx.enter_context(nc.sbuf_tensor("Cr16b", [128, 16], F32))
        r16c = _ctx.enter_context(nc.sbuf_tensor("Cr16c", [128, 16], F32))
        r8 = _ctx.enter_context(nc.sbuf_tensor("Cr8", [128, 8], F32))
        rcol = _ctx.enter_context(nc.sbuf_tensor("Crcol", [128, 2], F32))
        onesr = _ctx.enter_context(nc.sbuf_tensor("Conesr", [1, 128], F32))
        onesf = _ctx.enter_context(nc.sbuf_tensor("Conesf", [128, 1], F32))
        identf = _ctx.enter_context(nc.sbuf_tensor("Cidentf", [128, 128], F32))
        epsc = _ctx.enter_context(nc.sbuf_tensor("Cepsc", [1, 1], F32))
        x2 = _ctx.enter_context(nc.sbuf_tensor("Cx2", [128, 512], F32))
        srow = _ctx.enter_context(nc.sbuf_tensor("Csrow", [1, 3 * 512], F32))
        _sx = ExitStack()
        xnsb = _sx.enter_context(nc.sbuf_tensor("Cxnsb", [128, NCH * TPC], BF16))
        wge = _sx.enter_context(nc.sbuf_tensor("Cwge", [128, NCH * 1024], BF16))
        wue = _sx.enter_context(nc.sbuf_tensor("Cwue", [128, NCH * 1024], BF16))
        wde = _sx.enter_context(nc.sbuf_tensor("Cwde", [128, NCH * 1024], BF16))
        crow = _sx.enter_context(nc.sbuf_tensor("Ccrow", [1, TPC], F32))
        cbff = _sx.enter_context(nc.sbuf_tensor("Ccbff", [128, TPC], F32))
        hsl0 = _sx.enter_context(nc.sbuf_tensor("Chsl0", [128, 512], BF16))
        hsl1 = _sx.enter_context(nc.sbuf_tensor("Chsl1", [128, 512], BF16))
        ups0 = _sx.enter_context(nc.sbuf_tensor("Cups0", [128, 512], BF16))
        ups1 = _sx.enter_context(nc.sbuf_tensor("Cups1", [128, 512], BF16))
        hsb = _sx.enter_context(nc.sbuf_tensor("Chsb", [128, NCH * 4 * 512], BF16))
        xnv = xnsb.ap().rearrange("p (c t) -> p c t", c=NCH)
        xf = x1f.ap().rearrange("p (c t) -> p c t", c=NCH)
        wgv = wge.ap().rearrange("p (c t) -> p c t", c=NCH)
        wuv = wue.ap().rearrange("p (c t) -> p c t", c=NCH)
        wdv = wde.ap().rearrange("p (c t) -> p c t", c=NCH)
        gwv = gwsb.ap().rearrange("p (c t) -> p c t", c=NCH)
        hsls = (hsl0, hsl1)
        upss = (ups0, ups1)

        def hchunk(k, tt):
            return hsb[:, (k * 4 + tt) * 512:(k * 4 + tt) * 512 + 512]

        @sq.step(blk, "gpsimd")
        def _(eng, st):
            for c in range(NCH):
                st.dma(xnv[:, c], xnT[128 * c:128 * (c + 1), :])
                st.dma(gwv[:, c], gwT[128 * c:128 * (c + 1), :])
            st.dma(srow[0:1, 0:16], ebias[0:1, :])

        @sq.step(blk, "gpsimd")
        def _(eng, st):
            return make_ident(nc, eng, identf[:, :], st)

        @sq.step(blk, "vector")
        def _(eng, st):
            eng.memset(onesr[:, :], 1.0)
            eng.memset(onesf[:, :], 1.0)
            return eng.memset(epsc[:, :], LN_EPS)

        _pr = ExitStack()
        psR = _pr.enter_context(nc.psum_tensor("CpsR", [128, 16], F32))
        psT = _pr.enter_context(nc.psum_tensor("CpsT", [128, 128], F32))

        @sq.step(blk, "tensor")
        def _(eng, st):
            return eng.matmul(psR[:, :], onesr[:, :], srow[0:1, 0:16],
                              start=True, stop=True)

        @sq.step(blk, "vector")
        def _(eng, st):
            return eng.tensor_copy(out=ebb[:, :], in_=psR[:, :])

        for qt in range(16):
            @sq.step(blk, "tensor")
            def _(eng, st, qt=qt):
                last = None
                for k in range(NCH):
                    last = eng.matmul(psR[:, :],
                                      xnv[:, k][:, 128 * qt:128 * (qt + 1)],
                                      gwv[:, k], start=(k == 0), stop=(k == NCH - 1))
                return last

            @sq.step(blk, "scalar")
            def _(eng, st):
                return eng.activation(out=r16[:, :], in_=psR[:, :], func=AF.Sigmoid)

            @sq.step(blk, "vector")
            def _(eng, st, qt=qt):
                st.chain(eng.tensor_tensor(out=r16b[:, :], in0=r16[:, :],
                                           in1=ebb[:, :], op=ALU.add))
                st.chain(eng.max(out=r8[:, :], in_=r16b[:, :]))
                st.chain(eng.match_replace(out=r16c[:, :], in_to_replace=r8[:, :],
                                           in_values=r16b[:, :], imm_value=-1e30))
                st.chain(eng.tensor_scalar(out=r16c[:, :], in0=r16c[:, :],
                                           scalar1=-1e29, scalar2=None, op0=ALU.is_lt))
                st.chain(eng.tensor_tensor(out=r16c[:, :], in0=r16c[:, :],
                                           in1=r16[:, :], op=ALU.mult))
                st.chain(eng.reduce_sum(out=rcol[:, 0:1], in_=r16c[:, :],
                                        axis=mybir.AxisListType.X))
                st.chain(eng.reciprocal(out=rcol[:, 1:2], in_=rcol[:, 0:1]))
                return eng.tensor_scalar(out=cmb[:, 16 * qt:16 * (qt + 1)],
                                         in0=r16c[:, :], scalar1=rcol[:, 1:2],
                                         scalar2=2.826, op0=ALU.mult, op1=ALU.mult)

            @sq.step(blk, "tensor")
            def _(eng, st, qt=qt):
                return eng.transpose(psT[0:16, :].bitcast(F32),
                                     cmb[:, 16 * qt:16 * (qt + 1)], identf[:, :])

            @sq.step(blk, "vector")
            def _(eng, st, qt=qt):
                return eng.tensor_copy(out=ctst[0:16, :],
                                       in_=psT[0:16, 0:128].bitcast(F32))

            @sq.step(blk, "sync")
            def _(eng, st, qt=qt):
                st.dma(ctrD[0:16, 128 * qt:128 * (qt + 1)], ctst[0:16, :])

        _pr.close()

        _px = ExitStack()
        psGs = tuple(_px.enter_context(nc.psum_tensor(f"CpsG{i}", [128, 512], F32))
                     for i in range(2))
        psUs = tuple(_px.enter_context(nc.psum_tensor(f"CpsU{i}", [128, 512], F32))
                     for i in range(2))
        psDs = tuple(_px.enter_context(nc.psum_tensor(f"CpsD{i}", [128, 512], F32))
                     for i in range(2))
        psBs = tuple(_px.enter_context(nc.psum_tensor(f"CpsB{i}", [128, 512], F32))
                     for i in range(2))

        semT = nc.alloc_semaphore("c_semT")
        semS = nc.alloc_semaphore("c_semS")
        semV = nc.alloc_semaphore("c_semV")
        semWG = nc.alloc_semaphore("c_semWG")
        semWD = nc.alloc_semaphore("c_semWD")
        semCR = nc.alloc_semaphore("c_semCR")
        semXA = nc.alloc_semaphore("c_semXA")

        NE = n_experts + 1
        slocal = [(ic, tt) for ic in range(NCH) for tt in range(NTT)]
        seq_tsv = [(e, ic, tt) for e in range(NE) for (ic, tt) in slocal]
        seq_d = [(e, dc, tt) for e in range(NE) for (dc, tt) in slocal]

        valT, valS, valV = {}, {}, {}
        valWG, valWD, valCR = {}, {}, {}
        vt = vs = vv = vwg = vwd = vcr = 0
        for e in range(NE):
            sh = e == n_experts
            for h in range(2):
                vwg += 16 * 16
                valWG[(e, h)] = vwg
                vwd += 8 * 16
                valWD[(e, h)] = vwd
            if not sh:
                vcr += 16
                valCR[e] = vcr
                for tt in range(NTT):
                    vt += 1
                    valT[(e, "CB", tt)] = vt
                    vv += 1
                    valV[(e, "CBv", tt)] = vv
            for (ic, tt) in slocal:
                vt += 1
                valT[(e, "T", ic, tt)] = vt
                vs += 1
                valS[(e, ic, tt)] = vs
            if sh:
                for (ic, tt) in slocal:
                    vv += 1
                    valV[(e, "M2", ic, tt)] = vv
            else:
                vv += 1
                for i in range(1, 32):
                    vv += 1
                    vv += 1
                    valV[(e, "M2") + slocal[i - 1]] = vv
                vv += 1
                valV[(e, "M2") + slocal[31]] = vv
            for (dc, tt) in slocal:
                vt += 1
                valT[(e, "TD", dc, tt)] = vt
                vv += 1
                valV[(e, "VD", dc, tt)] = vv

        entry = [(sq.sems[k], v) for k, v in sq.cnt.items() if v > 0]

        def _prelude(eng):
            for s_, v_ in entry:
                eng.wait_ge(s_, v_)

        @blk.gpsimd
        def _(eng):
            _prelude(eng)
            for e in range(NE):
                sh = e == n_experts
                for h in range(2):
                    if e > 0:
                        eng.wait_ge(semT, valT[(e - 1, "T", 4 * h + 3, 3)])
                    cs = slice(512 * h, 512 * (h + 1))
                    for k in range(NCH):
                        rs = slice(128 * k, 128 * (k + 1))
                        gsrc = sgT[rs, cs] if sh else wgT[e, rs, cs]
                        usrc = suT[rs, cs] if sh else wuT[e, rs, cs]
                        eng.dma_start(out=wgv[:, k][:, cs], in_=gsrc).then_inc(semWG, 16)
                        eng.dma_start(out=wuv[:, k][:, cs], in_=usrc).then_inc(semWG, 16)
                for h in range(2):
                    if e > 0:
                        eng.wait_ge(semT, valT[(e - 1, "TD", 4 * h + 3, 3)])
                    cs = slice(512 * h, 512 * (h + 1))
                    for k in range(NCH):
                        rs = slice(128 * k, 128 * (k + 1))
                        dsrc = sdT[rs, cs] if sh else wdT[e, rs, cs]
                        eng.dma_start(out=wdv[:, k][:, cs], in_=dsrc).then_inc(semWD, 16)

        @blk.sync
        def _(eng):
            _prelude(eng)
            for c in range(NCH):
                eng.dma_start(out=xf[:, c],
                              in_=xAT[128 * c:128 * (c + 1), :]).then_inc(semXA, 16)
            for e in range(NE):
                sh = e == n_experts
                if not sh:
                    if e > 0:
                        eng.wait_ge(semT, valT[(e - 1, "CB", 3)])
                    eng.dma_start(out=crow[0:1, :],
                                  in_=ctrD[e:e + 1, :]).then_inc(semCR, 16)

        @blk.tensor
        def _(eng):
            _prelude(eng)
            for e in range(NE):
                sh = e == n_experts
                if not sh:
                    eng.wait_ge(semCR, valCR[e])
                    for tt in range(NTT):
                        if tt >= 2:
                            eng.wait_ge(semV, valV[(e, "CBv", tt - 2)])
                        elif e > 0:
                            eng.wait_ge(semV, valV[(e - 1, "CBv", tt + 2)])
                        eng.matmul(psBs[tt % 2][:, :], onesr[:, :],
                                   crow[0:1, 512 * tt:512 * (tt + 1)],
                                   start=True, stop=True).then_inc(semT, 1)
                for (ic, tt) in slocal:
                    if ic % 4 == 0 and tt == 0:
                        eng.wait_ge(semWG, valWG[(e, ic // 4)])
                    g = e * 32 + slocal.index((ic, tt))
                    if g >= 2:
                        ep, icp, ttp = seq_tsv[g - 2]
                        eng.wait_ge(semV, valV[(ep, "M2", icp, ttp)])
                    p = g % 2
                    for k in range(NCH):
                        eng.matmul(psGs[p][:, :],
                                   wgv[:, k][:, 128 * ic:128 * (ic + 1)],
                                   xnv[:, k][:, 512 * tt:512 * (tt + 1)],
                                   start=(k == 0), stop=(k == NCH - 1))
                    last = None
                    for k in range(NCH):
                        last = eng.matmul(psUs[p][:, :],
                                          wuv[:, k][:, 128 * ic:128 * (ic + 1)],
                                          xnv[:, k][:, 512 * tt:512 * (tt + 1)],
                                          start=(k == 0), stop=(k == NCH - 1))
                    last.then_inc(semT, 1)
                for (dc, tt) in slocal:
                    if dc % 4 == 0 and tt == 0:
                        eng.wait_ge(semWD, valWD[(e, dc // 4)])
                    u = e * 32 + slocal.index((dc, tt))
                    w = valV[(e, "M2", 7, tt)]
                    if u >= 2:
                        ep, dcp, ttp = seq_d[u - 2]
                        w = max(w, valV[(ep, "VD", dcp, ttp)])
                    eng.wait_ge(semV, w)
                    q = u % 2
                    last = None
                    for k in range(NCH):
                        last = eng.matmul(psDs[q][:, :],
                                          wdv[:, k][:, 128 * dc:128 * (dc + 1)],
                                          hchunk(k, tt), start=(k == 0),
                                          stop=(k == NCH - 1))
                    last.then_inc(semT, 1)

        @blk.scalar
        def _(eng):
            _prelude(eng)
            for e in range(NE):
                for i, (ic, tt) in enumerate(slocal):
                    g = e * 32 + i
                    p = g % 2
                    eng.wait_ge(semT, valT[(e, "T", ic, tt)])
                    eng.activation(out=hsls[p][:, :], in_=psGs[p][:, :],
                                   func=AF.Silu).then_inc(semS, 1)

        @blk.vector
        def _(eng):
            _prelude(eng)
            for e in range(NE):
                sh = e == n_experts
                if not sh:
                    for tt in range(NTT):
                        eng.wait_ge(semT, valT[(e, "CB", tt)])
                        eng.tensor_copy(
                            out=cbff[:, 512 * tt:512 * (tt + 1)],
                            in_=psBs[tt % 2][:, :]).then_inc(semV, 1)

                    def m1(i, e=e):
                        ic, tt = slocal[i]
                        p = (e * 32 + i) % 2
                        eng.wait_ge(semT, valT[(e, "T", ic, tt)])
                        eng.tensor_tensor(
                            out=upss[p][:, :], in0=psUs[p][:, :],
                            in1=cbff[:, 512 * tt:512 * (tt + 1)],
                            op=ALU.mult).then_inc(semV, 1)

                    def m2(i, e=e):
                        ic, tt = slocal[i]
                        p = (e * 32 + i) % 2
                        eng.wait_ge(semS, valS[(e, ic, tt)])
                        eng.tensor_tensor(
                            out=hchunk(ic, tt), in0=hsls[p][:, :],
                            in1=upss[p][:, :], op=ALU.mult).then_inc(semV, 1)

                    m1(0)
                    for i in range(1, 32):
                        m1(i)
                        m2(i - 1)
                    m2(31)
                else:
                    for i, (ic, tt) in enumerate(slocal):
                        p = (e * 32 + i) % 2
                        eng.wait_ge(semS, valS[(e, ic, tt)])
                        eng.tensor_tensor(
                            out=hchunk(ic, tt), in0=hsls[p][:, :],
                            in1=psUs[p][:, :], op=ALU.mult).then_inc(semV, 1)
                for i, (dc, tt) in enumerate(slocal):
                    u = e * 32 + i
                    q = u % 2
                    if u == 0:
                        eng.wait_ge(semXA, 128)
                    eng.wait_ge(semT, valT[(e, "TD", dc, tt)])
                    dst = xf[:, dc][:, 512 * tt:512 * (tt + 1)]
                    eng.tensor_tensor(out=dst, in0=dst, in1=psDs[q][:, :],
                                      op=ALU.add).then_inc(semV, 1)

        for key, sem, v in (("cT", semT, vt), ("cS", semS, vs), ("cV", semV, vv),
                            ("cWG", semWG, vwg), ("cWD", semWD, vwd),
                            ("cCR", semCR, vcr), ("cXA", semXA, 128)):
            sq.sems[key] = sem
            sq.cnt[key] = v
        _px.close()
        _sx.close()

        _pl = ExitStack()
        psG = _pl.enter_context(nc.psum_tensor("CpsLg", [128, 512], F32))
        psU = _pl.enter_context(nc.psum_tensor("CpsLu", [128, 512], F32))
        psT2 = _pl.enter_context(nc.psum_tensor("CpsLt", [128, 128], F32))
        prow = _pl.enter_context(nc.psum_tensor("CpsLr", [1, 1024], F32))
        hslb = _pl.enter_context(nc.sbuf_tensor("Chslb", [128, 512], BF16))
        mb = _pl.enter_context(nc.sbuf_tensor("Cmb", [128, 512], F32))
        rb = _pl.enter_context(nc.sbuf_tensor("Crb", [128, 512], F32))
        mxc = _pl.enter_context(nc.sbuf_tensor("Cmxc", [128, 2], F32))
        mrow = _pl.enter_context(nc.sbuf_tensor("Cmrow", [1, 128], F32))

        @sq.step(blk, "vector")
        def _(eng, st):
            return eng.memset(mxc[:, 0:1], 0.0)

        for tt in range(NTT):
            def x1v_(c, tt=tt):
                return xf[:, c][:, 512 * tt:512 * (tt + 1)]

            @sq.step(blk, "sync")
            def _(eng, st, tt=tt):
                for c in range(NCH):
                    st.dma(x1T[128 * c:128 * (c + 1), 512 * tt:512 * (tt + 1)],
                           x1v_(c))

            for c in range(NCH):
                @sq.step(blk, "vector")
                def _(eng, st, c=c):
                    return eng.tensor_tensor(out=x2[:, :], in0=x1v_(c),
                                             in1=x1v_(c), op=ALU.mult)

                @sq.step(blk, "tensor")
                def _(eng, st, c=c):
                    eng.matmul(prow[:, 0:512], onesf[:, :], x1v_(c),
                               start=(c == 0), stop=(c == NCH - 1))
                    return eng.matmul(prow[:, 512:1024], onesf[:, :], x2[:, :],
                                      start=(c == 0), stop=(c == NCH - 1))

            @sq.step(blk, "vector")
            def _(eng, st):
                eng.tensor_scalar(out=srow[:, 0:512], in0=prow[:, 0:512],
                                  scalar1=1.0 / C, scalar2=None, op0=ALU.mult)
                st.chain(eng.tensor_scalar(out=srow[:, 512:1024], in0=prow[:, 512:1024],
                                           scalar1=1.0 / C, scalar2=None, op0=ALU.mult))
                st.chain(eng.tensor_tensor(out=srow[:, 1024:1536], in0=srow[:, 0:512],
                                           in1=srow[:, 0:512], op=ALU.mult))
                return eng.tensor_tensor(out=srow[:, 512:1024], in0=srow[:, 512:1024],
                                         in1=srow[:, 1024:1536], op=ALU.subtract)

            @sq.step(blk, "scalar")
            def _(eng, st):
                return eng.activation(out=srow[:, 1024:1536], in_=srow[:, 512:1024],
                                      func=AF.Sqrt, bias=epsc[0:1, 0:1])

            @sq.step(blk, "vector")
            def _(eng, st):
                return eng.reciprocal(out=srow[:, 512:1024], in_=srow[:, 1024:1536])

            @sq.step(blk, "tensor")
            def _(eng, st):
                eng.matmul(psG[:, :], onesr[:, :], srow[:, 0:512],
                           start=True, stop=True)
                return eng.matmul(psU[:, :], onesr[:, :], srow[:, 512:1024],
                                  start=True, stop=True)

            @sq.step(blk, "vector")
            def _(eng, st):
                eng.tensor_copy(out=mb[:, :], in_=psG[:, :])
                return eng.tensor_copy(out=rb[:, :], in_=psU[:, :])

            for c in range(NCH):
                @sq.step(blk, "vector")
                def _(eng, st, c=c):
                    st.chain(eng.tensor_tensor(out=x2[:, :], in0=x1v_(c),
                                               in1=mb[:, :], op=ALU.subtract))
                    st.chain(eng.tensor_tensor(out=x2[:, :], in0=x2[:, :],
                                               in1=rb[:, :], op=ALU.mult))
                    st.chain(eng.tensor_copy(out=hslb[:, :], in_=x2[:, :]))
                    st.chain(eng.reduce_max(out=mxc[:, 1:2], in_=x2[:, :],
                                            axis=mybir.AxisListType.X,
                                            apply_absolute_value=True))
                    return eng.tensor_tensor(out=mxc[:, 0:1], in0=mxc[:, 0:1],
                                             in1=mxc[:, 1:2], op=ALU.max)

                @sq.step(blk, "sync")
                def _(eng, st, c=c, tt=tt):
                    st.dma(xn1T[128 * c:128 * (c + 1), 512 * tt:512 * (tt + 1)],
                           hslb[:, :])
                    st.dma(xn1F[128 * c:128 * (c + 1), 512 * tt:512 * (tt + 1)],
                           x2[:, :])

        @sq.step(blk, "tensor")
        def _(eng, st):
            return eng.transpose(psT2[0:1, 0:128].bitcast(F32), mxc[:, 0:1],
                                 identf[:, :])

        @sq.step(blk, "vector")
        def _(eng, st):
            st.chain(eng.tensor_copy(out=mrow[0:1, :], in_=psT2[0:1, 0:128].bitcast(F32)))
            return eng.reduce_max(out=srow[0:1, 0:1], in_=mrow[0:1, :],
                                  axis=mybir.AxisListType.X)

        @sq.step(blk, "sync")
        def _(eng, st):
            st.dma(xmax[0:1, 0:1], srow[0:1, 0:1])
        _pl.close()



def emit_D(nc, blk, sq, io):
    xn1T = io["xn1T"]
    x1T = io["x1T"]
    wkv = io["wkv"]
    bkv = io["bkv"]
    wq1 = io["wq1"]
    bq1 = io["bq1"]
    wp1 = io["wp1"]
    bp1 = io["bp1"]
    xBT = io["xBT"]

    from contextlib import ExitStack
    with ExitStack() as _ctx:
        xnsb = _ctx.enter_context(nc.sbuf_tensor("Dxnsb", [128, NCH * TPC], BF16))
        wkvb = _ctx.enter_context(nc.sbuf_tensor("Dwkvb", [128, NCH * 2048], BF16))
        wqb = _ctx.enter_context(nc.sbuf_tensor("Dwqb", [128, NCH * 1024], BF16))
        wpb = _ctx.enter_context(nc.sbuf_tensor("Dwpb", [128, NCH * 1024], BF16))
        bkvb = _ctx.enter_context(nc.sbuf_tensor("Dbkvb", [128, 2048], F32))
        bqb = _ctx.enter_context(nc.sbuf_tensor("Dbqb", [128, 1024], F32))
        bpc = _ctx.enter_context(nc.sbuf_tensor("Dbpc", [128, NCH], F32))
        brow = _ctx.enter_context(nc.sbuf_tensor("Dbrow", [1, 2048], F32))
        onesr = _ctx.enter_context(nc.sbuf_tensor("Donesr", [1, 128], F32))
        identb = _ctx.enter_context(nc.sbuf_tensor("Didentb", [128, 128], BF16))
        qtok = _ctx.enter_context(nc.sbuf_tensor("Dqtok", [128, 4 * 1024], BF16))
        ktok = _ctx.enter_context(nc.sbuf_tensor("Dktok2", [128, 4 * 1024], BF16))
        vtok = _ctx.enter_context(nc.sbuf_tensor("Dvtok2", [128, 4 * 1024], BF16))
        atile = _ctx.enter_context(nc.sbuf_tensor("Datile", [128, 4 * 16 * 4], F32))
        amx = _ctx.enter_context(nc.sbuf_tensor("Damx", [128, 4 * 16], F32))
        tmp1 = _ctx.enter_context(nc.sbuf_tensor("Dtmp1", [128, 1024], F32))
        avt = _ctx.enter_context(nc.sbuf_tensor("Davt", [128, 4 * 1024], BF16))
        avT = _ctx.enter_context(nc.sbuf_tensor("DavT", [128, NCH * TPC], BF16))
        xbu = _ctx.enter_context(nc.sbuf_tensor("Dxbu", [128, 512], F32))
        xao = _ctx.enter_context(nc.sbuf_tensor("Dxao", [128, 512], F32))
        psA = _ctx.enter_context(nc.psum_tensor("DpsA", [128, 512], F32))
        psB2 = _ctx.enter_context(nc.psum_tensor("DpsB2", [128, 512], F32))
        psC2 = _ctx.enter_context(nc.psum_tensor("DpsC2", [128, 512], F32))
        psD2 = _ctx.enter_context(nc.psum_tensor("DpsD2", [128, 512], F32))
        ptr = _ctx.enter_context(nc.psum_tensor("Dptr", [128, 512], BF16))
        xnv = xnsb.ap().rearrange("p (c t) -> p c t", c=NCH)
        wkvv = wkvb.ap().rearrange("p (c t) -> p c t", c=NCH)
        wqv = wqb.ap().rearrange("p (c t) -> p c t", c=NCH)
        wpv = wpb.ap().rearrange("p (c t) -> p c t", c=NCH)
        qtv = qtok.ap().rearrange("p (b f) -> p b f", b=4)
        ktv = ktok.ap().rearrange("p (b f) -> p b f", b=4)
        vtv = vtok.ap().rearrange("p (b f) -> p b f", b=4)
        avv = avt.ap().rearrange("p (b f) -> p b f", b=4)
        avTv = avT.ap().rearrange("p (c t) -> p c t", c=NCH)
        av_ = atile.ap().rearrange("p (s h t) -> p s h t", s=4, h=16)
        amv = amx.ap().rearrange("p (s h) -> p s h", s=4)

        @sq.step(blk, "gpsimd")
        def _(eng, st):
            for c in range(NCH):
                st.dma(xnv[:, c], xn1T[128 * c:128 * (c + 1), :])
                st.dma(wkvv[:, c], wkv[128 * c:128 * (c + 1), :])
                st.dma(wqv[:, c], wq1[128 * c:128 * (c + 1), :])
                st.dma(wpv[:, c], wp1[128 * c:128 * (c + 1), :])
            for c in range(NCH):
                st.dma(bpc[:, c:c + 1], bp1[0:1, 128 * c:128 * (c + 1)].rearrange("o p -> p o"))

        @sq.step(blk, "gpsimd")
        def _(eng, st):
            return make_ident(nc, eng, identb[:, :], st)

        @sq.step(blk, "sync")
        def _(eng, st):
            st.dma(brow[0:1, :], bkv[0:1, :])

        @sq.step(blk, "vector")
        def _(eng, st):
            return eng.memset(onesr[:, :], 1.0)

        for i in range(4):
            @sq.step(blk, "tensor")
            def _(eng, st, i=i):
                return eng.matmul(psA[:, :], onesr[:, :],
                                  brow[0:1, 512 * i:512 * (i + 1)],
                                  start=True, stop=True)

            @sq.step(blk, "vector")
            def _(eng, st, i=i):
                return eng.tensor_copy(out=bkvb[:, 512 * i:512 * (i + 1)], in_=psA[:, :])

        @sq.step(blk, "sync")
        def _(eng, st):
            st.dma(brow[0:1, 0:1024], bq1[0:1, :])

        for i in range(2):
            @sq.step(blk, "tensor")
            def _(eng, st, i=i):
                return eng.matmul(psA[:, :], onesr[:, :],
                                  brow[0:1, 512 * i:512 * (i + 1)],
                                  start=True, stop=True)

            @sq.step(blk, "vector")
            def _(eng, st, i=i):
                return eng.tensor_copy(out=bqb[:, 512 * i:512 * (i + 1)], in_=psA[:, :])

        for jb in range(4):
            for b in range(4):
                tsl = slice(512 * b + 128 * jb, 512 * b + 128 * (jb + 1))
                for fg in range(4):
                    @sq.step(blk, "tensor")
                    def _(eng, st, fg=fg, tsl=tsl):
                        last = None
                        for k in range(NCH):
                            last = eng.matmul(psA[:, :],
                                              xnv[:, k][:, tsl],
                                              wkvv[:, k][:, 512 * fg:512 * (fg + 1)],
                                              start=(k == 0), stop=(k == NCH - 1))
                        return last

                    @sq.step(blk, "vector")
                    def _(eng, st, fg=fg, b=b):
                        dst = ktv[:, b][:, 512 * fg:512 * (fg + 1)] if fg < 2 else \
                            vtv[:, b][:, 512 * (fg - 2):512 * (fg - 1)]
                        return eng.tensor_tensor(out=dst, in0=psA[:, :],
                                                 in1=bkvb[:, 512 * fg:512 * (fg + 1)],
                                                 op=ALU.add)

                for fg in range(2):
                    @sq.step(blk, "tensor")
                    def _(eng, st, fg=fg, tsl=tsl):
                        last = None
                        for k in range(NCH):
                            last = eng.matmul(psA[:, :],
                                              xnv[:, k][:, tsl],
                                              wqv[:, k][:, 512 * fg:512 * (fg + 1)],
                                              start=(k == 0), stop=(k == NCH - 1))
                        return last

                    @sq.step(blk, "vector")
                    def _(eng, st, fg=fg, b=b):
                        return eng.tensor_tensor(out=qtv[:, b][:, 512 * fg:512 * (fg + 1)],
                                                 in0=psA[:, :],
                                                 in1=bqb[:, 512 * fg:512 * (fg + 1)],
                                                 op=ALU.add)

            for s in range(4):
                for t in range(4):
                    @sq.step(blk, "vector")
                    def _(eng, st, s=s, t=t):
                        st.chain(eng.tensor_tensor(out=tmp1[:, :], in0=qtv[:, s],
                                                   in1=ktv[:, t], op=ALU.mult))
                        return eng.reduce_sum(
                            out=av_[:, s, :, t].unsqueeze(-1),
                            in_=tmp1.ap().rearrange("p (h d) -> p h d", h=16),
                            axis=mybir.AxisListType.X)

            @sq.step(blk, "vector")
            def _(eng, st):
                return eng.reduce_max(out=amx[:, :], in_=atile[:, :].rearrange(
                    "p (s h t) -> p (s h) t", s=4, h=16), axis=mybir.AxisListType.X)

            @sq.step(blk, "vector")
            def _(eng, st):
                return eng.tensor_tensor(
                    out=atile[:, :].rearrange("p (s h t) -> p (s h) t", s=4, h=16),
                    in0=atile[:, :].rearrange("p (s h t) -> p (s h) t", s=4, h=16),
                    in1=amx[:, :].unsqueeze(-1).to_broadcast([128, 64, 4]),
                    op=ALU.subtract)

            @sq.step(blk, "scalar")
            def _(eng, st):
                return eng.activation(out=atile[:, :], in_=atile[:, :], func=AF.Exp,
                                      scale=1.0 / 8.0)

            @sq.step(blk, "vector")
            def _(eng, st):
                return eng.reduce_sum(out=amx[:, :], in_=atile[:, :].rearrange(
                    "p (s h t) -> p (s h) t", s=4, h=16), axis=mybir.AxisListType.X)

            @sq.step(blk, "vector")
            def _(eng, st):
                st.chain(eng.reciprocal(out=amx[:, :], in_=amx[:, :]))
                return eng.tensor_tensor(
                    out=atile[:, :].rearrange("p (s h t) -> p (s h) t", s=4, h=16),
                    in0=atile[:, :].rearrange("p (s h t) -> p (s h) t", s=4, h=16),
                    in1=amx[:, :].unsqueeze(-1).to_broadcast([128, 64, 4]),
                    op=ALU.mult)

            for s in range(4):
                for t in range(4):
                    @sq.step(blk, "vector")
                    def _(eng, st, s=s, t=t):
                        pview = av_[:, s, :, t].unsqueeze(-1).to_broadcast([128, 16, 64])
                        if t == 0:
                            return eng.tensor_tensor(
                                out=avv[:, s].rearrange("p (h d) -> p h d", h=16),
                                in0=vtv[:, t].rearrange("p (h d) -> p h d", h=16),
                                in1=pview, op=ALU.mult)
                        st.chain(eng.tensor_tensor(
                            out=tmp1.ap().rearrange("p (h d) -> p h d", h=16),
                            in0=vtv[:, t].rearrange("p (h d) -> p h d", h=16),
                            in1=pview, op=ALU.mult))
                        return eng.tensor_tensor(out=avv[:, s], in0=avv[:, s],
                                                 in1=tmp1[:, :], op=ALU.add)

            for s in range(4):
                for fg in range(2):
                    @sq.step(blk, "tensor")
                    def _(eng, st, s=s, fg=fg):
                        last = None
                        for u in range(4):
                            fc = 4 * fg + u
                            last = eng.transpose(ptr[:, 128 * u:128 * (u + 1)],
                                                 avv[:, s][:, 128 * fc:128 * (fc + 1)],
                                                 identb[:, :])
                        return last

                    @sq.step(blk, "vector")
                    def _(eng, st, s=s, fg=fg, jb=jb):
                        last = None
                        for u in range(4):
                            fc = 4 * fg + u
                            last = eng.tensor_copy(
                                out=avTv[:, fc][:, 512 * s + 128 * jb:512 * s + 128 * (jb + 1)],
                                in_=ptr[:, 128 * u:128 * (u + 1)])
                        return last

        for oc in range(NCH):
            for tt in range(NTT):
                @sq.step(blk, "tensor")
                def _(eng, st, oc=oc, tt=tt):
                    last = None
                    for k in range(NCH):
                        last = eng.matmul(psD2[:, :],
                                          wpv[:, k][:, 128 * oc:128 * (oc + 1)],
                                          avTv[:, k][:, 512 * tt:512 * (tt + 1)],
                                          start=(k == 0), stop=(k == NCH - 1))
                    return last

                @sq.step(blk, "sync")
                def _(eng, st, oc=oc, tt=tt):
                    st.dma(xbu[:, :], x1T[128 * oc:128 * (oc + 1), 512 * tt:512 * (tt + 1)])

                @sq.step(blk, "vector")
                def _(eng, st, oc=oc):
                    st.chain(eng.tensor_scalar(out=xao[:, :], in0=psD2[:, :],
                                               scalar1=bpc[:, oc:oc + 1], scalar2=None,
                                               op0=ALU.add))
                    return eng.tensor_tensor(out=xao[:, :], in0=xao[:, :],
                                             in1=xbu[:, :], op=ALU.add)

                @sq.step(blk, "sync")
                def _(eng, st, oc=oc, tt=tt):
                    st.dma(xBT[128 * oc:128 * (oc + 1), 512 * tt:512 * (tt + 1)],
                           xao[:, :])



def emit_round_inplace(eng, st, r, ti, tf, td):
    st.chain(eng.tensor_copy(out=ti, in_=r))
    st.chain(eng.tensor_copy(out=tf, in_=ti))
    st.chain(eng.tensor_tensor(out=td, in0=r, in1=tf, op=ALU.subtract))
    st.chain(eng.tensor_scalar(out=r, in0=td, scalar1=0.5, scalar2=None, op0=ALU.is_gt))
    st.chain(eng.tensor_tensor(out=tf, in0=tf, in1=r, op=ALU.add))
    st.chain(eng.tensor_scalar(out=r, in0=td, scalar1=-0.5, scalar2=None, op0=ALU.is_lt))
    return eng.tensor_tensor(out=r, in0=tf, in1=r, op=ALU.subtract)


def emit_qdq(eng, st, out_ap, x_ap, inv_col, s_col, r, ti, tf, td):
    st.chain(eng.tensor_scalar(out=r, in0=x_ap, scalar1=inv_col, scalar2=None,
                               op0=ALU.mult))
    st.chain(emit_round_inplace(eng, st, r, ti, tf, td))
    return eng.tensor_scalar(out=out_ap, in0=r, scalar1=s_col, scalar2=None,
                             op0=ALU.mult)


def emit_scale_from_max(sq, blk, nc, onesr, m_in_row, n_in, qmax, srow, scol, icol, ps):
    @sq.step(blk, "vector")
    def _(eng, st):
        st.chain(eng.reduce_max(out=srow[0:1, 0:1], in_=m_in_row[0:1, 0:n_in],
                                axis=mybir.AxisListType.X))
        st.chain(eng.tensor_scalar(out=srow[0:1, 1:2], in0=srow[0:1, 0:1],
                                   scalar1=1.0 / qmax, scalar2=1e-12,
                                   op0=ALU.mult, op1=ALU.max))
        return eng.reciprocal(out=srow[0:1, 2:3], in_=srow[0:1, 1:2])

    @sq.step(blk, "tensor")
    def _(eng, st):
        eng.matmul(ps[:, 0:1], onesr[:, :], srow[0:1, 1:2], start=True, stop=True)
        return eng.matmul(ps[:, 1:2], onesr[:, :], srow[0:1, 2:3], start=True, stop=True)

    @sq.step(blk, "vector")
    def _(eng, st):
        st.chain(eng.tensor_copy(out=scol, in_=ps[:, 0:1]))
        return eng.tensor_copy(out=icol, in_=ps[:, 1:2])


def emit_colmax_to_scalar(sq, blk, nc, mcol, identf, psT, mrow, out11):
    @sq.step(blk, "tensor")
    def _(eng, st):
        return eng.transpose(psT[0:1, 0:128], mcol, identf[:, :])

    @sq.step(blk, "vector")
    def _(eng, st):
        st.chain(eng.tensor_copy(out=mrow[0:1, :], in_=psT[0:1, 0:128]))
        return eng.reduce_max(out=out11, in_=mrow[0:1, :], axis=mybir.AxisListType.X)


def emit_dequant_chunk(eng, st, w_ap, wq_ap, nblk, sw, swr, r, ti, tf, td):
    wv = w_ap.rearrange("p (b i) -> p b i", i=32)
    st.chain(eng.reduce_max(out=sw, in_=wv, axis=mybir.AxisListType.X,
                            apply_absolute_value=True))
    st.chain(eng.tensor_scalar(out=sw, in0=sw, scalar1=1.0 / 7.0, scalar2=1e-12,
                               op0=ALU.mult, op1=ALU.max))
    st.chain(eng.reciprocal(out=swr, in_=sw))
    rv = r.rearrange("p (b i) -> p b i", i=32)
    st.chain(eng.tensor_tensor(out=rv, in0=wv,
                               in1=swr.unsqueeze(-1).to_broadcast(list(sw.shape) + [32]),
                               op=ALU.mult))
    st.chain(emit_round_inplace(eng, st, r, ti, tf, td))
    return eng.tensor_tensor(out=wq_ap.rearrange("p (b i) -> p b i", i=32), in0=rv,
                             in1=sw.unsqueeze(-1).to_broadcast(list(sw.shape) + [32]),
                             op=ALU.mult)



def emit_E1(nc, blk, sq, io):
    xn1F = io["xn1F"]
    wgn = io["wgn"]
    wun = io["wun"]
    upD = io["upD"]
    gateD = io["gateD"]

    NIC = I_L1 // 128

    from contextlib import ExitStack
    with ExitStack() as _ctx:
        xqsb = _ctx.enter_context(nc.sbuf_tensor("Exqsb", [128, NCH * TPC], BF16))
        wgT = _ctx.enter_context(nc.sbuf_tensor("EwgT", [128, NCH * I_L1], BF16))
        wuT = _ctx.enter_context(nc.sbuf_tensor("EwuT", [128, NCH * I_L1], BF16))
        wch = _ctx.enter_context(nc.sbuf_tensor("Ewch", [128, 1024], F32))
        wqc = _ctx.enter_context(nc.sbuf_tensor("Ewqc", [128, 1024], BF16))
        sw = _ctx.enter_context(nc.sbuf_tensor("Esw", [128, 32], F32))
        swr = _ctx.enter_context(nc.sbuf_tensor("Eswr", [128, 32], F32))
        rr = _ctx.enter_context(nc.sbuf_tensor("Err", [128, 2048], F32))
        ti = _ctx.enter_context(nc.sbuf_tensor("Eti", [128, 2048], I32))
        tf = _ctx.enter_context(nc.sbuf_tensor("Etf", [128, 2048], F32))
        td = _ctx.enter_context(nc.sbuf_tensor("Etd", [128, 2048], F32))
        xch = _ctx.enter_context(nc.sbuf_tensor("Exch", [128, 2048], F32))
        urow = _ctx.enter_context(nc.sbuf_tensor("Eurow", [128, 512], BF16))
        onesr = _ctx.enter_context(nc.sbuf_tensor("Eonesr", [1, 128], F32))
        identf = _ctx.enter_context(nc.sbuf_tensor("Eidentf", [128, 128], F32))
        identb = _ctx.enter_context(nc.sbuf_tensor("Eidentb", [128, 128], BF16))
        srow = _ctx.enter_context(nc.sbuf_tensor("Esrow", [1, 8], F32))
        mrow = _ctx.enter_context(nc.sbuf_tensor("Emrow", [1, 128], F32))
        scol = _ctx.enter_context(nc.sbuf_tensor("Escol", [128, 1], F32))
        icol = _ctx.enter_context(nc.sbuf_tensor("Eicol", [128, 1], F32))
        mxu = _ctx.enter_context(nc.sbuf_tensor("Emxu", [128, 3], F32))
        mxg = _ctx.enter_context(nc.sbuf_tensor("Emxg", [128, 3], F32))
        psA = _ctx.enter_context(nc.psum_tensor("EpsA", [128, 512], F32))
        psW = _ctx.enter_context(nc.psum_tensor("EpsW", [128, 512], BF16))
        psT = _ctx.enter_context(nc.psum_tensor("EpsT", [128, 128], F32))
        xqv = xqsb.ap().rearrange("p (c t) -> p c t", c=NCH)
        wgv = wgT.ap().rearrange("p (c t) -> p c t", c=NCH)
        wuv = wuT.ap().rearrange("p (c t) -> p c t", c=NCH)

        @sq.step(blk, "gpsimd")
        def _(eng, st):
            st.dma(srow[0:1, :], io["xmax_g"].rearrange("r o -> o r")[0:1, :])

        @sq.step(blk, "gpsimd")
        def _(eng, st):
            st.chain(make_ident(nc, eng, identf[:, :], st))
            return make_ident(nc, eng, identb[:, :], st)

        @sq.step(blk, "vector")
        def _(eng, st):
            eng.memset(onesr[:, :], 1.0)
            eng.memset(mxu[:, 0:1], 0.0)
            return eng.memset(mxg[:, 0:1], 0.0)

        emit_scale_from_max(sq, blk, nc, onesr, srow, 8, 32767.0, srow, scol[:, 0:1], icol[:, 0:1], psT)

        for c in range(NCH):
            @sq.step(blk, "sync")
            def _(eng, st, c=c):
                st.dma(xch[:, :], xn1F[128 * c:128 * (c + 1), :])

            @sq.step(blk, "vector")
            def _(eng, st, c=c):
                return emit_qdq(eng, st, xqv[:, c], xch[:, :], icol[:, 0:1],
                                scol[:, 0:1], rr[:, :], ti[:, :], tf[:, :], td[:, :])

        for which, wn, wTv in (("g", wgn, wgv), ("u", wun, wuv)):
            for oc in range(NIC):
                @sq.step(blk, "sync")
                def _(eng, st, wn=wn, oc=oc):
                    st.dma(wch[:, :], wn[128 * oc:128 * (oc + 1), :])

                @sq.step(blk, "vector")
                def _(eng, st):
                    return emit_dequant_chunk(eng, st, wch[:, :], wqc[:, :], 32,
                                              sw[:, :], swr[:, :], rr[:, 0:1024],
                                              ti[:, 0:1024], tf[:, 0:1024], td[:, 0:1024])

                for hc in range(NCH):
                    @sq.step(blk, "tensor")
                    def _(eng, st, hc=hc):
                        return eng.transpose(psW[:, 0:128],
                                             wqc[:, 128 * hc:128 * (hc + 1)],
                                             identb[:, :])

                    @sq.step(blk, "vector")
                    def _(eng, st, hc=hc, oc=oc, wTv=wTv):
                        return eng.tensor_copy(out=wTv[:, hc][:, 128 * oc:128 * (oc + 1)],
                                               in_=psW[:, 0:128])

        for which, wTv, outD, mx in (("g", wgv, gateD, mxg), ("u", wuv, upD, mxu)):
            for ic in range(NIC):
                for tt in range(NTT):
                    @sq.step(blk, "tensor")
                    def _(eng, st, ic=ic, tt=tt, wTv=wTv):
                        last = None
                        for k in range(NCH):
                            last = eng.matmul(psA[:, :],
                                              wTv[:, k][:, 128 * ic:128 * (ic + 1)],
                                              xqv[:, k][:, 512 * tt:512 * (tt + 1)],
                                              start=(k == 0), stop=(k == NCH - 1))
                        return last

                    @sq.step(blk, "vector")
                    def _(eng, st, mx=mx):
                        st.chain(eng.tensor_copy(out=urow[:, :], in_=psA[:, :]))
                        st.chain(eng.reduce_max(out=mx[:, 1:2], in_=psA[:, :],
                                                axis=mybir.AxisListType.X,
                                                apply_absolute_value=True))
                        return eng.tensor_tensor(out=mx[:, 0:1], in0=mx[:, 0:1],
                                                 in1=mx[:, 1:2], op=ALU.max)

                    @sq.step(blk, "sync")
                    def _(eng, st, ic=ic, tt=tt, outD=outD):
                        st.dma(outD[128 * ic:128 * (ic + 1), 512 * tt:512 * (tt + 1)],
                               urow[:, :])

        emit_colmax_to_scalar(sq, blk, nc, mxg[:, 0:1], identf, psT, mrow, srow[0:1, 3:4])
        emit_colmax_to_scalar(sq, blk, nc, mxu[:, 0:1], identf, psT, mrow, srow[0:1, 4:5])

        @sq.step(blk, "sync")
        def _(eng, st):
            st.dma(io["m1_in"][0:1, 0:2], srow[0:1, 3:5])



def emit_E2(nc, blk, sq, io):
    upD = io["upD"]
    gateD = io["gateD"]
    upqD = io["upqD"]
    g2D = io["g2D"]

    NIC = I_L1 // 128

    from contextlib import ExitStack
    with ExitStack() as _ctx:
        ch = _ctx.enter_context(nc.sbuf_tensor("Fch", [128, TPC], F32))
        ch2 = _ctx.enter_context(nc.sbuf_tensor("Fch2", [128, TPC], F32))
        chb = _ctx.enter_context(nc.sbuf_tensor("Fchb", [128, TPC], BF16))
        rr = _ctx.enter_context(nc.sbuf_tensor("Frr", [128, TPC], F32))
        ti = _ctx.enter_context(nc.sbuf_tensor("Fti", [128, TPC], I32))
        tf = _ctx.enter_context(nc.sbuf_tensor("Ftf", [128, TPC], F32))
        td = _ctx.enter_context(nc.sbuf_tensor("Ftd", [128, TPC], F32))
        onesr = _ctx.enter_context(nc.sbuf_tensor("Fonesr", [1, 128], F32))
        identf = _ctx.enter_context(nc.sbuf_tensor("Fidentf", [128, 128], F32))
        srow = _ctx.enter_context(nc.sbuf_tensor("Fsrow", [1, 8], F32))
        mrow = _ctx.enter_context(nc.sbuf_tensor("Fmrow", [1, 128], F32))
        sgc = _ctx.enter_context(nc.sbuf_tensor("Fsgc", [128, 2], F32))
        suc = _ctx.enter_context(nc.sbuf_tensor("Fsuc", [128, 2], F32))
        mxc = _ctx.enter_context(nc.sbuf_tensor("Fmxc", [128, 2], F32))
        psT = _ctx.enter_context(nc.psum_tensor("FpsT", [128, 128], F32))

        @sq.step(blk, "gpsimd")
        def _(eng, st):
            st.dma(srow[0:1, 0:8], io["m1_g"].rearrange("c two -> two c")[0:1, :],
                   allow_slow_non_contiguous=True)

        @sq.step(blk, "gpsimd")
        def _(eng, st):
            return make_ident(nc, eng, identf[:, :], st)

        @sq.step(blk, "vector")
        def _(eng, st):
            eng.memset(onesr[:, :], 1.0)
            return eng.memset(mxc[:, 0:1], 0.0)

        emit_scale_from_max(sq, blk, nc, onesr, srow, 8, 32767.0, srow,
                            sgc[:, 0:1], sgc[:, 1:2], psT)

        @sq.step(blk, "gpsimd")
        def _(eng, st):
            st.dma(srow[0:1, 0:8], io["m1_g"].rearrange("c two -> two c")[1:2, :],
                   allow_slow_non_contiguous=True)

        emit_scale_from_max(sq, blk, nc, onesr, srow, 8, 32767.0, srow,
                            suc[:, 0:1], suc[:, 1:2], psT)

        for ic in range(NIC):
            @sq.step(blk, "sync")
            def _(eng, st, ic=ic):
                st.dma(chb[:, :], gateD[128 * ic:128 * (ic + 1), :])

            @sq.step(blk, "vector")
            def _(eng, st):
                st.chain(eng.tensor_copy(out=ch[:, :], in_=chb[:, :]))
                return emit_qdq(eng, st, ch[:, :], ch[:, :], sgc[:, 1:2],
                                sgc[:, 0:1], rr[:, :], ti[:, :], tf[:, :], td[:, :])

            @sq.step(blk, "scalar")
            def _(eng, st):
                return eng.activation(out=ch2[:, :], in_=ch[:, :], func=AF.Sigmoid)

            @sq.step(blk, "vector")
            def _(eng, st):
                st.chain(eng.tensor_scalar(out=rr[:, :], in0=ch2[:, :],
                                           scalar1=65536.0, scalar2=None, op0=ALU.mult))
                st.chain(emit_round_inplace(eng, st, rr[:, :], ti[:, :], tf[:, :], td[:, :]))
                st.chain(eng.tensor_scalar(out=rr[:, :], in0=rr[:, :], scalar1=65535.0,
                                           scalar2=1.0 / 65536.0, op0=ALU.min, op1=ALU.mult))
                st.chain(eng.tensor_tensor(out=ch[:, :], in0=ch[:, :], in1=rr[:, :],
                                           op=ALU.mult))
                st.chain(eng.tensor_copy(out=chb[:, :], in_=ch[:, :]))
                st.chain(eng.reduce_max(out=mxc[:, 1:2], in_=ch[:, :],
                                        axis=mybir.AxisListType.X,
                                        apply_absolute_value=True))
                return eng.tensor_tensor(out=mxc[:, 0:1], in0=mxc[:, 0:1],
                                         in1=mxc[:, 1:2], op=ALU.max)

            @sq.step(blk, "sync")
            def _(eng, st, ic=ic):
                st.dma(g2D[128 * ic:128 * (ic + 1), :], chb[:, :])

            @sq.step(blk, "sync")
            def _(eng, st, ic=ic):
                st.dma(chb[:, :], upD[128 * ic:128 * (ic + 1), :])

            @sq.step(blk, "vector")
            def _(eng, st):
                st.chain(eng.tensor_copy(out=ch[:, :], in_=chb[:, :]))
                st.chain(emit_qdq(eng, st, ch[:, :], ch[:, :], suc[:, 1:2],
                                  suc[:, 0:1], rr[:, :], ti[:, :], tf[:, :], td[:, :]))
                return eng.tensor_copy(out=chb[:, :], in_=ch[:, :])

            @sq.step(blk, "sync")
            def _(eng, st, ic=ic):
                st.dma(upqD[128 * ic:128 * (ic + 1), :], chb[:, :])

        emit_colmax_to_scalar(sq, blk, nc, mxc[:, 0:1], identf, psT, mrow, srow[0:1, 3:4])

        @sq.step(blk, "sync")
        def _(eng, st):
            st.dma(io["mg2_in"][0:1, 0:1], srow[0:1, 3:4])



def emit_E3(nc, blk, sq, io):
    upqD = io["upqD"]
    g2D = io["g2D"]
    oD = io["oD"]

    NIC = I_L1 // 128

    from contextlib import ExitStack
    with ExitStack() as _ctx:
        ch = _ctx.enter_context(nc.sbuf_tensor("Gch", [128, TPC], F32))
        chb = _ctx.enter_context(nc.sbuf_tensor("Gchb", [128, TPC], BF16))
        chb2 = _ctx.enter_context(nc.sbuf_tensor("Gchb2", [128, TPC], BF16))
        rr = _ctx.enter_context(nc.sbuf_tensor("Grr", [128, TPC], F32))
        ti = _ctx.enter_context(nc.sbuf_tensor("Gti", [128, TPC], I32))
        tf = _ctx.enter_context(nc.sbuf_tensor("Gtf", [128, TPC], F32))
        td = _ctx.enter_context(nc.sbuf_tensor("Gtd", [128, TPC], F32))
        onesr = _ctx.enter_context(nc.sbuf_tensor("Gonesr", [1, 128], F32))
        identf = _ctx.enter_context(nc.sbuf_tensor("Gidentf", [128, 128], F32))
        srow = _ctx.enter_context(nc.sbuf_tensor("Gsrow", [1, 8], F32))
        mrow = _ctx.enter_context(nc.sbuf_tensor("Gmrow", [1, 128], F32))
        sgc = _ctx.enter_context(nc.sbuf_tensor("Gsgc", [128, 2], F32))
        mxc = _ctx.enter_context(nc.sbuf_tensor("Gmxc", [128, 2], F32))
        psT = _ctx.enter_context(nc.psum_tensor("GpsT", [128, 128], F32))

        @sq.step(blk, "gpsimd")
        def _(eng, st):
            st.dma(srow[0:1, 0:8], io["mg2_g"].rearrange("r o -> o r")[0:1, :])

        @sq.step(blk, "gpsimd")
        def _(eng, st):
            return make_ident(nc, eng, identf[:, :], st)

        @sq.step(blk, "vector")
        def _(eng, st):
            eng.memset(onesr[:, :], 1.0)
            return eng.memset(mxc[:, 0:1], 0.0)

        emit_scale_from_max(sq, blk, nc, onesr, srow, 8, 32767.0, srow,
                            sgc[:, 0:1], sgc[:, 1:2], psT)

        for ic in range(NIC):
            @sq.step(blk, "sync")
            def _(eng, st, ic=ic):
                st.dma(chb[:, :], g2D[128 * ic:128 * (ic + 1), :])
                st.dma(chb2[:, :], upqD[128 * ic:128 * (ic + 1), :])

            @sq.step(blk, "vector")
            def _(eng, st):
                st.chain(eng.tensor_copy(out=ch[:, :], in_=chb[:, :]))
                st.chain(emit_qdq(eng, st, ch[:, :], ch[:, :], sgc[:, 1:2],
                                  sgc[:, 0:1], rr[:, :], ti[:, :], tf[:, :], td[:, :]))
                st.chain(eng.tensor_tensor(out=ch[:, :], in0=ch[:, :], in1=chb2[:, :],
                                           op=ALU.mult))
                st.chain(eng.tensor_copy(out=chb[:, :], in_=ch[:, :]))
                st.chain(eng.reduce_max(out=mxc[:, 1:2], in_=ch[:, :],
                                        axis=mybir.AxisListType.X,
                                        apply_absolute_value=True))
                return eng.tensor_tensor(out=mxc[:, 0:1], in0=mxc[:, 0:1],
                                         in1=mxc[:, 1:2], op=ALU.max)

            @sq.step(blk, "sync")
            def _(eng, st, ic=ic):
                st.dma(oD[128 * ic:128 * (ic + 1), :], chb[:, :])

        emit_colmax_to_scalar(sq, blk, nc, mxc[:, 0:1], identf, psT, mrow, srow[0:1, 3:4])

        @sq.step(blk, "sync")
        def _(eng, st):
            st.dma(io["mo_in"][0:1, 0:1], srow[0:1, 3:4])



def emit_E4(nc, blk, sq, io):
    oD = io["oD"]
    wdn = io["wdn"]
    xBT = io["xBT"]
    xoutT = io["xoutT"]

    NIC = I_L1 // 128

    from contextlib import ExitStack
    with ExitStack() as _ctx:
        oq = _ctx.enter_context(nc.sbuf_tensor("Hoq", [128, NIC * TPC // 4], BF16))
        wdT = _ctx.enter_context(nc.sbuf_tensor("HwdT", [128, NIC * 1024], BF16))
        wch = _ctx.enter_context(nc.sbuf_tensor("Hwch", [128, I_L1], F32))
        wqc = _ctx.enter_context(nc.sbuf_tensor("Hwqc", [128, I_L1], BF16))
        sw = _ctx.enter_context(nc.sbuf_tensor("Hsw", [128, 96], F32))
        swr = _ctx.enter_context(nc.sbuf_tensor("Hswr", [128, 96], F32))
        rr = _ctx.enter_context(nc.sbuf_tensor("Hrr", [128, I_L1], F32))
        ti = _ctx.enter_context(nc.sbuf_tensor("Hti", [128, I_L1], I32))
        tf = _ctx.enter_context(nc.sbuf_tensor("Htf", [128, I_L1], F32))
        td = _ctx.enter_context(nc.sbuf_tensor("Htd", [128, I_L1], F32))
        chb = _ctx.enter_context(nc.sbuf_tensor("Hchb", [128, TPC], BF16))
        ch = _ctx.enter_context(nc.sbuf_tensor("Hch", [128, TPC], F32))
        onesr = _ctx.enter_context(nc.sbuf_tensor("Honesr", [1, 128], F32))
        identb = _ctx.enter_context(nc.sbuf_tensor("Hidentb", [128, 128], BF16))
        srow = _ctx.enter_context(nc.sbuf_tensor("Hsrow", [1, 8], F32))
        scol = _ctx.enter_context(nc.sbuf_tensor("Hscol", [128, 1], F32))
        icol = _ctx.enter_context(nc.sbuf_tensor("Hicol", [128, 1], F32))
        xbu = _ctx.enter_context(nc.sbuf_tensor("Hxbu", [128, 512], F32))
        xao = _ctx.enter_context(nc.sbuf_tensor("Hxao", [128, 512], F32))
        psA = _ctx.enter_context(nc.psum_tensor("HpsA", [128, 512], F32))
        psW = _ctx.enter_context(nc.psum_tensor("HpsW", [128, 128], BF16))
        psT = _ctx.enter_context(nc.psum_tensor("HpsT", [128, 128], F32))
        oqv = oq.ap().rearrange("p (c t) -> p c t", c=NIC)
        wdv = wdT.ap().rearrange("p (c t) -> p c t", c=NIC)

        @sq.step(blk, "gpsimd")
        def _(eng, st):
            st.dma(srow[0:1, 0:8], io["mo_g"].rearrange("r o -> o r")[0:1, :])

        @sq.step(blk, "gpsimd")
        def _(eng, st):
            return make_ident(nc, eng, identb[:, :], st)

        @sq.step(blk, "vector")
        def _(eng, st):
            return eng.memset(onesr[:, :], 1.0)

        emit_scale_from_max(sq, blk, nc, onesr, srow, 8, 32767.0, srow, scol[:, 0:1], icol[:, 0:1], psT)

        for hc in range(NCH):
            @sq.step(blk, "sync")
            def _(eng, st, hc=hc):
                st.dma(wch[:, :], wdn[128 * hc:128 * (hc + 1), :])

            @sq.step(blk, "vector")
            def _(eng, st):
                return emit_dequant_chunk(eng, st, wch[:, :], wqc[:, :], 96,
                                          sw[:, :], swr[:, :], rr[:, :],
                                          ti[:, :], tf[:, :], td[:, :])

            for ic in range(NIC):
                @sq.step(blk, "tensor")
                def _(eng, st, ic=ic):
                    return eng.transpose(psW[:, 0:128],
                                         wqc[:, 128 * ic:128 * (ic + 1)],
                                         identb[:, :])

                @sq.step(blk, "vector")
                def _(eng, st, ic=ic, hc=hc):
                    return eng.tensor_copy(out=wdv[:, ic][:, 128 * hc:128 * (hc + 1)],
                                           in_=psW[:, 0:128])

        for tt in range(NTT):
            for ic in range(NIC):
                @sq.step(blk, "sync")
                def _(eng, st, ic=ic, tt=tt):
                    st.dma(chb[:, 0:512], oD[128 * ic:128 * (ic + 1),
                                             512 * tt:512 * (tt + 1)])

                @sq.step(blk, "vector")
                def _(eng, st, ic=ic):
                    st.chain(eng.tensor_copy(out=ch[:, 0:512], in_=chb[:, 0:512]))
                    return emit_qdq(eng, st, oqv[:, ic], ch[:, 0:512], icol[:, 0:1],
                                    scol[:, 0:1], rr[:, 0:512], ti[:, 0:512],
                                    tf[:, 0:512], td[:, 0:512])

            for oc in range(NCH):
                @sq.step(blk, "tensor")
                def _(eng, st, oc=oc):
                    last = None
                    for k in range(NIC):
                        last = eng.matmul(psA[:, :],
                                          wdv[:, k][:, 128 * oc:128 * (oc + 1)],
                                          oqv[:, k], start=(k == 0), stop=(k == NIC - 1))
                    return last

                @sq.step(blk, "sync")
                def _(eng, st, oc=oc, tt=tt):
                    st.dma(xbu[:, :], xBT[128 * oc:128 * (oc + 1), 512 * tt:512 * (tt + 1)])

                @sq.step(blk, "vector")
                def _(eng, st):
                    return eng.tensor_tensor(out=xao[:, :], in0=psA[:, :],
                                             in1=xbu[:, :], op=ALU.add)

                @sq.step(blk, "sync")
                def _(eng, st, oc=oc, tt=tt):
                    st.dma(xoutT[128 * oc:128 * (oc + 1), 512 * tt:512 * (tt + 1)],
                           xao[:, :])



_EXT = [
    ("xT", [C, TPC], F32), ("posoff", [128, 1], F32),
    ("wq", [C, 1024], F32), ("wk", [C, 256], F32), ("wvv", [C, 256], F32),
    ("wvg", [C, 256], F32), ("wqi", [C, 256], F32), ("wki", [C, 64], F32),
    ("wwi", [C, 4], F32), ("wog", [C, 1024], F32),
    ("bvg", [1, 256], F32), ("bog", [1, 1024], F32),
    ("bi_in", [1, 4], F32), ("wo", [1024, 1024], F32),
    ("gwT", [C, 16], F32), ("ebias", [1, 16], F32),
    ("wgT", [16, C, 1024], F32), ("wuT", [16, C, 1024], F32),
    ("wdT", [16, 1024, C], F32),
    ("sgT", [C, 1024], F32), ("suT", [C, 1024], F32), ("sdT", [1024, C], F32),
    ("wkv", [C, 2048], F32), ("bkv", [1, 2048], F32),
    ("wq1", [C, 1024], F32), ("bq1", [1, 1024], F32),
    ("wp1", [C, 1024], F32), ("bp1", [1, 1024], F32),
    ("wgn", [I_L1, C], F32), ("wun", [I_L1, C], F32), ("wdn", [C, I_L1], F32),
]
_INT = [
    ("kv_loc", [TPC, 576], BF16), ("kv_all", [B * L, 576], BF16),
    ("qrT", [C, TPC], BF16), ("qiT", [256, TPC], BF16),
    ("g1T", [C, TPC], BF16), ("xnT", [C, TPC], BF16), ("wI_tok", [TPC, 4], F32),
    ("xAT", [C, TPC], F32), ("ctrD", [16, TPC], F32),
    ("x1T", [C, TPC], F32), ("xn1T", [C, TPC], BF16), ("xn1F", [C, TPC], F32),
    ("xmax_in", [1, 1], F32), ("xmax_g", [NCORE, 1], F32),
    ("xBT", [C, TPC], F32),
    ("upD", [I_L1, TPC], BF16), ("gateD", [I_L1, TPC], BF16),
    ("m1_in", [1, 2], F32), ("m1_g", [NCORE, 2], F32),
    ("upqD", [I_L1, TPC], BF16), ("g2D", [I_L1, TPC], BF16),
    ("mg2_in", [1, 1], F32), ("mg2_g", [NCORE, 1], F32),
    ("oD", [I_L1, TPC], BF16), ("mo_in", [1, 1], F32), ("mo_g", [NCORE, 1], F32),
]


def build_full():
    nc = bass.Bass("TRN2", target_bir_lowering=False)
    io, ioh = {}, {}
    for name, shape, dt in _EXT:
        io[name] = nc.dram_tensor(name, shape, dt, kind="ExternalInput").ap()
    io["xoutT"] = nc.dram_tensor("xoutT", [C, TPC], F32, kind="ExternalOutput").ap()
    for name, shape, dt in _INT:
        h = nc.dram_tensor(name, shape, dt)
        ioh[name] = h
        io[name] = h.ap()
    with nc.Block() as blk:
        sq = Seq(nc)
        emit_A(nc, blk, sq, io)
        sq.collective(blk, ioh["kv_loc"], ioh["kv_all"])
        emit_B(nc, blk, sq, io)
        emit_C(nc, blk, sq, io)
        sq.collective(blk, ioh["xmax_in"], ioh["xmax_g"])
        emit_D(nc, blk, sq, io)
        emit_E1(nc, blk, sq, io)
        sq.collective(blk, ioh["m1_in"], ioh["m1_g"])
        emit_E2(nc, blk, sq, io)
        sq.collective(blk, ioh["mg2_in"], ioh["mg2_g"])
        emit_E3(nc, blk, sq, io)
        sq.collective(blk, ioh["mo_in"], ioh["mo_g"])
        emit_E4(nc, blk, sq, io)

        @sq.step(blk, "sync")
        def _(eng, st):
            return eng.nop()

    return nc


def _shard_x_T(x):
    out = []
    for c in range(NCORE):
        sl = x[:, 512 * c:512 * (c + 1), :]
        sl = sl.reshape(TPC, C).T
        out.append(np.ascontiguousarray(sl, dtype=np.float32))
    return out


def kernel(**inputs):
    inputs = {k: np.asarray(v) for k, v in inputs.items()}
    if "full" not in _prog_cache:
        _prog_cache["full"] = build_full()
    nc = _prog_cache["full"]
    f32 = lambda a: np.ascontiguousarray(np.asarray(a, np.float32))
    tr = lambda a, ax: np.ascontiguousarray(
        np.transpose(np.asarray(a, np.float32), ax))
    base = {
        "wq": f32(inputs["L0_attn_Wq"]),
        "wk": f32(inputs["L0_attn_Wk"]),
        "wvv": f32(inputs["L0_attn_Wv"]),
        "wvg": f32(inputs["L0_attn_Wvg"]),
        "wqi": f32(inputs["L0_attn_Wqi"]),
        "wki": f32(inputs["L0_attn_Wki"]),
        "wwi": f32(inputs["L0_attn_Wwi"]),
        "wog": f32(inputs["L0_attn_Wog"]),
        "bvg": f32(inputs["L0_attn_bvg"]).reshape(1, 256),
        "bog": f32(inputs["L0_attn_bog"]).reshape(1, 1024),
        "bi_in": f32(inputs["L0_attn_bi"]).reshape(1, 4),
        "wo": f32(inputs["L0_attn_Wo"]),
        "gwT": tr(inputs["L0_ffn_gate_w"], (1, 0)),
        "ebias": f32(inputs["L0_ffn_expert_bias"]).reshape(1, 16),
        "wgT": tr(inputs["L0_ffn_w_gate"], (0, 2, 1)),
        "wuT": tr(inputs["L0_ffn_w_up"], (0, 2, 1)),
        "wdT": tr(inputs["L0_ffn_w_down"], (0, 2, 1)),
        "sgT": tr(inputs["L0_ffn_shared_gate_w"], (1, 0)),
        "suT": tr(inputs["L0_ffn_shared_up_w"], (1, 0)),
        "sdT": tr(inputs["L0_ffn_shared_down_w"], (1, 0)),
        "wkv": f32(inputs["L1_attn_Wkv"]),
        "bkv": f32(inputs["L1_attn_bkv"]).reshape(1, 2048),
        "wq1": f32(inputs["L1_attn_Wq"]),
        "bq1": f32(inputs["L1_attn_bq"]).reshape(1, 1024),
        "wp1": f32(inputs["L1_attn_Wp"]),
        "bp1": f32(inputs["L1_attn_bp"]).reshape(1, 1024),
        "wgn": f32(inputs["L1_ffn_w_gate"]),
        "wun": f32(inputs["L1_ffn_w_up"]),
        "wdn": f32(inputs["L1_ffn_w_down"]),
    }
    xs = _shard_x_T(inputs["L0_attn_hidden_states"])
    in_maps = [dict(base, xT=xs[c],
                    posoff=np.full((128, 1), 512.0 * c, np.float32))
               for c in range(NCORE)]
    res = run_bass_kernel_spmd(nc, in_maps, core_ids=list(range(NCORE)))
    out = np.empty((B, L, C), np.float32)
    for c in range(NCORE):
        xo = np.asarray(res.results[c]["xoutT"])
        out[:, 512 * c:512 * (c + 1), :] = xo.T.reshape(B, 512, C)
    return out


# version 08
# speedup vs baseline: 1.7545x; 1.3237x over previous
import math
import numpy as np

import concourse.bass as bass
import concourse.mybir as mybir
from concourse.bass_utils import run_bass_kernel_spmd

F32 = mybir.dt.float32
F32R = mybir.dt.float32r
BF16 = mybir.dt.bfloat16
I32 = mybir.dt.int32
AF = mybir.ActivationFunctionType
ALU = mybir.AluOpType

NCORE = 8
B, L, C = 4, 4096, 1024
TPC = B * L // NCORE
PPC = L // NCORE
NCH = C // 128
NTT = TPC // 512
H, H_KV, D = 16, 4, 64
K_SEL = 256
H_I, D_I = 4, 64
ROPE_BASE = 10000.0
SM_SCALE = D ** -0.5
I_L1 = 3072
LN_EPS = 1e-5
NEG_BIG = -30000.0
TWO_PI = 2.0 * math.pi

ENGINES = ("sync", "tensor", "vector", "scalar", "gpsimd")

_prog_cache = {}


class Seq:

    def __init__(self, nc):
        self.nc = nc
        keys = list(ENGINES) + [e + "_dma" for e in ENGINES]
        self.sems = {e: nc.alloc_semaphore(f"sq_{e}") for e in keys}
        self.cnt = {e: 0 for e in keys}
        self.waited = {}

    def _emit_waits(self, eng, ename, pairs):
        cache = self.waited.setdefault(ename, {})
        for key, val in pairs:
            if val > cache.get(key, 0):
                eng.wait_ge(self.sems[key], val)
                cache[key] = val

    def collective(self, blk, in_handle, out_handle):
        key = f"coll{len([k for k in self.cnt if k.startswith('coll')])}"
        sem = self.nc.alloc_semaphore("sq_" + key)
        self.sems[key] = sem
        self.cnt[key] = 0
        waits = [(self.sems[e], self.cnt[e]) for e in self.cnt if self.cnt[e] > 0]

        def body(eng):
            for s, val in waits:
                eng.wait_ge(s, val)
            eng.collective_compute(
                "AllGather", mybir.AluOpType.bypass,
                replica_groups=[list(range(NCORE))],
                ins=[in_handle.ap().opt()], outs=[out_handle.ap().opt()],
            ).then_inc(sem)

        blk.gpsimd(body)
        self.cnt[key] = 1

    def step(self, blk, ename):
        waits = [(e, self.cnt[e]) for e in self.cnt if self.cnt[e] > 0]

        def deco(f):
            seq = self

            def body(eng):
                seq._emit_waits(eng, ename, waits)
                st = _Step(seq, ename, eng)
                last = f(eng, st)
                if last is not None:
                    last.then_inc(seq.sems[ename], 1)
                    seq.cnt[ename] += 1
                else:
                    assert st.marked, "step emitted nothing trackable"

            getattr(blk, ename)(body)

        return deco


class _Step:
    def __init__(self, seq, ename, eng):
        self.seq = seq
        self.ename = ename
        self.eng = eng
        self.marked = False

    def dma(self, out, in_, **kw):
        key = self.ename + "_dma"
        inst = self.eng.dma_start(out=out, in_=in_, **kw)
        inst.then_inc(self.seq.sems[key], 16)
        self.seq.cnt[key] += 16
        self.marked = True
        return inst

    def chain(self, inst):
        inst.then_inc(self.seq.sems[self.ename], 1)
        self.seq.cnt[self.ename] += 1
        self.eng.wait_ge(self.seq.sems[self.ename], self.seq.cnt[self.ename])
        self.seq.waited.setdefault(self.ename, {})[self.ename] = \
            self.seq.cnt[self.ename]
        self.marked = True
        return inst


def make_rot_T(nc, g, rt, st):
    st.chain(g.memset(rt, 0.0))
    last = None
    for blk_ in (0, 64):
        g.affine_select(out=rt[:, blk_:blk_ + 32], in_=rt[:, blk_:blk_ + 32],
                        compare_op=ALU.not_equal, fill=-1.0,
                        base=-(blk_ + 32), pattern=[[-1, 32]], channel_multiplier=1)
        last = g.affine_select(out=rt[:, blk_ + 32:blk_ + 64], in_=rt[:, blk_ + 32:blk_ + 64],
                               compare_op=ALU.not_equal, fill=1.0,
                               base=-blk_, pattern=[[-1, 32]], channel_multiplier=1)
    return last


def make_ident(nc, g, ident, st):
    st.chain(g.memset(ident, 0.0))
    return g.affine_select(out=ident, in_=ident, compare_op=ALU.not_equal, fill=1.0,
                           base=0, pattern=[[-1, ident.shape[1]]], channel_multiplier=1)



def emit_A(nc, blk, sq, io):
    xT = io["xT"]
    win = {}
    for name, cols in [("wq", 1024), ("wk", 256), ("wvv", 256), ("wvg", 256),
                       ("wqi", 256), ("wki", 64), ("wwi", 4), ("wog", 1024)]:
        win[name] = (io[name], cols)
    posoff = io["posoff"]
    bvg = io["bvg"]
    bog = io["bog"]
    kv_loc = io["kv_loc"]
    qrT = io["qrT"]
    qiT = io["qiT"]
    g1T = io["g1T"]
    xnT = io["xnT"]
    wI_tok = io["wI_tok"]

    woff = {}
    off = 0
    for name in ("wq", "wk", "wvv", "wvg", "wqi", "wki", "wwi", "wog"):
        woff[name] = off
        off += win[name][1]
    WCOLS = off

    from contextlib import ExitStack
    with ExitStack() as _ctx:
        wsb = _ctx.enter_context(nc.sbuf_tensor("Awsb", [128, NCH * WCOLS], BF16))
        xtt = _ctx.enter_context(nc.sbuf_tensor("Axtt", [128, NCH * 512], F32))
        xntt = _ctx.enter_context(nc.sbuf_tensor("Axntt", [128, NCH * 512], BF16))
        x2 = _ctx.enter_context(nc.sbuf_tensor("Ax2", [128, 512], F32))
        onesf = _ctx.enter_context(nc.sbuf_tensor("Aonesf", [128, 1], F32))
        identb = _ctx.enter_context(nc.sbuf_tensor("Aidentb", [128, 128], BF16))
        srow = _ctx.enter_context(nc.sbuf_tensor("Asrow", [1, 3 * 512], F32))
        mb = _ctx.enter_context(nc.sbuf_tensor("Amb", [128, 512], F32))
        rb = _ctx.enter_context(nc.sbuf_tensor("Arb", [128, 512], F32))
        rotT = _ctx.enter_context(nc.sbuf_tensor("ArotT", [128, 128], BF16))
        costab = _ctx.enter_context(nc.sbuf_tensor("Acostab", [128, 512], BF16))
        sintab = _ctx.enter_context(nc.sbuf_tensor("Asintab", [128, 512], BF16))
        tsc = _ctx.enter_context(nc.sbuf_tensor("Atsc", [128, 512], F32))
        tsc2 = _ctx.enter_context(nc.sbuf_tensor("Atsc2", [128, 512], F32))
        tin = _ctx.enter_context(nc.sbuf_tensor("Atin", [128, 512], I32))
        ifr = _ctx.enter_context(nc.sbuf_tensor("Aifr", [128, 1], F32))
        pidf = _ctx.enter_context(nc.sbuf_tensor("Apidf", [128, 1], F32))
        epsc = _ctx.enter_context(nc.sbuf_tensor("Aepsc", [1, 1], F32))
        onesr = _ctx.enter_context(nc.sbuf_tensor("Aonesr", [1, 128], F32))
        qsb = _ctx.enter_context(nc.sbuf_tensor("Aqsb", [128, 512], BF16))
        rosb = _ctx.enter_context(nc.sbuf_tensor("Arosb", [128, 512], BF16))
        t1 = _ctx.enter_context(nc.sbuf_tensor("At1", [128, 512], F32))
        gsb = _ctx.enter_context(nc.sbuf_tensor("Agsb", [128, 512], BF16))
        biasb = _ctx.enter_context(nc.sbuf_tensor("Abiasb", [128, 10], F32))
        wisb = _ctx.enter_context(nc.sbuf_tensor("Awisb", [128, 4], F32))
        ktt = _ctx.enter_context(nc.sbuf_tensor("Aktt", [128, 5 * 512], BF16))
        kvsb = _ctx.enter_context(nc.sbuf_tensor("Akvsb", [128, 4 * 576], BF16))
        ps0 = _ctx.enter_context(nc.psum_tensor("Aps0", [128, 512], F32))
        ps1 = _ctx.enter_context(nc.psum_tensor("Aps1", [128, 512], F32))
        ps2 = _ctx.enter_context(nc.psum_tensor("Aps2", [128, 512], F32))
        prow = _ctx.enter_context(nc.psum_tensor("Aprow", [1, 1024], F32))
        pqt = _ctx.enter_context(nc.psum_tensor("Apqt", [128, 4], F32))
        ptr = _ctx.enter_context(nc.psum_tensor("Aptr", [128, 640], BF16))
        wv_ = wsb.ap().rearrange("p (k c) -> p k c", k=NCH)
        xv_ = xtt.ap().rearrange("p (c t) -> p c t", c=NCH)
        xnv = xntt.ap().rearrange("p (c t) -> p c t", c=NCH)
        ktv = ktt.ap().rearrange("p (c t) -> p c t", c=5)

        def W(name, k, c0, c1):
            return wv_[:, k, woff[name] + c0:woff[name] + c1]

        @sq.step(blk, "gpsimd")
        def _(eng, st):
            for name in woff:
                wap, cols = win[name]
                for k in range(NCH):
                    st.dma(wv_[:, k, woff[name]:woff[name] + cols],
                           wap[128 * k:128 * (k + 1), :])
            st.dma(pidf[:, :], posoff[:, :])
            st.dma(biasb[:, 0:1], bvg[0:1, 0:128].rearrange("o p -> p o"))
            st.dma(biasb[:, 1:2], bvg[0:1, 128:256].rearrange("o p -> p o"))
            for cc in range(8):
                st.dma(biasb[:, 2 + cc:3 + cc],
                       bog[0:1, 128 * cc:128 * (cc + 1)].rearrange("o p -> p o"))

        @sq.step(blk, "gpsimd")
        def _(eng, st):
            make_rot_T(nc, eng, rotT[:, :], st)
            make_ident(nc, eng, identb[:, :], st)
            eng.iota(tin[:, 0:1], pattern=[[0, 1]], base=0, channel_multiplier=1)
            return eng.iota(tin[:, 1:2], pattern=[[0, 1]], base=0, channel_multiplier=1)

        @sq.step(blk, "vector")
        def _(eng, st):
            eng.memset(onesf[:, :], 1.0)
            eng.memset(epsc[:, :], LN_EPS)
            eng.memset(onesr[:, :], 1.0)
            st.chain(eng.tensor_copy(out=t1[:, 0:1], in_=tin[:, 0:1]))
            return eng.tensor_copy(out=ifr[:, :], in_=t1[:, 0:1])

        for thr in (32.0, 64.0, 96.0):
            @sq.step(blk, "vector")
            def _(eng, st, thr=thr):
                st.chain(eng.tensor_scalar(out=tsc[:, 0:1], in0=t1[:, 0:1], scalar1=thr,
                                           scalar2=-32.0, op0=ALU.is_ge, op1=ALU.mult))
                return eng.tensor_tensor(out=ifr[:, :], in0=ifr[:, :],
                                         in1=tsc[:, 0:1], op=ALU.add)

        @sq.step(blk, "scalar")
        def _(eng, st):
            return eng.activation(out=ifr[:, :], in_=ifr[:, :], func=AF.Exp,
                                  scale=-math.log(ROPE_BASE) / 32.0)

        @sq.step(blk, "gpsimd")
        def _(eng, st):
            return eng.iota(tin[:, :], pattern=[[1, 512]], base=0, channel_multiplier=0)

        @sq.step(blk, "vector")
        def _(eng, st):
            return eng.tensor_copy(out=tsc[:, :], in_=tin[:, :])

        @sq.step(blk, "vector")
        def _(eng, st):
            return eng.tensor_scalar(out=tsc[:, :], in0=tsc[:, :],
                                     scalar1=pidf[:, 0:1], scalar2=None, op0=ALU.add)

        @sq.step(blk, "vector")
        def _(eng, st):
            st.chain(eng.tensor_scalar(out=tsc[:, :], in0=tsc[:, :], scalar1=ifr[:, 0:1],
                                       scalar2=None, op0=ALU.mult))
            return eng.tensor_scalar(out=tsc2[:, :], in0=tsc[:, :],
                                     scalar1=1.0 / TWO_PI, scalar2=None, op0=ALU.mult)

        @sq.step(blk, "vector")
        def _(eng, st):
            return eng.tensor_copy(out=tin[:, :], in_=tsc2[:, :])

        @sq.step(blk, "vector")
        def _(eng, st):
            return eng.tensor_copy(out=tsc[:, :], in_=tin[:, :])

        @sq.step(blk, "vector")
        def _(eng, st):
            st.chain(eng.tensor_tensor(out=tsc2[:, :], in0=tsc2[:, :], in1=tsc[:, :],
                                       op=ALU.subtract))
            st.chain(eng.tensor_scalar(out=t1[:, :], in0=tsc2[:, :], scalar1=0.5,
                                       scalar2=None, op0=ALU.is_gt))
            st.chain(eng.tensor_tensor(out=tsc2[:, :], in0=tsc2[:, :], in1=t1[:, :],
                                       op=ALU.subtract))
            st.chain(eng.tensor_scalar(out=t1[:, :], in0=tsc2[:, :], scalar1=-0.5,
                                       scalar2=None, op0=ALU.is_lt))
            st.chain(eng.tensor_tensor(out=tsc2[:, :], in0=tsc2[:, :], in1=t1[:, :],
                                       op=ALU.add))
            return eng.tensor_scalar(out=tsc[:, :], in0=tsc2[:, :],
                                     scalar1=TWO_PI, scalar2=None, op0=ALU.mult)

        @sq.step(blk, "scalar")
        def _(eng, st):
            return eng.activation(out=sintab[:, :], in_=tsc[:, :], func=AF.Sin)

        @sq.step(blk, "vector")
        def _(eng, st):
            st.chain(eng.tensor_scalar(out=tsc2[:, :], in0=tsc[:, :], scalar1=-1.0,
                                       scalar2=math.pi / 2.0, op0=ALU.mult, op1=ALU.add))
            st.chain(eng.tensor_scalar(out=tsc[:, :], in0=tsc2[:, :], scalar1=math.pi,
                                       scalar2=-TWO_PI, op0=ALU.is_gt, op1=ALU.mult))
            return eng.tensor_tensor(out=tsc2[:, :], in0=tsc2[:, :], in1=tsc[:, :],
                                     op=ALU.add)

        @sq.step(blk, "scalar")
        def _(eng, st):
            return eng.activation(out=costab[:, :], in_=tsc2[:, :], func=AF.Sin)

        for tt in range(NTT):
            @sq.step(blk, "sync")
            def _(eng, st, tt=tt):
                for c in range(NCH):
                    st.dma(xv_[:, c], xT[128 * c:128 * (c + 1), 512 * tt:512 * (tt + 1)])

            for c in range(NCH):
                @sq.step(blk, "vector")
                def _(eng, st, c=c):
                    return eng.tensor_tensor(out=x2[:, :], in0=xv_[:, c],
                                             in1=xv_[:, c], op=ALU.mult)

                @sq.step(blk, "tensor")
                def _(eng, st, c=c):
                    eng.matmul(prow[:, 0:512], onesf[:, :],
                               xv_[:, c],
                               start=(c == 0), stop=(c == NCH - 1))
                    return eng.matmul(prow[:, 512:1024], onesf[:, :],
                                      x2[:, :],
                                      start=(c == 0), stop=(c == NCH - 1))

            @sq.step(blk, "vector")
            def _(eng, st):
                eng.tensor_scalar(out=srow[:, 0:512], in0=prow[:, 0:512],
                                  scalar1=1.0 / C, scalar2=None, op0=ALU.mult)
                st.chain(eng.tensor_scalar(out=srow[:, 512:1024], in0=prow[:, 512:1024],
                                           scalar1=1.0 / C, scalar2=None, op0=ALU.mult))
                st.chain(eng.tensor_tensor(out=srow[:, 1024:1536], in0=srow[:, 0:512],
                                           in1=srow[:, 0:512], op=ALU.mult))
                return eng.tensor_tensor(out=srow[:, 512:1024], in0=srow[:, 512:1024],
                                         in1=srow[:, 1024:1536], op=ALU.subtract)

            @sq.step(blk, "scalar")
            def _(eng, st):
                return eng.activation(out=srow[:, 1024:1536], in_=srow[:, 512:1024],
                                      func=AF.Sqrt, bias=epsc[0:1, 0:1])

            @sq.step(blk, "vector")
            def _(eng, st):
                return eng.reciprocal(out=srow[:, 512:1024], in_=srow[:, 1024:1536])

            @sq.step(blk, "tensor")
            def _(eng, st):
                eng.matmul(ps0[:, :], onesr[:, :],
                           srow[:, 0:512], start=True, stop=True)
                return eng.matmul(ps1[:, :], onesr[:, :],
                                  srow[:, 512:1024], start=True, stop=True)

            @sq.step(blk, "vector")
            def _(eng, st):
                eng.tensor_copy(out=mb[:, :], in_=ps0[:, :])
                return eng.tensor_copy(out=rb[:, :], in_=ps1[:, :])

            for c in range(NCH):
                @sq.step(blk, "vector")
                def _(eng, st, c=c):
                    st.chain(eng.tensor_tensor(out=x2[:, :], in0=xv_[:, c], in1=mb[:, :],
                                               op=ALU.subtract))
                    return eng.tensor_tensor(out=xnv[:, c], in0=x2[:, :], in1=rb[:, :],
                                             op=ALU.mult)

            @sq.step(blk, "sync")
            def _(eng, st, tt=tt):
                for c in range(NCH):
                    st.dma(xnT[128 * c:128 * (c + 1), 512 * tt:512 * (tt + 1)],
                           xnv[:, c])

            def proj_step(name, oc, psb, c0=None, c1=None):
                c0 = 128 * oc if c0 is None else c0
                c1 = 128 * (oc + 1) if c1 is None else c1

                @sq.step(blk, "tensor")
                def _(eng, st):
                    last = None
                    for k in range(NCH):
                        last = eng.matmul(psb[0:c1 - c0, 0:512], W(name, k, c0, c1),
                                          xnv[:, k], start=(k == 0), stop=(k == NCH - 1))
                    return last

            def rope_steps(oc, dst_ap, dma_to=None, tt=tt):
                @sq.step(blk, "scalar")
                def _(eng, st):
                    return eng.activation(out=qsb[:, :], in_=ps0[:, :], func=AF.Copy)

                @sq.step(blk, "tensor")
                def _(eng, st):
                    return eng.matmul(ps1[:, :], rotT[:, :], qsb[:, :],
                                      start=True, stop=True)

                @sq.step(blk, "scalar")
                def _(eng, st):
                    return eng.activation(out=rosb[:, :], in_=ps1[:, :], func=AF.Copy)

                @sq.step(blk, "vector")
                def _(eng, st):
                    eng.tensor_tensor(out=t1[:, :], in0=qsb[:, :], in1=costab[:, :],
                                      op=ALU.mult)
                    st.chain(eng.tensor_tensor(out=x2[:, :], in0=rosb[:, :], in1=sintab[:, :],
                                               op=ALU.mult))
                    return eng.tensor_tensor(out=dst_ap, in0=t1[:, :], in1=x2[:, :],
                                             op=ALU.add)

                if dma_to is not None:
                    @sq.step(blk, "sync")
                    def _(eng, st):
                        st.dma(dma_to, dst_ap)

            for oc in range(NCH):
                proj_step("wq", oc, ps0)
                rope_steps(oc, qsb[:, :],
                           dma_to=qrT[128 * oc:128 * (oc + 1), 512 * tt:512 * (tt + 1)])

            for oc in range(2):
                proj_step("wk", oc, ps0)
                rope_steps(oc, ktv[:, oc])

            for oc in range(2):
                proj_step("wvv", oc, ps0)
                proj_step("wvg", oc, ps1)

                @sq.step(blk, "scalar")
                def _(eng, st, oc=oc):
                    return eng.activation(out=gsb[:, :], in_=ps1[:, :],
                                          func=AF.Sigmoid, bias=biasb[:, oc:oc + 1])

                @sq.step(blk, "vector")
                def _(eng, st, oc=oc):
                    return eng.tensor_tensor(out=ktv[:, 2 + oc], in0=ps0[:, :],
                                             in1=gsb[:, :], op=ALU.mult)

            proj_step("wki", 0, ps0, 0, 64)

            @sq.step(blk, "vector")
            def _(eng, st):
                st.chain(eng.memset(ktv[:, 4], 0.0))
                return eng.tensor_copy(out=ktv[:, 4][0:64, :], in_=ps0[0:64, 0:512])

            for oc in range(2):
                proj_step("wqi", oc, ps2)

                @sq.step(blk, "scalar")
                def _(eng, st):
                    return eng.activation(out=gsb[:, :], in_=ps2[:, :], func=AF.Copy)

                @sq.step(blk, "sync")
                def _(eng, st, oc=oc, tt=tt):
                    st.dma(qiT[128 * oc:128 * (oc + 1), 512 * tt:512 * (tt + 1)],
                           gsb[:, :])

            for oc in range(NCH):
                proj_step("wog", oc, ps2)

                @sq.step(blk, "scalar")
                def _(eng, st, oc=oc):
                    return eng.activation(out=gsb[:, :], in_=ps2[:, :],
                                          func=AF.Sigmoid, bias=biasb[:, 2 + oc:3 + oc])

                @sq.step(blk, "sync")
                def _(eng, st, oc=oc, tt=tt):
                    st.dma(g1T[128 * oc:128 * (oc + 1), 512 * tt:512 * (tt + 1)],
                           gsb[:, :])

            for qs in range(4):
                @sq.step(blk, "tensor")
                def _(eng, st, qs=qs):
                    last = None
                    for k in range(NCH):
                        last = eng.matmul(pqt[:, :], xnv[:, k][:, 128 * qs:128 * (qs + 1)],
                                          W("wwi", k, 0, 4), start=(k == 0), stop=(k == NCH - 1))
                    return last

                @sq.step(blk, "scalar")
                def _(eng, st):
                    return eng.activation(out=wisb[:, :], in_=pqt[:, :], func=AF.Sigmoid)

                @sq.step(blk, "sync")
                def _(eng, st, qs=qs, tt=tt):
                    st.dma(wI_tok[512 * tt + 128 * qs:512 * tt + 128 * (qs + 1), :],
                           wisb[:, :])

            for ts in range(4):
                @sq.step(blk, "tensor")
                def _(eng, st, ts=ts):
                    last = None
                    for fc in range(5):
                        last = eng.transpose(ptr[:, 128 * fc:128 * (fc + 1)],
                                             ktv[:, fc][:, 128 * ts:128 * (ts + 1)],
                                             identb[:, :])
                    return last

                @sq.step(blk, "vector")
                def _(eng, st, ts=ts):
                    for fc in range(4):
                        eng.tensor_copy(out=kvsb[:, 576 * ts + 128 * fc:576 * ts + 128 * (fc + 1)],
                                        in_=ptr[:, 128 * fc:128 * (fc + 1)])
                    return eng.tensor_copy(out=kvsb[:, 576 * ts + 512:576 * ts + 576],
                                           in_=ptr[:, 512:576])

            @sq.step(blk, "gpsimd")
            def _(eng, st, tt=tt):
                for ts in range(4):
                    st.dma(kv_loc[512 * tt + 128 * ts:512 * tt + 128 * (ts + 1), :],
                           kvsb[:, 576 * ts:576 * (ts + 1)])



def emit_B(nc, blk, sq, io, n_batches=4, bisect_iters=34):
    qrT = io["qrT"]
    qiT = io["qiT"]
    g1T = io["g1T"]
    xT = io["xT"]
    wI_tok = io["wI_tok"]
    kv_all = io["kv_all"]
    bi_in = io["bi_in"]
    wo = io["wo"]
    xAT = io["xAT"]

    from contextlib import ExitStack
    with ExitStack() as _ctx:
        wosb = _ctx.enter_context(nc.sbuf_tensor("Bwosb", [128, NCH * 1024], BF16))
        identb = _ctx.enter_context(nc.sbuf_tensor("Bidentb", [128, 128], BF16))
        onesr = _ctx.enter_context(nc.sbuf_tensor("Bonesr", [1, 128], F32))
        bib = _ctx.enter_context(nc.sbuf_tensor("Bbib", [128, 4], F32))
        kt2 = _ctx.enter_context(nc.sbuf_tensor("Bkt2", [128, 2 * L], BF16))
        ki64 = _ctx.enter_context(nc.sbuf_tensor("Bki64", [64, L], BF16))
        vtok = _ctx.enter_context(nc.sbuf_tensor("Bvtok", [128, 32 * 260], BF16))
        qr2 = _ctx.enter_context(nc.sbuf_tensor("Bqr2", [128, 8 * 512], BF16))
        qi64 = _ctx.enter_context(nc.sbuf_tensor("Bqi64", [64, 4 * 512], BF16))
        wib = _ctx.enter_context(nc.sbuf_tensor("Bwib", [128, 4 * 4], F32))
        ktok0 = _ctx.enter_context(nc.sbuf_tensor("Bktok0", [128, 384], BF16))
        ktok1 = _ctx.enter_context(nc.sbuf_tensor("Bktok1", [128, 384], BF16))
        mtsb = _ctx.enter_context(nc.sbuf_tensor("Bmtsb", [128, 32 * 512], BF16))
        isbA = _ctx.enter_context(nc.sbuf_tensor("BisbA", [128, L], F32))
        isbB = _ctx.enter_context(nc.sbuf_tensor("BisbB", [128, L], F32))
        scr = _ctx.enter_context(nc.sbuf_tensor("Bscr", [128, L], BF16))
        mk0 = _ctx.enter_context(nc.sbuf_tensor("Bmk0", [128, L], BF16))
        mk1 = _ctx.enter_context(nc.sbuf_tensor("Bmk1", [128, L], BF16))
        bst = _ctx.enter_context(nc.sbuf_tensor("Bbst", [128, 8], F32))
        sx0 = _ctx.enter_context(nc.sbuf_tensor("Bsx0", [128, 512], F32))
        sx1 = _ctx.enter_context(nc.sbuf_tensor("Bsx1", [128, 512], F32))
        sx2 = _ctx.enter_context(nc.sbuf_tensor("Bsx2", [128, 512], F32))
        sx3 = _ctx.enter_context(nc.sbuf_tensor("Bsx3", [128, 512], F32))
        pt0 = _ctx.enter_context(nc.sbuf_tensor("Bpt0", [128, 512], BF16))
        pt1 = _ctx.enter_context(nc.sbuf_tensor("Bpt1", [128, 512], BF16))
        pt2 = _ctx.enter_context(nc.sbuf_tensor("Bpt2", [128, 512], BF16))
        pt3 = _ctx.enter_context(nc.sbuf_tensor("Bpt3", [128, 512], BF16))
        drw = _ctx.enter_context(nc.sbuf_tensor("Bdrw", [1, 512], F32))
        drb = _ctx.enter_context(nc.sbuf_tensor("Bdrb", [64, 512], F32))
        att = _ctx.enter_context(nc.sbuf_tensor("Batt", [128, NCH * 512], BF16))
        g1b = _ctx.enter_context(nc.sbuf_tensor("Bg1b", [128, NCH * 512], BF16))
        xres = _ctx.enter_context(nc.sbuf_tensor("Bxres", [128, NCH * 512], F32))
        psS0 = _ctx.enter_context(nc.psum_tensor("BpsS0", [128, 512], F32))
        psS1 = _ctx.enter_context(nc.psum_tensor("BpsS1", [128, 512], F32))
        psS2 = _ctx.enter_context(nc.psum_tensor("BpsS2", [128, 512], F32))
        psS3 = _ctx.enter_context(nc.psum_tensor("BpsS3", [128, 512], F32))
        psO = _ctx.enter_context(nc.psum_tensor("BpsO", [65, 512], F32))
        psB = _ctx.enter_context(nc.psum_tensor("BpsB", [64, 512], F32))
        ptr0 = _ctx.enter_context(nc.psum_tensor("Bptr0", [128, 512], BF16))
        ptr1 = _ctx.enter_context(nc.psum_tensor("Bptr1", [128, 512], BF16))
        psS = (psS0, psS1, psS2, psS3)
        sxs = (sx0, sx1, sx2, sx3)
        pts = (pt0, pt1, pt2, pt3)
        ptrs = (ptr0, ptr1)
        ktoks = (ktok0, ktok1)
        isbs = (isbA, isbB)
        mks = (mk0, mk1)
        vtv = vtok.ap().rearrange("p (c t) -> p c t", c=32)
        mtv = mtsb.ap().rearrange("p (c t) -> p c t", c=32)
        attv = att.ap().rearrange("p (c t) -> p c t", c=NCH)
        g1v = g1b.ap().rearrange("p (c t) -> p c t", c=NCH)
        xrv = xres.ap().rearrange("p (c t) -> p c t", c=NCH)
        wov = wosb.ap().rearrange("p (c t) -> p c t", c=NCH)

        def ktg(g, cs):
            b0 = 64 * (g // 2)
            return kt2[b0:b0 + 64, (g % 2) * L:][:, cs]

        def qrh(h):
            b0 = 64 * (h // 8)
            return qr2[b0:b0 + 64, (h % 8) * 512:(h % 8) * 512 + 512]

        def qih(h, cs):
            return qi64[0:64, h * 512:(h + 1) * 512][:, cs]

        @sq.step(blk, "gpsimd")
        def _(eng, st):
            for k in range(NCH):
                st.dma(wov[:, k], wo[128 * k:128 * (k + 1), :])

        @sq.step(blk, "gpsimd")
        def _(eng, st):
            return make_ident(nc, eng, identb[:, :], st)

        @sq.step(blk, "vector")
        def _(eng, st):
            eng.memset(onesr[:, :], 1.0)
            return eng.memset(
                vtok.ap().rearrange("p (c g o) -> p c g o", c=32, g=4)[:, :, :, 64:65], 1.0)

        @sq.step(blk, "sync")
        def _(eng, st):
            st.dma(drw[0:1, 0:4], bi_in[0:1, :])

        @sq.step(blk, "tensor")
        def _(eng, st):
            return eng.matmul(psS0[:, 0:4], onesr[:, :],
                              drw[0:1, 0:4], start=True, stop=True)

        @sq.step(blk, "vector")
        def _(eng, st):
            return eng.tensor_copy(out=bib[:, :], in_=psS0[:, 0:4])

        semT = nc.alloc_semaphore("b_semT")
        semS = nc.alloc_semaphore("b_semS")
        semV = nc.alloc_semaphore("b_semV")
        semLQ = nc.alloc_semaphore("b_semLQ")
        semLV = nc.alloc_semaphore("b_semLV")
        semSO = nc.alloc_semaphore("b_semSO")

        NB = n_batches
        val = {}
        vt = vs = vv = vlq = vlv = vso = 0
        for b in range(NB):
            for j in range(32):
                vlv += 32
                val[("kl", b, j)] = vlv
            vlv += 128 * 16
            val[("vt", b)] = vlv
            vlq += 40 * 16
            val[("lq", b)] = vlq
            for j in range(32):
                vt += 1
                val[("ktr", b, j)] = vt
                vv += 1
                val[("ktev", b, j)] = vv
            for qt in range(4):
                for h in range(H_I):
                    for kc in range(NCH):
                        vt += 1
                        val[("ix", b, qt, h, kc)] = vt
                        vs += 1
                        val[("sg", b, qt, h, kc)] = vs
                    vv += 1
                    val[("ia", b, qt, h)] = vv
                if qt % 2 == 1:
                    vv += 1
                    val[("mg", b, qt - 1)] = vv
                    vv += 1
                    val[("mg", b, qt)] = vv
                    for q2 in (qt - 1, qt):
                        for g in range(8):
                            vt += 1
                            val[("mtr", b, q2, g)] = vt
                            vv += 1
                            val[("mev", b, q2, g)] = vv
            for h in range(H):
                for kc in range(4):
                    vt += 1
                    val[("sc", b, h, kc)] = vt
                for kc in range(28):
                    vt += 1
                    val[("pv", b, h, kc)] = vt
                    vt += 1
                    val[("sc", b, h, kc + 4)] = vt
                for kc in range(28, 32):
                    vt += 1
                    val[("pv", b, h, kc)] = vt
                vt += 1
                val[("dbm", b, h)] = vt
                for kc in range(32):
                    vs += 1
                    val[("ex", b, h, kc)] = vs
            for h in range(H):
                vv += 1
                val[("sx", b, h, 0)] = vv
                if h > 0:
                    vv += 1
                    val[("av", b, h - 1)] = vv
                for kc in range(1, 32):
                    vv += 1
                    val[("sx", b, h, kc)] = vv
                vv += 1
                val[("rc", b, h)] = vv
            vv += 1
            val[("gt", b, 0)] = vv
            vv += 1
            val[("av", b, H - 1)] = vv
            for c in range(1, NCH):
                vv += 1
                val[("gt", b, c)] = vv
            for oc in range(NCH):
                vt += 1
                val[("wo", b, oc)] = vt
                vv += 1
                val[("rs", b, oc)] = vv
            for oc in range(NCH):
                vso += 16
                val[("so", b, oc)] = vso

        entry = [(sq.sems[k], v) for k, v in sq.cnt.items() if v > 0]

        def _prelude(eng):
            for s_, v_ in entry:
                eng.wait_ge(s_, v_)

        def kvrow(b, j):
            return 2048 * (j // 4) + 512 * b + 128 * (j % 4)

        @blk.gpsimd
        def _(eng):
            _prelude(eng)
            for b in range(NB):
                for j in range(32):
                    if b > 0 and j < 2:
                        eng.wait_ge(semT, val[("ktr", b - 1, 30 + j)])
                    r0 = kvrow(b, j)
                    kt = ktoks[j % 2]
                    if j >= 2:
                        eng.wait_ge(semT, val[("ktr", b, j - 2)])
                    eng.dma_start(out=kt[:, 0:256],
                                  in_=kv_all[r0:r0 + 128, 0:256]).then_inc(semLV, 16)
                    eng.dma_start(out=kt[:, 256:320],
                                  in_=kv_all[r0:r0 + 128, 512:576]).then_inc(semLV, 16)
                if b > 0:
                    eng.wait_ge(semT, val[("pv", b - 1, H - 1, 31)])
                for j in range(32):
                    r0 = kvrow(b, j)
                    for g in range(4):
                        eng.dma_start(
                            out=vtv[:, j][:, 65 * g:65 * g + 64],
                            in_=kv_all[r0:r0 + 128, 256 + 64 * g:256 + 64 * (g + 1)],
                        ).then_inc(semLV, 16)

        @blk.sync
        def _(eng):
            _prelude(eng)
            for b in range(NB):
                if b > 0:
                    eng.wait_ge(semT, val[("wo", b - 1, 7)])
                    eng.wait_ge(semV, val[("gt", b - 1, 7)])
                for h in range(H):
                    eng.dma_start(out=qrh(h),
                                  in_=qrT[64 * h:64 * (h + 1),
                                          512 * b:512 * (b + 1)]).then_inc(semLQ, 16)
                for h in range(H_I):
                    eng.dma_start(out=qi64[0:64, h * 512:(h + 1) * 512],
                                  in_=qiT[64 * h:64 * (h + 1),
                                          512 * b:512 * (b + 1)]).then_inc(semLQ, 16)
                for q4 in range(4):
                    eng.dma_start(out=wib[:, 4 * q4:4 * (q4 + 1)],
                                  in_=wI_tok[512 * b + 128 * q4:
                                             512 * b + 128 * (q4 + 1), :]).then_inc(semLQ, 16)
                for c in range(NCH):
                    eng.dma_start(out=g1v[:, c],
                                  in_=g1T[128 * c:128 * (c + 1),
                                          512 * b:512 * (b + 1)]).then_inc(semLQ, 16)
                for c in range(NCH):
                    eng.dma_start(out=xrv[:, c],
                                  in_=xT[128 * c:128 * (c + 1),
                                         512 * b:512 * (b + 1)]).then_inc(semLQ, 16)
                for oc in range(NCH):
                    eng.wait_ge(semV, val[("rs", b, oc)])
                    eng.dma_start(out=xAT[128 * oc:128 * (oc + 1),
                                          512 * b:512 * (b + 1)],
                                  in_=xrv[:, oc]).then_inc(semSO, 16)

        @blk.tensor
        def _(eng):
            _prelude(eng)
            for b in range(NB):
                for j in range(32):
                    eng.wait_ge(semLV, val[("kl", b, j)])
                    w = j
                    if w >= 2:
                        eng.wait_ge(semV, val[("ktev", b, w - 2)])
                    elif b > 0:
                        eng.wait_ge(semV, val[("mev", b - 1, 3, 6 + w)])
                    kt = ktoks[j % 2]
                    pt = ptrs[w % 2]
                    last = None
                    for fc in range(3):
                        last = eng.transpose(pt[:, 128 * fc:128 * (fc + 1)],
                                             kt[:, 128 * fc:128 * (fc + 1)],
                                             identb[:, :])
                    last.then_inc(semT, 1)
                for qt in range(4):
                    for h in range(H_I):
                        for kc in range(NCH):
                            j2 = (qt * 4 + h) * 8 + kc
                            if j2 == 0:
                                eng.wait_ge(semV, val[("ktev", b, 31)])
                                eng.wait_ge(semLQ, val[("lq", b)])
                                if b > 0:
                                    eng.wait_ge(semV, val[("rs", b - 1, 7)])
                            if j2 >= 4:
                                qp, hp, kp = (j2 - 4) // 32, ((j2 - 4) // 8) % 4, (j2 - 4) % 8
                                eng.wait_ge(semS, val[("sg", b, qp, hp, kp)])
                            eng.matmul(psS[j2 % 4][:, :],
                                       qih(h, slice(128 * qt, 128 * (qt + 1))),
                                       ki64[0:64, 512 * kc:512 * (kc + 1)],
                                       start=True, stop=True).then_inc(semT, 1)
                    if qt % 2 == 0:
                        continue
                    for q2 in (qt - 1, qt):
                        for g in range(8):
                            eng.wait_ge(semV, val[("mg", b, q2)])
                            w = 32 + q2 * 8 + g
                            wp = w - 2
                            if wp >= 32:
                                qp, gp = (wp - 32) // 8, (wp - 32) % 8
                                eng.wait_ge(semV, val[("mev", b, qp, gp)])
                            else:
                                eng.wait_ge(semV, val[("ktev", b, wp)])
                            pt = ptrs[w % 2]
                            mk = mks[q2 % 2]
                            last = None
                            for u in range(4):
                                kc = 4 * g + u
                                last = eng.transpose(pt[:, 128 * u:128 * (u + 1)],
                                                     mk[:, 128 * kc:128 * (kc + 1)],
                                                     identb[:, :])
                            last.then_inc(semT, 1)
                for h in range(H):
                    g = h // 4

                    def sc(kc, h=h, g=g, b=b):
                        kcg = 32 * h + kc
                        if kcg >= 4:
                            hp, kp = (kcg - 4) // 32, (kcg - 4) % 32
                            eng.wait_ge(semV, val[("sx", b, hp, kp)])
                        elif kcg == 0:
                            eng.wait_ge(semS, val[("sg", b, 3, 3, 7)])
                        eng.matmul(psS[kcg % 4][:, :],
                                   ktg(g, slice(128 * kc, 128 * (kc + 1))),
                                   qrh(h), start=True, stop=True).then_inc(semT, 1)

                    def pv(kc, h=h, g=g, b=b):
                        if h == 0 and kc == 0:
                            eng.wait_ge(semLV, val[("vt", b)])
                        if kc == 0 and h > 0:
                            eng.wait_ge(semV, val[("av", b, h - 1)])
                        eng.wait_ge(semS, val[("ex", b, h, kc)])
                        eng.matmul(psO[0:65, :],
                                   vtv[:, kc][:, 65 * g:65 * (g + 1)],
                                   pts[(32 * h + kc) % 4][:, :],
                                   start=(kc == 0), stop=(kc == 31)).then_inc(semT, 1)

                    for kc in range(4):
                        sc(kc)
                    for kc in range(28):
                        pv(kc)
                        sc(kc + 4)
                    for kc in range(28, 32):
                        pv(kc)
                    eng.wait_ge(semV, val[("rc", b, h)])
                    eng.matmul(psB[0:64, :], onesr[0:1, 0:64], drw[0:1, :],
                               start=True, stop=True).then_inc(semT, 1)
                for oc in range(NCH):
                    if oc == 0:
                        eng.wait_ge(semV, val[("gt", b, 7)])
                    if oc >= 2:
                        eng.wait_ge(semV, val[("rs", b, oc - 2)])
                    last = None
                    for k in range(NCH):
                        last = eng.matmul(psS[oc % 2][:, :],
                                          wov[:, k][:, 128 * oc:128 * (oc + 1)],
                                          attv[:, k], start=(k == 0),
                                          stop=(k == NCH - 1))
                    last.then_inc(semT, 1)

        @blk.scalar
        def _(eng):
            _prelude(eng)
            for b in range(NB):
                for qt in range(4):
                    for h in range(H_I):
                        for kc in range(NCH):
                            j2 = (qt * 4 + h) * 8 + kc
                            if kc == 0:
                                if qt == 0 and h == 0:
                                    if b > 0:
                                        eng.wait_ge(semV, val[("ia", b - 1, 3, 3)])
                                else:
                                    qp, hp = (qt, h - 1) if h > 0 else (qt - 1, 3)
                                    eng.wait_ge(semV, val[("ia", b, qp, hp)])
                            eng.wait_ge(semT, val[("ix", b, qt, h, kc)])
                            eng.activation(out=scr[:, 512 * kc:512 * (kc + 1)],
                                           in_=psS[j2 % 4][:, :], func=AF.Sigmoid,
                                           bias=bib[:, h:h + 1]).then_inc(semS, 1)
                for h in range(H):
                    for kc in range(32):
                        kcg = 32 * h + kc
                        eng.wait_ge(semV, val[("sx", b, h, kc)])
                        if kcg >= 4:
                            hp, kp = (kcg - 4) // 32, (kcg - 4) % 32
                            eng.wait_ge(semT, val[("pv", b, hp, kp)])
                        eng.activation(out=pts[kcg % 4][:, :], in_=sxs[kcg % 4][:, :],
                                       func=AF.Exp).then_inc(semS, 1)

        @blk.vector
        def _(eng):
            _prelude(eng)
            vloc = {"v": 0}

            def vinc(inst):
                vloc["v"] += 1
                inst.then_inc(semV, 1)

            for b in range(NB):
                for j in range(32):
                    eng.wait_ge(semT, val[("ktr", b, j)])
                    pt = ptrs[j % 2]
                    sl = slice(128 * j, 128 * (j + 1))
                    eng.tensor_copy(out=ktg(0, sl), in_=pt[0:64, 0:128])
                    eng.tensor_copy(out=ktg(1, sl), in_=pt[64:128, 0:128])
                    eng.tensor_copy(out=ktg(2, sl), in_=pt[0:64, 128:256])
                    eng.tensor_copy(out=ktg(3, sl), in_=pt[64:128, 128:256])
                    vinc(eng.tensor_copy(out=ki64[0:64, sl], in_=pt[0:64, 256:384]))
                for qt in range(4):
                    for h in range(H_I):
                        eng.wait_ge(semS, val[("sg", b, qt, h, 7)])
                        if h == 0:
                            inst = eng.tensor_scalar(
                                out=isbs[qt % 2][:, :], in0=scr[:, :],
                                scalar1=wib[:, 4 * qt + h:4 * qt + h + 1],
                                scalar2=None, op0=ALU.mult)
                        else:
                            inst = eng.scalar_tensor_tensor(
                                out=isbs[qt % 2][:, :], in0=scr[:, :],
                                scalar=wib[:, 4 * qt + h:4 * qt + h + 1],
                                in1=isbs[qt % 2][:, :], op0=ALU.mult, op1=ALU.add)
                        vinc(inst)
                        eng.wait_ge(semV, vloc["v"])
                    if qt % 2 == 1:
                        eng.memset(bst[:, 0:1], 0.0)
                        eng.memset(bst[:, 4:5], 0.0)
                        eng.memset(bst[:, 1:2], 4.0)
                        eng.memset(bst[:, 5:6], 4.0)
                        for it in range(bisect_iters):
                            for s in (0, 1):
                                o = 4 * s
                                eng.tensor_scalar(out=bst[:, o + 1:o + 2],
                                                  in0=bst[:, o + 1:o + 2],
                                                  scalar1=0.5, scalar2=None,
                                                  op0=ALU.mult)
                            for s in (0, 1):
                                o = 4 * s
                                eng.tensor_tensor(out=bst[:, o + 2:o + 3],
                                                  in0=bst[:, o:o + 1],
                                                  in1=bst[:, o + 1:o + 2], op=ALU.add)
                            for s in (0, 1):
                                o = 4 * s
                                eng.tensor_scalar(out=mks[s][:, :],
                                                  in0=isbs[s][:, :],
                                                  scalar1=bst[:, o + 2:o + 3],
                                                  scalar2=0.0, op0=ALU.is_gt,
                                                  op1=ALU.add,
                                                  accum_out=bst[:, o + 3:o + 4])
                            for s in (0, 1):
                                o = 4 * s
                                eng.tensor_scalar(out=bst[:, o + 3:o + 4],
                                                  in0=bst[:, o + 3:o + 4],
                                                  scalar1=float(K_SEL), scalar2=None,
                                                  op0=ALU.is_ge)
                            for s in (0, 1):
                                o = 4 * s
                                eng.scalar_tensor_tensor(out=bst[:, o:o + 1],
                                                         in0=bst[:, o + 1:o + 2],
                                                         scalar=bst[:, o + 3:o + 4],
                                                         in1=bst[:, o:o + 1],
                                                         op0=ALU.mult, op1=ALU.add)
                        for q2 in (qt - 1, qt):
                            o = 4 * (q2 % 2)
                            vinc(eng.tensor_scalar(out=mks[q2 % 2][:, :],
                                                   in0=isbs[q2 % 2][:, :],
                                                   scalar1=bst[:, o:o + 1],
                                                   scalar2=NEG_BIG, op0=ALU.is_le,
                                                   op1=ALU.mult))
                        for q2 in (qt - 1, qt):
                            for g in range(8):
                                eng.wait_ge(semT, val[("mtr", b, q2, g)])
                                pt = ptrs[(32 + q2 * 8 + g) % 2]
                                last = None
                                for u in range(4):
                                    kc = 4 * g + u
                                    last = eng.tensor_copy(
                                        out=mtv[:, kc][:, 128 * q2:128 * (q2 + 1)],
                                        in_=pt[:, 128 * u:128 * (u + 1)])
                                vinc(last)
                for h in range(H):
                    def sx(kc, h=h, b=b):
                        kcg = 32 * h + kc
                        eng.wait_ge(semT, val[("sc", b, h, kc)])
                        if kcg >= 4:
                            hp, kp = (kcg - 4) // 32, (kcg - 4) % 32
                            eng.wait_ge(semS, val[("ex", b, hp, kp)])
                        vinc(eng.scalar_tensor_tensor(
                            out=sxs[kcg % 4][:, :], in0=psS[kcg % 4][:, :],
                            scalar=SM_SCALE, in1=mtv[:, kc], op0=ALU.mult,
                            op1=ALU.add))

                    sx(0)
                    if h > 0:
                        dst = attv[:, (h - 1) // 2][64 * ((h - 1) % 2):
                                                    64 * ((h - 1) % 2) + 64, :]
                        vinc(eng.tensor_tensor(out=dst, in0=psO[0:64, :],
                                               in1=drb[0:64, :], op=ALU.mult))
                    for kc in range(1, 32):
                        sx(kc)
                    eng.wait_ge(semT, val[("pv", b, h, 31)])
                    vinc(eng.reciprocal(out=drw[0:1, :], in_=psO[64:65, :]))
                    eng.wait_ge(semT, val[("dbm", b, h)])
                    eng.tensor_copy(out=drb[0:64, :], in_=psB[0:64, :])
                eng.wait_ge(semLQ, val[("lq", b)])
                vinc(eng.tensor_tensor(out=attv[:, 0], in0=attv[:, 0],
                                       in1=g1v[:, 0], op=ALU.mult))
                dst = attv[:, 7][64:128, :]
                vinc(eng.tensor_tensor(out=dst, in0=psO[0:64, :],
                                       in1=drb[0:64, :], op=ALU.mult))
                for c in range(1, NCH):
                    vinc(eng.tensor_tensor(out=attv[:, c], in0=attv[:, c],
                                           in1=g1v[:, c], op=ALU.mult))
                for oc in range(NCH):
                    eng.wait_ge(semT, val[("wo", b, oc)])
                    vinc(eng.tensor_tensor(out=xrv[:, oc], in0=xrv[:, oc],
                                           in1=psS[oc % 2][:, :], op=ALU.add))

        for key, sem, v in (("bT", semT, vt), ("bS", semS, vs), ("bV", semV, vv),
                            ("bLQ", semLQ, vlq), ("bLV", semLV, vlv),
                            ("bSO", semSO, vso)):
            sq.sems[key] = sem
            sq.cnt[key] = v



def emit_C(nc, blk, sq, io, n_experts=16):
    xnT = io["xnT"]
    xAT = io["xAT"]
    gwT = io["gwT"]
    ebias = io["ebias"]
    wgT = io["wgT"]
    wuT = io["wuT"]
    wdT = io["wdT"]
    sgT = io["sgT"]
    suT = io["suT"]
    sdT = io["sdT"]
    x1T = io["x1T"]
    xn1T = io["xn1T"]
    xn1F = io["xn1F"]
    xmax = io["xmax_in"]

    ctrD = io["ctrD"]

    from contextlib import ExitStack
    with ExitStack() as _ctx:
        x1f = _ctx.enter_context(nc.sbuf_tensor("Cx1f", [128, NCH * TPC], F32))
        gwsb = _ctx.enter_context(nc.sbuf_tensor("Cgwsb", [128, NCH * 16], BF16))
        ebb = _ctx.enter_context(nc.sbuf_tensor("Cebb", [128, 16], F32))
        cmb = _ctx.enter_context(nc.sbuf_tensor("Ccmb", [128, 16 * 16], F32))
        ctst = _ctx.enter_context(nc.sbuf_tensor("Cctst", [16, 128], F32))
        r16 = _ctx.enter_context(nc.sbuf_tensor("Cr16", [128, 16], F32))
        r16b = _ctx.enter_context(nc.sbuf_tensor("Cr16b", [128, 16], F32))
        r16c = _ctx.enter_context(nc.sbuf_tensor("Cr16c", [128, 16], F32))
        r8 = _ctx.enter_context(nc.sbuf_tensor("Cr8", [128, 8], F32))
        rcol = _ctx.enter_context(nc.sbuf_tensor("Crcol", [128, 2], F32))
        onesr = _ctx.enter_context(nc.sbuf_tensor("Conesr", [1, 128], F32))
        onesf = _ctx.enter_context(nc.sbuf_tensor("Conesf", [128, 1], F32))
        identf = _ctx.enter_context(nc.sbuf_tensor("Cidentf", [128, 128], F32))
        epsc = _ctx.enter_context(nc.sbuf_tensor("Cepsc", [1, 1], F32))
        x2 = _ctx.enter_context(nc.sbuf_tensor("Cx2", [128, 512], F32))
        srow = _ctx.enter_context(nc.sbuf_tensor("Csrow", [1, 3 * 512], F32))
        _sx = ExitStack()
        xnsb = _sx.enter_context(nc.sbuf_tensor("Cxnsb", [128, NCH * TPC], BF16))
        wge = _sx.enter_context(nc.sbuf_tensor("Cwge", [128, NCH * 1024], BF16))
        wue = _sx.enter_context(nc.sbuf_tensor("Cwue", [128, NCH * 1024], BF16))
        wde = _sx.enter_context(nc.sbuf_tensor("Cwde", [128, NCH * 1024], BF16))
        crow = _sx.enter_context(nc.sbuf_tensor("Ccrow", [1, TPC], F32))
        cbff = _sx.enter_context(nc.sbuf_tensor("Ccbff", [128, TPC], F32))
        hsl0 = _sx.enter_context(nc.sbuf_tensor("Chsl0", [128, 512], BF16))
        hsl1 = _sx.enter_context(nc.sbuf_tensor("Chsl1", [128, 512], BF16))
        ups0 = _sx.enter_context(nc.sbuf_tensor("Cups0", [128, 512], BF16))
        ups1 = _sx.enter_context(nc.sbuf_tensor("Cups1", [128, 512], BF16))
        hsb = _sx.enter_context(nc.sbuf_tensor("Chsb", [128, NCH * 4 * 512], BF16))
        xnv = xnsb.ap().rearrange("p (c t) -> p c t", c=NCH)
        xf = x1f.ap().rearrange("p (c t) -> p c t", c=NCH)
        wgv = wge.ap().rearrange("p (c t) -> p c t", c=NCH)
        wuv = wue.ap().rearrange("p (c t) -> p c t", c=NCH)
        wdv = wde.ap().rearrange("p (c t) -> p c t", c=NCH)
        gwv = gwsb.ap().rearrange("p (c t) -> p c t", c=NCH)
        hsls = (hsl0, hsl1)
        upss = (ups0, ups1)

        def hchunk(k, tt):
            return hsb[:, (k * 4 + tt) * 512:(k * 4 + tt) * 512 + 512]

        @sq.step(blk, "gpsimd")
        def _(eng, st):
            for c in range(NCH):
                st.dma(xnv[:, c], xnT[128 * c:128 * (c + 1), :])
                st.dma(gwv[:, c], gwT[128 * c:128 * (c + 1), :])
            st.dma(srow[0:1, 0:16], ebias[0:1, :])

        @sq.step(blk, "gpsimd")
        def _(eng, st):
            return make_ident(nc, eng, identf[:, :], st)

        @sq.step(blk, "vector")
        def _(eng, st):
            eng.memset(onesr[:, :], 1.0)
            eng.memset(onesf[:, :], 1.0)
            return eng.memset(epsc[:, :], LN_EPS)

        _pr = ExitStack()
        psR = _pr.enter_context(nc.psum_tensor("CpsR", [128, 16], F32))
        psT = _pr.enter_context(nc.psum_tensor("CpsT", [128, 128], F32))

        @sq.step(blk, "tensor")
        def _(eng, st):
            return eng.matmul(psR[:, :], onesr[:, :], srow[0:1, 0:16],
                              start=True, stop=True)

        @sq.step(blk, "vector")
        def _(eng, st):
            return eng.tensor_copy(out=ebb[:, :], in_=psR[:, :])

        for qt in range(16):
            @sq.step(blk, "tensor")
            def _(eng, st, qt=qt):
                last = None
                for k in range(NCH):
                    last = eng.matmul(psR[:, :],
                                      xnv[:, k][:, 128 * qt:128 * (qt + 1)],
                                      gwv[:, k], start=(k == 0), stop=(k == NCH - 1))
                return last

            @sq.step(blk, "scalar")
            def _(eng, st):
                return eng.activation(out=r16[:, :], in_=psR[:, :], func=AF.Sigmoid)

            @sq.step(blk, "vector")
            def _(eng, st, qt=qt):
                st.chain(eng.tensor_tensor(out=r16b[:, :], in0=r16[:, :],
                                           in1=ebb[:, :], op=ALU.add))
                st.chain(eng.max(out=r8[:, :], in_=r16b[:, :]))
                st.chain(eng.match_replace(out=r16c[:, :], in_to_replace=r8[:, :],
                                           in_values=r16b[:, :], imm_value=-1e30))
                st.chain(eng.tensor_scalar(out=r16c[:, :], in0=r16c[:, :],
                                           scalar1=-1e29, scalar2=None, op0=ALU.is_lt))
                st.chain(eng.tensor_tensor(out=r16c[:, :], in0=r16c[:, :],
                                           in1=r16[:, :], op=ALU.mult))
                st.chain(eng.reduce_sum(out=rcol[:, 0:1], in_=r16c[:, :],
                                        axis=mybir.AxisListType.X))
                st.chain(eng.reciprocal(out=rcol[:, 1:2], in_=rcol[:, 0:1]))
                return eng.tensor_scalar(out=cmb[:, 16 * qt:16 * (qt + 1)],
                                         in0=r16c[:, :], scalar1=rcol[:, 1:2],
                                         scalar2=2.826, op0=ALU.mult, op1=ALU.mult)

            @sq.step(blk, "tensor")
            def _(eng, st, qt=qt):
                return eng.transpose(psT[0:16, :].bitcast(F32),
                                     cmb[:, 16 * qt:16 * (qt + 1)], identf[:, :])

            @sq.step(blk, "vector")
            def _(eng, st, qt=qt):
                return eng.tensor_copy(out=ctst[0:16, :],
                                       in_=psT[0:16, 0:128].bitcast(F32))

            @sq.step(blk, "sync")
            def _(eng, st, qt=qt):
                st.dma(ctrD[0:16, 128 * qt:128 * (qt + 1)], ctst[0:16, :])

        _pr.close()

        _px = ExitStack()
        psGs = tuple(_px.enter_context(nc.psum_tensor(f"CpsG{i}", [128, 512], F32))
                     for i in range(2))
        psUs = tuple(_px.enter_context(nc.psum_tensor(f"CpsU{i}", [128, 512], F32))
                     for i in range(2))
        psDs = tuple(_px.enter_context(nc.psum_tensor(f"CpsD{i}", [128, 512], F32))
                     for i in range(2))
        psBs = tuple(_px.enter_context(nc.psum_tensor(f"CpsB{i}", [128, 512], F32))
                     for i in range(2))

        semT = nc.alloc_semaphore("c_semT")
        semS = nc.alloc_semaphore("c_semS")
        semV = nc.alloc_semaphore("c_semV")
        semWG = nc.alloc_semaphore("c_semWG")
        semWD = nc.alloc_semaphore("c_semWD")
        semCR = nc.alloc_semaphore("c_semCR")
        semXA = nc.alloc_semaphore("c_semXA")

        NE = n_experts + 1
        slocal = [(ic, tt) for ic in range(NCH) for tt in range(NTT)]
        seq_tsv = [(e, ic, tt) for e in range(NE) for (ic, tt) in slocal]
        seq_d = [(e, dc, tt) for e in range(NE) for (dc, tt) in slocal]

        valT, valS, valV = {}, {}, {}
        valWG, valWD, valCR = {}, {}, {}
        vt = vs = vv = vwg = vwd = vcr = 0
        for e in range(NE):
            sh = e == n_experts
            for h in range(2):
                vwg += 16 * 16
                valWG[(e, h)] = vwg
                vwd += 8 * 16
                valWD[(e, h)] = vwd
            if not sh:
                vcr += 16
                valCR[e] = vcr
                for tt in range(NTT):
                    vt += 1
                    valT[(e, "CB", tt)] = vt
                    vv += 1
                    valV[(e, "CBv", tt)] = vv
            for (ic, tt) in slocal:
                vt += 1
                valT[(e, "T", ic, tt)] = vt
                vs += 1
                valS[(e, ic, tt)] = vs
            if sh:
                for (ic, tt) in slocal:
                    vv += 1
                    valV[(e, "M2", ic, tt)] = vv
            else:
                vv += 1
                for i in range(1, 32):
                    vv += 1
                    vv += 1
                    valV[(e, "M2") + slocal[i - 1]] = vv
                vv += 1
                valV[(e, "M2") + slocal[31]] = vv
            for (dc, tt) in slocal:
                vt += 1
                valT[(e, "TD", dc, tt)] = vt
                vv += 1
                valV[(e, "VD", dc, tt)] = vv

        entry = [(sq.sems[k], v) for k, v in sq.cnt.items() if v > 0]

        def _prelude(eng):
            for s_, v_ in entry:
                eng.wait_ge(s_, v_)

        @blk.gpsimd
        def _(eng):
            _prelude(eng)
            for e in range(NE):
                sh = e == n_experts
                for h in range(2):
                    if e > 0:
                        eng.wait_ge(semT, valT[(e - 1, "T", 4 * h + 3, 3)])
                    cs = slice(512 * h, 512 * (h + 1))
                    for k in range(NCH):
                        rs = slice(128 * k, 128 * (k + 1))
                        gsrc = sgT[rs, cs] if sh else wgT[e, rs, cs]
                        usrc = suT[rs, cs] if sh else wuT[e, rs, cs]
                        eng.dma_start(out=wgv[:, k][:, cs], in_=gsrc).then_inc(semWG, 16)
                        eng.dma_start(out=wuv[:, k][:, cs], in_=usrc).then_inc(semWG, 16)
                for h in range(2):
                    if e > 0:
                        eng.wait_ge(semT, valT[(e - 1, "TD", 4 * h + 3, 3)])
                    cs = slice(512 * h, 512 * (h + 1))
                    for k in range(NCH):
                        rs = slice(128 * k, 128 * (k + 1))
                        dsrc = sdT[rs, cs] if sh else wdT[e, rs, cs]
                        eng.dma_start(out=wdv[:, k][:, cs], in_=dsrc).then_inc(semWD, 16)

        @blk.sync
        def _(eng):
            _prelude(eng)
            for c in range(NCH):
                eng.dma_start(out=xf[:, c],
                              in_=xAT[128 * c:128 * (c + 1), :]).then_inc(semXA, 16)
            for e in range(NE):
                sh = e == n_experts
                if not sh:
                    if e > 0:
                        eng.wait_ge(semT, valT[(e - 1, "CB", 3)])
                    eng.dma_start(out=crow[0:1, :],
                                  in_=ctrD[e:e + 1, :]).then_inc(semCR, 16)

        @blk.tensor
        def _(eng):
            _prelude(eng)
            for e in range(NE):
                sh = e == n_experts
                if not sh:
                    eng.wait_ge(semCR, valCR[e])
                    for tt in range(NTT):
                        if tt >= 2:
                            eng.wait_ge(semV, valV[(e, "CBv", tt - 2)])
                        elif e > 0:
                            eng.wait_ge(semV, valV[(e - 1, "CBv", tt + 2)])
                        eng.matmul(psBs[tt % 2][:, :], onesr[:, :],
                                   crow[0:1, 512 * tt:512 * (tt + 1)],
                                   start=True, stop=True).then_inc(semT, 1)
                for (ic, tt) in slocal:
                    if ic % 4 == 0 and tt == 0:
                        eng.wait_ge(semWG, valWG[(e, ic // 4)])
                    g = e * 32 + slocal.index((ic, tt))
                    if g >= 2:
                        ep, icp, ttp = seq_tsv[g - 2]
                        eng.wait_ge(semV, valV[(ep, "M2", icp, ttp)])
                    p = g % 2
                    for k in range(NCH):
                        eng.matmul(psGs[p][:, :],
                                   wgv[:, k][:, 128 * ic:128 * (ic + 1)],
                                   xnv[:, k][:, 512 * tt:512 * (tt + 1)],
                                   start=(k == 0), stop=(k == NCH - 1))
                    last = None
                    for k in range(NCH):
                        last = eng.matmul(psUs[p][:, :],
                                          wuv[:, k][:, 128 * ic:128 * (ic + 1)],
                                          xnv[:, k][:, 512 * tt:512 * (tt + 1)],
                                          start=(k == 0), stop=(k == NCH - 1))
                    last.then_inc(semT, 1)
                for (dc, tt) in slocal:
                    if dc % 4 == 0 and tt == 0:
                        eng.wait_ge(semWD, valWD[(e, dc // 4)])
                    u = e * 32 + slocal.index((dc, tt))
                    w = valV[(e, "M2", 7, tt)]
                    if u >= 2:
                        ep, dcp, ttp = seq_d[u - 2]
                        w = max(w, valV[(ep, "VD", dcp, ttp)])
                    eng.wait_ge(semV, w)
                    q = u % 2
                    last = None
                    for k in range(NCH):
                        last = eng.matmul(psDs[q][:, :],
                                          wdv[:, k][:, 128 * dc:128 * (dc + 1)],
                                          hchunk(k, tt), start=(k == 0),
                                          stop=(k == NCH - 1))
                    last.then_inc(semT, 1)

        @blk.scalar
        def _(eng):
            _prelude(eng)
            for e in range(NE):
                for i, (ic, tt) in enumerate(slocal):
                    g = e * 32 + i
                    p = g % 2
                    eng.wait_ge(semT, valT[(e, "T", ic, tt)])
                    eng.activation(out=hsls[p][:, :], in_=psGs[p][:, :],
                                   func=AF.Silu).then_inc(semS, 1)

        @blk.vector
        def _(eng):
            _prelude(eng)
            for e in range(NE):
                sh = e == n_experts
                if not sh:
                    for tt in range(NTT):
                        eng.wait_ge(semT, valT[(e, "CB", tt)])
                        eng.tensor_copy(
                            out=cbff[:, 512 * tt:512 * (tt + 1)],
                            in_=psBs[tt % 2][:, :]).then_inc(semV, 1)

                    def m1(i, e=e):
                        ic, tt = slocal[i]
                        p = (e * 32 + i) % 2
                        eng.wait_ge(semT, valT[(e, "T", ic, tt)])
                        eng.tensor_tensor(
                            out=upss[p][:, :], in0=psUs[p][:, :],
                            in1=cbff[:, 512 * tt:512 * (tt + 1)],
                            op=ALU.mult).then_inc(semV, 1)

                    def m2(i, e=e):
                        ic, tt = slocal[i]
                        p = (e * 32 + i) % 2
                        eng.wait_ge(semS, valS[(e, ic, tt)])
                        eng.tensor_tensor(
                            out=hchunk(ic, tt), in0=hsls[p][:, :],
                            in1=upss[p][:, :], op=ALU.mult).then_inc(semV, 1)

                    m1(0)
                    for i in range(1, 32):
                        m1(i)
                        m2(i - 1)
                    m2(31)
                else:
                    for i, (ic, tt) in enumerate(slocal):
                        p = (e * 32 + i) % 2
                        eng.wait_ge(semS, valS[(e, ic, tt)])
                        eng.tensor_tensor(
                            out=hchunk(ic, tt), in0=hsls[p][:, :],
                            in1=psUs[p][:, :], op=ALU.mult).then_inc(semV, 1)
                for i, (dc, tt) in enumerate(slocal):
                    u = e * 32 + i
                    q = u % 2
                    if u == 0:
                        eng.wait_ge(semXA, 128)
                    eng.wait_ge(semT, valT[(e, "TD", dc, tt)])
                    dst = xf[:, dc][:, 512 * tt:512 * (tt + 1)]
                    eng.tensor_tensor(out=dst, in0=dst, in1=psDs[q][:, :],
                                      op=ALU.add).then_inc(semV, 1)

        for key, sem, v in (("cT", semT, vt), ("cS", semS, vs), ("cV", semV, vv),
                            ("cWG", semWG, vwg), ("cWD", semWD, vwd),
                            ("cCR", semCR, vcr), ("cXA", semXA, 128)):
            sq.sems[key] = sem
            sq.cnt[key] = v
        _px.close()
        _sx.close()

        _pl = ExitStack()
        psG = _pl.enter_context(nc.psum_tensor("CpsLg", [128, 512], F32))
        psU = _pl.enter_context(nc.psum_tensor("CpsLu", [128, 512], F32))
        psT2 = _pl.enter_context(nc.psum_tensor("CpsLt", [128, 128], F32))
        prow = _pl.enter_context(nc.psum_tensor("CpsLr", [1, 1024], F32))
        hslb = _pl.enter_context(nc.sbuf_tensor("Chslb", [128, 512], BF16))
        mb = _pl.enter_context(nc.sbuf_tensor("Cmb", [128, 512], F32))
        rb = _pl.enter_context(nc.sbuf_tensor("Crb", [128, 512], F32))
        mxc = _pl.enter_context(nc.sbuf_tensor("Cmxc", [128, 2], F32))
        mrow = _pl.enter_context(nc.sbuf_tensor("Cmrow", [1, 128], F32))

        @sq.step(blk, "vector")
        def _(eng, st):
            return eng.memset(mxc[:, 0:1], 0.0)

        for tt in range(NTT):
            def x1v_(c, tt=tt):
                return xf[:, c][:, 512 * tt:512 * (tt + 1)]

            @sq.step(blk, "sync")
            def _(eng, st, tt=tt):
                for c in range(NCH):
                    st.dma(x1T[128 * c:128 * (c + 1), 512 * tt:512 * (tt + 1)],
                           x1v_(c))

            for c in range(NCH):
                @sq.step(blk, "vector")
                def _(eng, st, c=c):
                    return eng.tensor_tensor(out=x2[:, :], in0=x1v_(c),
                                             in1=x1v_(c), op=ALU.mult)

                @sq.step(blk, "tensor")
                def _(eng, st, c=c):
                    eng.matmul(prow[:, 0:512], onesf[:, :], x1v_(c),
                               start=(c == 0), stop=(c == NCH - 1))
                    return eng.matmul(prow[:, 512:1024], onesf[:, :], x2[:, :],
                                      start=(c == 0), stop=(c == NCH - 1))

            @sq.step(blk, "vector")
            def _(eng, st):
                eng.tensor_scalar(out=srow[:, 0:512], in0=prow[:, 0:512],
                                  scalar1=1.0 / C, scalar2=None, op0=ALU.mult)
                st.chain(eng.tensor_scalar(out=srow[:, 512:1024], in0=prow[:, 512:1024],
                                           scalar1=1.0 / C, scalar2=None, op0=ALU.mult))
                st.chain(eng.tensor_tensor(out=srow[:, 1024:1536], in0=srow[:, 0:512],
                                           in1=srow[:, 0:512], op=ALU.mult))
                return eng.tensor_tensor(out=srow[:, 512:1024], in0=srow[:, 512:1024],
                                         in1=srow[:, 1024:1536], op=ALU.subtract)

            @sq.step(blk, "scalar")
            def _(eng, st):
                return eng.activation(out=srow[:, 1024:1536], in_=srow[:, 512:1024],
                                      func=AF.Sqrt, bias=epsc[0:1, 0:1])

            @sq.step(blk, "vector")
            def _(eng, st):
                return eng.reciprocal(out=srow[:, 512:1024], in_=srow[:, 1024:1536])

            @sq.step(blk, "tensor")
            def _(eng, st):
                eng.matmul(psG[:, :], onesr[:, :], srow[:, 0:512],
                           start=True, stop=True)
                return eng.matmul(psU[:, :], onesr[:, :], srow[:, 512:1024],
                                  start=True, stop=True)

            @sq.step(blk, "vector")
            def _(eng, st):
                eng.tensor_copy(out=mb[:, :], in_=psG[:, :])
                return eng.tensor_copy(out=rb[:, :], in_=psU[:, :])

            for c in range(NCH):
                @sq.step(blk, "vector")
                def _(eng, st, c=c):
                    st.chain(eng.tensor_tensor(out=x2[:, :], in0=x1v_(c),
                                               in1=mb[:, :], op=ALU.subtract))
                    st.chain(eng.tensor_tensor(out=x2[:, :], in0=x2[:, :],
                                               in1=rb[:, :], op=ALU.mult))
                    st.chain(eng.tensor_copy(out=hslb[:, :], in_=x2[:, :]))
                    st.chain(eng.reduce_max(out=mxc[:, 1:2], in_=x2[:, :],
                                            axis=mybir.AxisListType.X,
                                            apply_absolute_value=True))
                    return eng.tensor_tensor(out=mxc[:, 0:1], in0=mxc[:, 0:1],
                                             in1=mxc[:, 1:2], op=ALU.max)

                @sq.step(blk, "sync")
                def _(eng, st, c=c, tt=tt):
                    st.dma(xn1T[128 * c:128 * (c + 1), 512 * tt:512 * (tt + 1)],
                           hslb[:, :])
                    st.dma(xn1F[128 * c:128 * (c + 1), 512 * tt:512 * (tt + 1)],
                           x2[:, :])

        @sq.step(blk, "tensor")
        def _(eng, st):
            return eng.transpose(psT2[0:1, 0:128].bitcast(F32), mxc[:, 0:1],
                                 identf[:, :])

        @sq.step(blk, "vector")
        def _(eng, st):
            st.chain(eng.tensor_copy(out=mrow[0:1, :], in_=psT2[0:1, 0:128].bitcast(F32)))
            return eng.reduce_max(out=srow[0:1, 0:1], in_=mrow[0:1, :],
                                  axis=mybir.AxisListType.X)

        @sq.step(blk, "sync")
        def _(eng, st):
            st.dma(xmax[0:1, 0:1], srow[0:1, 0:1])
        _pl.close()



def emit_D(nc, blk, sq, io):
    xn1T = io["xn1T"]
    x1T = io["x1T"]
    wkv = io["wkv"]
    bkv = io["bkv"]
    wq1 = io["wq1"]
    bq1 = io["bq1"]
    wp1 = io["wp1"]
    bp1 = io["bp1"]
    xBT = io["xBT"]

    from contextlib import ExitStack
    with ExitStack() as _ctx:
        xnsb = _ctx.enter_context(nc.sbuf_tensor("Dxnsb", [128, NCH * TPC], BF16))
        wkvb = _ctx.enter_context(nc.sbuf_tensor("Dwkvb", [128, NCH * 2048], BF16))
        wqb = _ctx.enter_context(nc.sbuf_tensor("Dwqb", [128, NCH * 1024], BF16))
        wpb = _ctx.enter_context(nc.sbuf_tensor("Dwpb", [128, NCH * 1024], BF16))
        bkvb = _ctx.enter_context(nc.sbuf_tensor("Dbkvb", [128, 2048], F32))
        bqb = _ctx.enter_context(nc.sbuf_tensor("Dbqb", [128, 1024], F32))
        bpc = _ctx.enter_context(nc.sbuf_tensor("Dbpc", [128, NCH], F32))
        brow = _ctx.enter_context(nc.sbuf_tensor("Dbrow", [1, 2048], F32))
        onesr = _ctx.enter_context(nc.sbuf_tensor("Donesr", [1, 128], F32))
        identb = _ctx.enter_context(nc.sbuf_tensor("Didentb", [128, 128], BF16))
        qtok = _ctx.enter_context(nc.sbuf_tensor("Dqtok", [128, 4 * 1024], BF16))
        ktok = _ctx.enter_context(nc.sbuf_tensor("Dktok2", [128, 4 * 1024], BF16))
        vtok = _ctx.enter_context(nc.sbuf_tensor("Dvtok2", [128, 4 * 1024], BF16))
        atile = _ctx.enter_context(nc.sbuf_tensor("Datile", [128, 4 * 16 * 4], F32))
        amx = _ctx.enter_context(nc.sbuf_tensor("Damx", [128, 4 * 16], F32))
        tmp1 = _ctx.enter_context(nc.sbuf_tensor("Dtmp1", [128, 1024], F32))
        avt = _ctx.enter_context(nc.sbuf_tensor("Davt", [128, 4 * 1024], BF16))
        avT = _ctx.enter_context(nc.sbuf_tensor("DavT", [128, NCH * TPC], BF16))
        xbu = _ctx.enter_context(nc.sbuf_tensor("Dxbu", [128, 512], F32))
        xao = _ctx.enter_context(nc.sbuf_tensor("Dxao", [128, 512], F32))
        psA = _ctx.enter_context(nc.psum_tensor("DpsA", [128, 512], F32))
        psB2 = _ctx.enter_context(nc.psum_tensor("DpsB2", [128, 512], F32))
        psC2 = _ctx.enter_context(nc.psum_tensor("DpsC2", [128, 512], F32))
        psD2 = _ctx.enter_context(nc.psum_tensor("DpsD2", [128, 512], F32))
        ptr = _ctx.enter_context(nc.psum_tensor("Dptr", [128, 512], BF16))
        xnv = xnsb.ap().rearrange("p (c t) -> p c t", c=NCH)
        wkvv = wkvb.ap().rearrange("p (c t) -> p c t", c=NCH)
        wqv = wqb.ap().rearrange("p (c t) -> p c t", c=NCH)
        wpv = wpb.ap().rearrange("p (c t) -> p c t", c=NCH)
        qtv = qtok.ap().rearrange("p (b f) -> p b f", b=4)
        ktv = ktok.ap().rearrange("p (b f) -> p b f", b=4)
        vtv = vtok.ap().rearrange("p (b f) -> p b f", b=4)
        avv = avt.ap().rearrange("p (b f) -> p b f", b=4)
        avTv = avT.ap().rearrange("p (c t) -> p c t", c=NCH)
        av_ = atile.ap().rearrange("p (s h t) -> p s h t", s=4, h=16)
        amv = amx.ap().rearrange("p (s h) -> p s h", s=4)

        @sq.step(blk, "gpsimd")
        def _(eng, st):
            for c in range(NCH):
                st.dma(xnv[:, c], xn1T[128 * c:128 * (c + 1), :])
                st.dma(wkvv[:, c], wkv[128 * c:128 * (c + 1), :])
                st.dma(wqv[:, c], wq1[128 * c:128 * (c + 1), :])
                st.dma(wpv[:, c], wp1[128 * c:128 * (c + 1), :])
            for c in range(NCH):
                st.dma(bpc[:, c:c + 1], bp1[0:1, 128 * c:128 * (c + 1)].rearrange("o p -> p o"))

        @sq.step(blk, "gpsimd")
        def _(eng, st):
            return make_ident(nc, eng, identb[:, :], st)

        @sq.step(blk, "sync")
        def _(eng, st):
            st.dma(brow[0:1, :], bkv[0:1, :])

        @sq.step(blk, "vector")
        def _(eng, st):
            return eng.memset(onesr[:, :], 1.0)

        for i in range(4):
            @sq.step(blk, "tensor")
            def _(eng, st, i=i):
                return eng.matmul(psA[:, :], onesr[:, :],
                                  brow[0:1, 512 * i:512 * (i + 1)],
                                  start=True, stop=True)

            @sq.step(blk, "vector")
            def _(eng, st, i=i):
                return eng.tensor_copy(out=bkvb[:, 512 * i:512 * (i + 1)], in_=psA[:, :])

        @sq.step(blk, "sync")
        def _(eng, st):
            st.dma(brow[0:1, 0:1024], bq1[0:1, :])

        for i in range(2):
            @sq.step(blk, "tensor")
            def _(eng, st, i=i):
                return eng.matmul(psA[:, :], onesr[:, :],
                                  brow[0:1, 512 * i:512 * (i + 1)],
                                  start=True, stop=True)

            @sq.step(blk, "vector")
            def _(eng, st, i=i):
                return eng.tensor_copy(out=bqb[:, 512 * i:512 * (i + 1)], in_=psA[:, :])

        for jb in range(4):
            for b in range(4):
                tsl = slice(512 * b + 128 * jb, 512 * b + 128 * (jb + 1))
                for fg in range(4):
                    @sq.step(blk, "tensor")
                    def _(eng, st, fg=fg, tsl=tsl):
                        last = None
                        for k in range(NCH):
                            last = eng.matmul(psA[:, :],
                                              xnv[:, k][:, tsl],
                                              wkvv[:, k][:, 512 * fg:512 * (fg + 1)],
                                              start=(k == 0), stop=(k == NCH - 1))
                        return last

                    @sq.step(blk, "vector")
                    def _(eng, st, fg=fg, b=b):
                        dst = ktv[:, b][:, 512 * fg:512 * (fg + 1)] if fg < 2 else \
                            vtv[:, b][:, 512 * (fg - 2):512 * (fg - 1)]
                        return eng.tensor_tensor(out=dst, in0=psA[:, :],
                                                 in1=bkvb[:, 512 * fg:512 * (fg + 1)],
                                                 op=ALU.add)

                for fg in range(2):
                    @sq.step(blk, "tensor")
                    def _(eng, st, fg=fg, tsl=tsl):
                        last = None
                        for k in range(NCH):
                            last = eng.matmul(psA[:, :],
                                              xnv[:, k][:, tsl],
                                              wqv[:, k][:, 512 * fg:512 * (fg + 1)],
                                              start=(k == 0), stop=(k == NCH - 1))
                        return last

                    @sq.step(blk, "vector")
                    def _(eng, st, fg=fg, b=b):
                        return eng.tensor_tensor(out=qtv[:, b][:, 512 * fg:512 * (fg + 1)],
                                                 in0=psA[:, :],
                                                 in1=bqb[:, 512 * fg:512 * (fg + 1)],
                                                 op=ALU.add)

            for s in range(4):
                for t in range(4):
                    @sq.step(blk, "vector")
                    def _(eng, st, s=s, t=t):
                        st.chain(eng.tensor_tensor(out=tmp1[:, :], in0=qtv[:, s],
                                                   in1=ktv[:, t], op=ALU.mult))
                        return eng.reduce_sum(
                            out=av_[:, s, :, t].unsqueeze(-1),
                            in_=tmp1.ap().rearrange("p (h d) -> p h d", h=16),
                            axis=mybir.AxisListType.X)

            @sq.step(blk, "vector")
            def _(eng, st):
                return eng.reduce_max(out=amx[:, :], in_=atile[:, :].rearrange(
                    "p (s h t) -> p (s h) t", s=4, h=16), axis=mybir.AxisListType.X)

            @sq.step(blk, "vector")
            def _(eng, st):
                return eng.tensor_tensor(
                    out=atile[:, :].rearrange("p (s h t) -> p (s h) t", s=4, h=16),
                    in0=atile[:, :].rearrange("p (s h t) -> p (s h) t", s=4, h=16),
                    in1=amx[:, :].unsqueeze(-1).to_broadcast([128, 64, 4]),
                    op=ALU.subtract)

            @sq.step(blk, "scalar")
            def _(eng, st):
                return eng.activation(out=atile[:, :], in_=atile[:, :], func=AF.Exp,
                                      scale=1.0 / 8.0)

            @sq.step(blk, "vector")
            def _(eng, st):
                return eng.reduce_sum(out=amx[:, :], in_=atile[:, :].rearrange(
                    "p (s h t) -> p (s h) t", s=4, h=16), axis=mybir.AxisListType.X)

            @sq.step(blk, "vector")
            def _(eng, st):
                st.chain(eng.reciprocal(out=amx[:, :], in_=amx[:, :]))
                return eng.tensor_tensor(
                    out=atile[:, :].rearrange("p (s h t) -> p (s h) t", s=4, h=16),
                    in0=atile[:, :].rearrange("p (s h t) -> p (s h) t", s=4, h=16),
                    in1=amx[:, :].unsqueeze(-1).to_broadcast([128, 64, 4]),
                    op=ALU.mult)

            for s in range(4):
                for t in range(4):
                    @sq.step(blk, "vector")
                    def _(eng, st, s=s, t=t):
                        pview = av_[:, s, :, t].unsqueeze(-1).to_broadcast([128, 16, 64])
                        if t == 0:
                            return eng.tensor_tensor(
                                out=avv[:, s].rearrange("p (h d) -> p h d", h=16),
                                in0=vtv[:, t].rearrange("p (h d) -> p h d", h=16),
                                in1=pview, op=ALU.mult)
                        st.chain(eng.tensor_tensor(
                            out=tmp1.ap().rearrange("p (h d) -> p h d", h=16),
                            in0=vtv[:, t].rearrange("p (h d) -> p h d", h=16),
                            in1=pview, op=ALU.mult))
                        return eng.tensor_tensor(out=avv[:, s], in0=avv[:, s],
                                                 in1=tmp1[:, :], op=ALU.add)

            for s in range(4):
                for fg in range(2):
                    @sq.step(blk, "tensor")
                    def _(eng, st, s=s, fg=fg):
                        last = None
                        for u in range(4):
                            fc = 4 * fg + u
                            last = eng.transpose(ptr[:, 128 * u:128 * (u + 1)],
                                                 avv[:, s][:, 128 * fc:128 * (fc + 1)],
                                                 identb[:, :])
                        return last

                    @sq.step(blk, "vector")
                    def _(eng, st, s=s, fg=fg, jb=jb):
                        last = None
                        for u in range(4):
                            fc = 4 * fg + u
                            last = eng.tensor_copy(
                                out=avTv[:, fc][:, 512 * s + 128 * jb:512 * s + 128 * (jb + 1)],
                                in_=ptr[:, 128 * u:128 * (u + 1)])
                        return last

        for oc in range(NCH):
            for tt in range(NTT):
                @sq.step(blk, "tensor")
                def _(eng, st, oc=oc, tt=tt):
                    last = None
                    for k in range(NCH):
                        last = eng.matmul(psD2[:, :],
                                          wpv[:, k][:, 128 * oc:128 * (oc + 1)],
                                          avTv[:, k][:, 512 * tt:512 * (tt + 1)],
                                          start=(k == 0), stop=(k == NCH - 1))
                    return last

                @sq.step(blk, "sync")
                def _(eng, st, oc=oc, tt=tt):
                    st.dma(xbu[:, :], x1T[128 * oc:128 * (oc + 1), 512 * tt:512 * (tt + 1)])

                @sq.step(blk, "vector")
                def _(eng, st, oc=oc):
                    st.chain(eng.tensor_scalar(out=xao[:, :], in0=psD2[:, :],
                                               scalar1=bpc[:, oc:oc + 1], scalar2=None,
                                               op0=ALU.add))
                    return eng.tensor_tensor(out=xao[:, :], in0=xao[:, :],
                                             in1=xbu[:, :], op=ALU.add)

                @sq.step(blk, "sync")
                def _(eng, st, oc=oc, tt=tt):
                    st.dma(xBT[128 * oc:128 * (oc + 1), 512 * tt:512 * (tt + 1)],
                           xao[:, :])



def emit_round_inplace(eng, st, r, ti, tf, td):
    st.chain(eng.tensor_copy(out=ti, in_=r))
    st.chain(eng.tensor_copy(out=tf, in_=ti))
    st.chain(eng.tensor_tensor(out=td, in0=r, in1=tf, op=ALU.subtract))
    st.chain(eng.tensor_scalar(out=r, in0=td, scalar1=0.5, scalar2=None, op0=ALU.is_gt))
    st.chain(eng.tensor_tensor(out=tf, in0=tf, in1=r, op=ALU.add))
    st.chain(eng.tensor_scalar(out=r, in0=td, scalar1=-0.5, scalar2=None, op0=ALU.is_lt))
    return eng.tensor_tensor(out=r, in0=tf, in1=r, op=ALU.subtract)


def emit_qdq(eng, st, out_ap, x_ap, inv_col, s_col, r, ti, tf, td):
    st.chain(eng.tensor_scalar(out=r, in0=x_ap, scalar1=inv_col, scalar2=None,
                               op0=ALU.mult))
    st.chain(emit_round_inplace(eng, st, r, ti, tf, td))
    return eng.tensor_scalar(out=out_ap, in0=r, scalar1=s_col, scalar2=None,
                             op0=ALU.mult)


def emit_scale_from_max(sq, blk, nc, onesr, m_in_row, n_in, qmax, srow, scol, icol, ps):
    @sq.step(blk, "vector")
    def _(eng, st):
        st.chain(eng.reduce_max(out=srow[0:1, 0:1], in_=m_in_row[0:1, 0:n_in],
                                axis=mybir.AxisListType.X))
        st.chain(eng.tensor_scalar(out=srow[0:1, 1:2], in0=srow[0:1, 0:1],
                                   scalar1=1.0 / qmax, scalar2=1e-12,
                                   op0=ALU.mult, op1=ALU.max))
        return eng.reciprocal(out=srow[0:1, 2:3], in_=srow[0:1, 1:2])

    @sq.step(blk, "tensor")
    def _(eng, st):
        eng.matmul(ps[:, 0:1], onesr[:, :], srow[0:1, 1:2], start=True, stop=True)
        return eng.matmul(ps[:, 1:2], onesr[:, :], srow[0:1, 2:3], start=True, stop=True)

    @sq.step(blk, "vector")
    def _(eng, st):
        st.chain(eng.tensor_copy(out=scol, in_=ps[:, 0:1]))
        return eng.tensor_copy(out=icol, in_=ps[:, 1:2])


def emit_colmax_to_scalar(sq, blk, nc, mcol, identf, psT, mrow, out11):
    @sq.step(blk, "tensor")
    def _(eng, st):
        return eng.transpose(psT[0:1, 0:128], mcol, identf[:, :])

    @sq.step(blk, "vector")
    def _(eng, st):
        st.chain(eng.tensor_copy(out=mrow[0:1, :], in_=psT[0:1, 0:128]))
        return eng.reduce_max(out=out11, in_=mrow[0:1, :], axis=mybir.AxisListType.X)


def emit_dequant_chunk(eng, st, w_ap, wq_ap, nblk, sw, swr, r, ti, tf, td):
    wv = w_ap.rearrange("p (b i) -> p b i", i=32)
    st.chain(eng.reduce_max(out=sw, in_=wv, axis=mybir.AxisListType.X,
                            apply_absolute_value=True))
    st.chain(eng.tensor_scalar(out=sw, in0=sw, scalar1=1.0 / 7.0, scalar2=1e-12,
                               op0=ALU.mult, op1=ALU.max))
    st.chain(eng.reciprocal(out=swr, in_=sw))
    rv = r.rearrange("p (b i) -> p b i", i=32)
    st.chain(eng.tensor_tensor(out=rv, in0=wv,
                               in1=swr.unsqueeze(-1).to_broadcast(list(sw.shape) + [32]),
                               op=ALU.mult))
    st.chain(emit_round_inplace(eng, st, r, ti, tf, td))
    return eng.tensor_tensor(out=wq_ap.rearrange("p (b i) -> p b i", i=32), in0=rv,
                             in1=sw.unsqueeze(-1).to_broadcast(list(sw.shape) + [32]),
                             op=ALU.mult)



def emit_E1(nc, blk, sq, io):
    xn1F = io["xn1F"]
    wgn = io["wgn"]
    wun = io["wun"]
    upD = io["upD"]
    gateD = io["gateD"]

    NIC = I_L1 // 128

    from contextlib import ExitStack
    with ExitStack() as _ctx:
        xqsb = _ctx.enter_context(nc.sbuf_tensor("Exqsb", [128, NCH * TPC], BF16))
        wgT = _ctx.enter_context(nc.sbuf_tensor("EwgT", [128, NCH * I_L1], BF16))
        wuT = _ctx.enter_context(nc.sbuf_tensor("EwuT", [128, NCH * I_L1], BF16))
        wch = _ctx.enter_context(nc.sbuf_tensor("Ewch", [128, 1024], F32))
        wqc = _ctx.enter_context(nc.sbuf_tensor("Ewqc", [128, 1024], BF16))
        sw = _ctx.enter_context(nc.sbuf_tensor("Esw", [128, 32], F32))
        swr = _ctx.enter_context(nc.sbuf_tensor("Eswr", [128, 32], F32))
        rr = _ctx.enter_context(nc.sbuf_tensor("Err", [128, 2048], F32))
        ti = _ctx.enter_context(nc.sbuf_tensor("Eti", [128, 2048], I32))
        tf = _ctx.enter_context(nc.sbuf_tensor("Etf", [128, 2048], F32))
        td = _ctx.enter_context(nc.sbuf_tensor("Etd", [128, 2048], F32))
        xch = _ctx.enter_context(nc.sbuf_tensor("Exch", [128, 2048], F32))
        urow = _ctx.enter_context(nc.sbuf_tensor("Eurow", [128, 512], BF16))
        onesr = _ctx.enter_context(nc.sbuf_tensor("Eonesr", [1, 128], F32))
        identf = _ctx.enter_context(nc.sbuf_tensor("Eidentf", [128, 128], F32))
        identb = _ctx.enter_context(nc.sbuf_tensor("Eidentb", [128, 128], BF16))
        srow = _ctx.enter_context(nc.sbuf_tensor("Esrow", [1, 8], F32))
        mrow = _ctx.enter_context(nc.sbuf_tensor("Emrow", [1, 128], F32))
        scol = _ctx.enter_context(nc.sbuf_tensor("Escol", [128, 1], F32))
        icol = _ctx.enter_context(nc.sbuf_tensor("Eicol", [128, 1], F32))
        mxu = _ctx.enter_context(nc.sbuf_tensor("Emxu", [128, 3], F32))
        mxg = _ctx.enter_context(nc.sbuf_tensor("Emxg", [128, 3], F32))
        psA = _ctx.enter_context(nc.psum_tensor("EpsA", [128, 512], F32))
        psW = _ctx.enter_context(nc.psum_tensor("EpsW", [128, 512], BF16))
        psT = _ctx.enter_context(nc.psum_tensor("EpsT", [128, 128], F32))
        xqv = xqsb.ap().rearrange("p (c t) -> p c t", c=NCH)
        wgv = wgT.ap().rearrange("p (c t) -> p c t", c=NCH)
        wuv = wuT.ap().rearrange("p (c t) -> p c t", c=NCH)

        @sq.step(blk, "gpsimd")
        def _(eng, st):
            st.dma(srow[0:1, :], io["xmax_g"].rearrange("r o -> o r")[0:1, :])

        @sq.step(blk, "gpsimd")
        def _(eng, st):
            st.chain(make_ident(nc, eng, identf[:, :], st))
            return make_ident(nc, eng, identb[:, :], st)

        @sq.step(blk, "vector")
        def _(eng, st):
            eng.memset(onesr[:, :], 1.0)
            eng.memset(mxu[:, 0:1], 0.0)
            return eng.memset(mxg[:, 0:1], 0.0)

        emit_scale_from_max(sq, blk, nc, onesr, srow, 8, 32767.0, srow, scol[:, 0:1], icol[:, 0:1], psT)

        for c in range(NCH):
            @sq.step(blk, "sync")
            def _(eng, st, c=c):
                st.dma(xch[:, :], xn1F[128 * c:128 * (c + 1), :])

            @sq.step(blk, "vector")
            def _(eng, st, c=c):
                return emit_qdq(eng, st, xqv[:, c], xch[:, :], icol[:, 0:1],
                                scol[:, 0:1], rr[:, :], ti[:, :], tf[:, :], td[:, :])

        for which, wn, wTv in (("g", wgn, wgv), ("u", wun, wuv)):
            for oc in range(NIC):
                @sq.step(blk, "sync")
                def _(eng, st, wn=wn, oc=oc):
                    st.dma(wch[:, :], wn[128 * oc:128 * (oc + 1), :])

                @sq.step(blk, "vector")
                def _(eng, st):
                    return emit_dequant_chunk(eng, st, wch[:, :], wqc[:, :], 32,
                                              sw[:, :], swr[:, :], rr[:, 0:1024],
                                              ti[:, 0:1024], tf[:, 0:1024], td[:, 0:1024])

                for hc in range(NCH):
                    @sq.step(blk, "tensor")
                    def _(eng, st, hc=hc):
                        return eng.transpose(psW[:, 0:128],
                                             wqc[:, 128 * hc:128 * (hc + 1)],
                                             identb[:, :])

                    @sq.step(blk, "vector")
                    def _(eng, st, hc=hc, oc=oc, wTv=wTv):
                        return eng.tensor_copy(out=wTv[:, hc][:, 128 * oc:128 * (oc + 1)],
                                               in_=psW[:, 0:128])

        for which, wTv, outD, mx in (("g", wgv, gateD, mxg), ("u", wuv, upD, mxu)):
            for ic in range(NIC):
                for tt in range(NTT):
                    @sq.step(blk, "tensor")
                    def _(eng, st, ic=ic, tt=tt, wTv=wTv):
                        last = None
                        for k in range(NCH):
                            last = eng.matmul(psA[:, :],
                                              wTv[:, k][:, 128 * ic:128 * (ic + 1)],
                                              xqv[:, k][:, 512 * tt:512 * (tt + 1)],
                                              start=(k == 0), stop=(k == NCH - 1))
                        return last

                    @sq.step(blk, "vector")
                    def _(eng, st, mx=mx):
                        st.chain(eng.tensor_copy(out=urow[:, :], in_=psA[:, :]))
                        st.chain(eng.reduce_max(out=mx[:, 1:2], in_=psA[:, :],
                                                axis=mybir.AxisListType.X,
                                                apply_absolute_value=True))
                        return eng.tensor_tensor(out=mx[:, 0:1], in0=mx[:, 0:1],
                                                 in1=mx[:, 1:2], op=ALU.max)

                    @sq.step(blk, "sync")
                    def _(eng, st, ic=ic, tt=tt, outD=outD):
                        st.dma(outD[128 * ic:128 * (ic + 1), 512 * tt:512 * (tt + 1)],
                               urow[:, :])

        emit_colmax_to_scalar(sq, blk, nc, mxg[:, 0:1], identf, psT, mrow, srow[0:1, 3:4])
        emit_colmax_to_scalar(sq, blk, nc, mxu[:, 0:1], identf, psT, mrow, srow[0:1, 4:5])

        @sq.step(blk, "sync")
        def _(eng, st):
            st.dma(io["m1_in"][0:1, 0:2], srow[0:1, 3:5])



def emit_E2(nc, blk, sq, io):
    upD = io["upD"]
    gateD = io["gateD"]
    upqD = io["upqD"]
    g2D = io["g2D"]

    NIC = I_L1 // 128

    from contextlib import ExitStack
    with ExitStack() as _ctx:
        ch = _ctx.enter_context(nc.sbuf_tensor("Fch", [128, TPC], F32))
        ch2 = _ctx.enter_context(nc.sbuf_tensor("Fch2", [128, TPC], F32))
        chb = _ctx.enter_context(nc.sbuf_tensor("Fchb", [128, TPC], BF16))
        rr = _ctx.enter_context(nc.sbuf_tensor("Frr", [128, TPC], F32))
        ti = _ctx.enter_context(nc.sbuf_tensor("Fti", [128, TPC], I32))
        tf = _ctx.enter_context(nc.sbuf_tensor("Ftf", [128, TPC], F32))
        td = _ctx.enter_context(nc.sbuf_tensor("Ftd", [128, TPC], F32))
        onesr = _ctx.enter_context(nc.sbuf_tensor("Fonesr", [1, 128], F32))
        identf = _ctx.enter_context(nc.sbuf_tensor("Fidentf", [128, 128], F32))
        srow = _ctx.enter_context(nc.sbuf_tensor("Fsrow", [1, 8], F32))
        mrow = _ctx.enter_context(nc.sbuf_tensor("Fmrow", [1, 128], F32))
        sgc = _ctx.enter_context(nc.sbuf_tensor("Fsgc", [128, 2], F32))
        suc = _ctx.enter_context(nc.sbuf_tensor("Fsuc", [128, 2], F32))
        mxc = _ctx.enter_context(nc.sbuf_tensor("Fmxc", [128, 2], F32))
        psT = _ctx.enter_context(nc.psum_tensor("FpsT", [128, 128], F32))

        @sq.step(blk, "gpsimd")
        def _(eng, st):
            st.dma(srow[0:1, 0:8], io["m1_g"].rearrange("c two -> two c")[0:1, :],
                   allow_slow_non_contiguous=True)

        @sq.step(blk, "gpsimd")
        def _(eng, st):
            return make_ident(nc, eng, identf[:, :], st)

        @sq.step(blk, "vector")
        def _(eng, st):
            eng.memset(onesr[:, :], 1.0)
            return eng.memset(mxc[:, 0:1], 0.0)

        emit_scale_from_max(sq, blk, nc, onesr, srow, 8, 32767.0, srow,
                            sgc[:, 0:1], sgc[:, 1:2], psT)

        @sq.step(blk, "gpsimd")
        def _(eng, st):
            st.dma(srow[0:1, 0:8], io["m1_g"].rearrange("c two -> two c")[1:2, :],
                   allow_slow_non_contiguous=True)

        emit_scale_from_max(sq, blk, nc, onesr, srow, 8, 32767.0, srow,
                            suc[:, 0:1], suc[:, 1:2], psT)

        for ic in range(NIC):
            @sq.step(blk, "sync")
            def _(eng, st, ic=ic):
                st.dma(chb[:, :], gateD[128 * ic:128 * (ic + 1), :])

            @sq.step(blk, "vector")
            def _(eng, st):
                st.chain(eng.tensor_copy(out=ch[:, :], in_=chb[:, :]))
                return emit_qdq(eng, st, ch[:, :], ch[:, :], sgc[:, 1:2],
                                sgc[:, 0:1], rr[:, :], ti[:, :], tf[:, :], td[:, :])

            @sq.step(blk, "scalar")
            def _(eng, st):
                return eng.activation(out=ch2[:, :], in_=ch[:, :], func=AF.Sigmoid)

            @sq.step(blk, "vector")
            def _(eng, st):
                st.chain(eng.tensor_scalar(out=rr[:, :], in0=ch2[:, :],
                                           scalar1=65536.0, scalar2=None, op0=ALU.mult))
                st.chain(emit_round_inplace(eng, st, rr[:, :], ti[:, :], tf[:, :], td[:, :]))
                st.chain(eng.tensor_scalar(out=rr[:, :], in0=rr[:, :], scalar1=65535.0,
                                           scalar2=1.0 / 65536.0, op0=ALU.min, op1=ALU.mult))
                st.chain(eng.tensor_tensor(out=ch[:, :], in0=ch[:, :], in1=rr[:, :],
                                           op=ALU.mult))
                st.chain(eng.tensor_copy(out=chb[:, :], in_=ch[:, :]))
                st.chain(eng.reduce_max(out=mxc[:, 1:2], in_=ch[:, :],
                                        axis=mybir.AxisListType.X,
                                        apply_absolute_value=True))
                return eng.tensor_tensor(out=mxc[:, 0:1], in0=mxc[:, 0:1],
                                         in1=mxc[:, 1:2], op=ALU.max)

            @sq.step(blk, "sync")
            def _(eng, st, ic=ic):
                st.dma(g2D[128 * ic:128 * (ic + 1), :], chb[:, :])

            @sq.step(blk, "sync")
            def _(eng, st, ic=ic):
                st.dma(chb[:, :], upD[128 * ic:128 * (ic + 1), :])

            @sq.step(blk, "vector")
            def _(eng, st):
                st.chain(eng.tensor_copy(out=ch[:, :], in_=chb[:, :]))
                st.chain(emit_qdq(eng, st, ch[:, :], ch[:, :], suc[:, 1:2],
                                  suc[:, 0:1], rr[:, :], ti[:, :], tf[:, :], td[:, :]))
                return eng.tensor_copy(out=chb[:, :], in_=ch[:, :])

            @sq.step(blk, "sync")
            def _(eng, st, ic=ic):
                st.dma(upqD[128 * ic:128 * (ic + 1), :], chb[:, :])

        emit_colmax_to_scalar(sq, blk, nc, mxc[:, 0:1], identf, psT, mrow, srow[0:1, 3:4])

        @sq.step(blk, "sync")
        def _(eng, st):
            st.dma(io["mg2_in"][0:1, 0:1], srow[0:1, 3:4])



def emit_E3(nc, blk, sq, io):
    upqD = io["upqD"]
    g2D = io["g2D"]
    oD = io["oD"]

    NIC = I_L1 // 128

    from contextlib import ExitStack
    with ExitStack() as _ctx:
        ch = _ctx.enter_context(nc.sbuf_tensor("Gch", [128, TPC], F32))
        chb = _ctx.enter_context(nc.sbuf_tensor("Gchb", [128, TPC], BF16))
        chb2 = _ctx.enter_context(nc.sbuf_tensor("Gchb2", [128, TPC], BF16))
        rr = _ctx.enter_context(nc.sbuf_tensor("Grr", [128, TPC], F32))
        ti = _ctx.enter_context(nc.sbuf_tensor("Gti", [128, TPC], I32))
        tf = _ctx.enter_context(nc.sbuf_tensor("Gtf", [128, TPC], F32))
        td = _ctx.enter_context(nc.sbuf_tensor("Gtd", [128, TPC], F32))
        onesr = _ctx.enter_context(nc.sbuf_tensor("Gonesr", [1, 128], F32))
        identf = _ctx.enter_context(nc.sbuf_tensor("Gidentf", [128, 128], F32))
        srow = _ctx.enter_context(nc.sbuf_tensor("Gsrow", [1, 8], F32))
        mrow = _ctx.enter_context(nc.sbuf_tensor("Gmrow", [1, 128], F32))
        sgc = _ctx.enter_context(nc.sbuf_tensor("Gsgc", [128, 2], F32))
        mxc = _ctx.enter_context(nc.sbuf_tensor("Gmxc", [128, 2], F32))
        psT = _ctx.enter_context(nc.psum_tensor("GpsT", [128, 128], F32))

        @sq.step(blk, "gpsimd")
        def _(eng, st):
            st.dma(srow[0:1, 0:8], io["mg2_g"].rearrange("r o -> o r")[0:1, :])

        @sq.step(blk, "gpsimd")
        def _(eng, st):
            return make_ident(nc, eng, identf[:, :], st)

        @sq.step(blk, "vector")
        def _(eng, st):
            eng.memset(onesr[:, :], 1.0)
            return eng.memset(mxc[:, 0:1], 0.0)

        emit_scale_from_max(sq, blk, nc, onesr, srow, 8, 32767.0, srow,
                            sgc[:, 0:1], sgc[:, 1:2], psT)

        for ic in range(NIC):
            @sq.step(blk, "sync")
            def _(eng, st, ic=ic):
                st.dma(chb[:, :], g2D[128 * ic:128 * (ic + 1), :])
                st.dma(chb2[:, :], upqD[128 * ic:128 * (ic + 1), :])

            @sq.step(blk, "vector")
            def _(eng, st):
                st.chain(eng.tensor_copy(out=ch[:, :], in_=chb[:, :]))
                st.chain(emit_qdq(eng, st, ch[:, :], ch[:, :], sgc[:, 1:2],
                                  sgc[:, 0:1], rr[:, :], ti[:, :], tf[:, :], td[:, :]))
                st.chain(eng.tensor_tensor(out=ch[:, :], in0=ch[:, :], in1=chb2[:, :],
                                           op=ALU.mult))
                st.chain(eng.tensor_copy(out=chb[:, :], in_=ch[:, :]))
                st.chain(eng.reduce_max(out=mxc[:, 1:2], in_=ch[:, :],
                                        axis=mybir.AxisListType.X,
                                        apply_absolute_value=True))
                return eng.tensor_tensor(out=mxc[:, 0:1], in0=mxc[:, 0:1],
                                         in1=mxc[:, 1:2], op=ALU.max)

            @sq.step(blk, "sync")
            def _(eng, st, ic=ic):
                st.dma(oD[128 * ic:128 * (ic + 1), :], chb[:, :])

        emit_colmax_to_scalar(sq, blk, nc, mxc[:, 0:1], identf, psT, mrow, srow[0:1, 3:4])

        @sq.step(blk, "sync")
        def _(eng, st):
            st.dma(io["mo_in"][0:1, 0:1], srow[0:1, 3:4])



def emit_E4(nc, blk, sq, io):
    oD = io["oD"]
    wdn = io["wdn"]
    xBT = io["xBT"]
    xoutT = io["xoutT"]

    NIC = I_L1 // 128

    from contextlib import ExitStack
    with ExitStack() as _ctx:
        oq = _ctx.enter_context(nc.sbuf_tensor("Hoq", [128, NIC * TPC // 4], BF16))
        wdT = _ctx.enter_context(nc.sbuf_tensor("HwdT", [128, NIC * 1024], BF16))
        wch = _ctx.enter_context(nc.sbuf_tensor("Hwch", [128, I_L1], F32))
        wqc = _ctx.enter_context(nc.sbuf_tensor("Hwqc", [128, I_L1], BF16))
        sw = _ctx.enter_context(nc.sbuf_tensor("Hsw", [128, 96], F32))
        swr = _ctx.enter_context(nc.sbuf_tensor("Hswr", [128, 96], F32))
        rr = _ctx.enter_context(nc.sbuf_tensor("Hrr", [128, I_L1], F32))
        ti = _ctx.enter_context(nc.sbuf_tensor("Hti", [128, I_L1], I32))
        tf = _ctx.enter_context(nc.sbuf_tensor("Htf", [128, I_L1], F32))
        td = _ctx.enter_context(nc.sbuf_tensor("Htd", [128, I_L1], F32))
        chb = _ctx.enter_context(nc.sbuf_tensor("Hchb", [128, TPC], BF16))
        ch = _ctx.enter_context(nc.sbuf_tensor("Hch", [128, TPC], F32))
        onesr = _ctx.enter_context(nc.sbuf_tensor("Honesr", [1, 128], F32))
        identb = _ctx.enter_context(nc.sbuf_tensor("Hidentb", [128, 128], BF16))
        srow = _ctx.enter_context(nc.sbuf_tensor("Hsrow", [1, 8], F32))
        scol = _ctx.enter_context(nc.sbuf_tensor("Hscol", [128, 1], F32))
        icol = _ctx.enter_context(nc.sbuf_tensor("Hicol", [128, 1], F32))
        xbu = _ctx.enter_context(nc.sbuf_tensor("Hxbu", [128, 512], F32))
        xao = _ctx.enter_context(nc.sbuf_tensor("Hxao", [128, 512], F32))
        psA = _ctx.enter_context(nc.psum_tensor("HpsA", [128, 512], F32))
        psW = _ctx.enter_context(nc.psum_tensor("HpsW", [128, 128], BF16))
        psT = _ctx.enter_context(nc.psum_tensor("HpsT", [128, 128], F32))
        oqv = oq.ap().rearrange("p (c t) -> p c t", c=NIC)
        wdv = wdT.ap().rearrange("p (c t) -> p c t", c=NIC)

        @sq.step(blk, "gpsimd")
        def _(eng, st):
            st.dma(srow[0:1, 0:8], io["mo_g"].rearrange("r o -> o r")[0:1, :])

        @sq.step(blk, "gpsimd")
        def _(eng, st):
            return make_ident(nc, eng, identb[:, :], st)

        @sq.step(blk, "vector")
        def _(eng, st):
            return eng.memset(onesr[:, :], 1.0)

        emit_scale_from_max(sq, blk, nc, onesr, srow, 8, 32767.0, srow, scol[:, 0:1], icol[:, 0:1], psT)

        for hc in range(NCH):
            @sq.step(blk, "sync")
            def _(eng, st, hc=hc):
                st.dma(wch[:, :], wdn[128 * hc:128 * (hc + 1), :])

            @sq.step(blk, "vector")
            def _(eng, st):
                return emit_dequant_chunk(eng, st, wch[:, :], wqc[:, :], 96,
                                          sw[:, :], swr[:, :], rr[:, :],
                                          ti[:, :], tf[:, :], td[:, :])

            for ic in range(NIC):
                @sq.step(blk, "tensor")
                def _(eng, st, ic=ic):
                    return eng.transpose(psW[:, 0:128],
                                         wqc[:, 128 * ic:128 * (ic + 1)],
                                         identb[:, :])

                @sq.step(blk, "vector")
                def _(eng, st, ic=ic, hc=hc):
                    return eng.tensor_copy(out=wdv[:, ic][:, 128 * hc:128 * (hc + 1)],
                                           in_=psW[:, 0:128])

        for tt in range(NTT):
            for ic in range(NIC):
                @sq.step(blk, "sync")
                def _(eng, st, ic=ic, tt=tt):
                    st.dma(chb[:, 0:512], oD[128 * ic:128 * (ic + 1),
                                             512 * tt:512 * (tt + 1)])

                @sq.step(blk, "vector")
                def _(eng, st, ic=ic):
                    st.chain(eng.tensor_copy(out=ch[:, 0:512], in_=chb[:, 0:512]))
                    return emit_qdq(eng, st, oqv[:, ic], ch[:, 0:512], icol[:, 0:1],
                                    scol[:, 0:1], rr[:, 0:512], ti[:, 0:512],
                                    tf[:, 0:512], td[:, 0:512])

            for oc in range(NCH):
                @sq.step(blk, "tensor")
                def _(eng, st, oc=oc):
                    last = None
                    for k in range(NIC):
                        last = eng.matmul(psA[:, :],
                                          wdv[:, k][:, 128 * oc:128 * (oc + 1)],
                                          oqv[:, k], start=(k == 0), stop=(k == NIC - 1))
                    return last

                @sq.step(blk, "sync")
                def _(eng, st, oc=oc, tt=tt):
                    st.dma(xbu[:, :], xBT[128 * oc:128 * (oc + 1), 512 * tt:512 * (tt + 1)])

                @sq.step(blk, "vector")
                def _(eng, st):
                    return eng.tensor_tensor(out=xao[:, :], in0=psA[:, :],
                                             in1=xbu[:, :], op=ALU.add)

                @sq.step(blk, "sync")
                def _(eng, st, oc=oc, tt=tt):
                    st.dma(xoutT[128 * oc:128 * (oc + 1), 512 * tt:512 * (tt + 1)],
                           xao[:, :])



_EXT = [
    ("xT", [C, TPC], F32), ("posoff", [128, 1], F32),
    ("wq", [C, 1024], F32), ("wk", [C, 256], F32), ("wvv", [C, 256], F32),
    ("wvg", [C, 256], F32), ("wqi", [C, 256], F32), ("wki", [C, 64], F32),
    ("wwi", [C, 4], F32), ("wog", [C, 1024], F32),
    ("bvg", [1, 256], F32), ("bog", [1, 1024], F32),
    ("bi_in", [1, 4], F32), ("wo", [1024, 1024], F32),
    ("gwT", [C, 16], F32), ("ebias", [1, 16], F32),
    ("wgT", [16, C, 1024], F32), ("wuT", [16, C, 1024], F32),
    ("wdT", [16, 1024, C], F32),
    ("sgT", [C, 1024], F32), ("suT", [C, 1024], F32), ("sdT", [1024, C], F32),
    ("wkv", [C, 2048], F32), ("bkv", [1, 2048], F32),
    ("wq1", [C, 1024], F32), ("bq1", [1, 1024], F32),
    ("wp1", [C, 1024], F32), ("bp1", [1, 1024], F32),
    ("wgn", [I_L1, C], F32), ("wun", [I_L1, C], F32), ("wdn", [C, I_L1], F32),
]
_INT = [
    ("kv_loc", [TPC, 576], BF16), ("kv_all", [B * L, 576], BF16),
    ("qrT", [C, TPC], BF16), ("qiT", [256, TPC], BF16),
    ("g1T", [C, TPC], BF16), ("xnT", [C, TPC], BF16), ("wI_tok", [TPC, 4], F32),
    ("xAT", [C, TPC], F32), ("ctrD", [16, TPC], F32),
    ("x1T", [C, TPC], F32), ("xn1T", [C, TPC], BF16), ("xn1F", [C, TPC], F32),
    ("xmax_in", [1, 1], F32), ("xmax_g", [NCORE, 1], F32),
    ("xBT", [C, TPC], F32),
    ("upD", [I_L1, TPC], BF16), ("gateD", [I_L1, TPC], BF16),
    ("m1_in", [1, 2], F32), ("m1_g", [NCORE, 2], F32),
    ("upqD", [I_L1, TPC], BF16), ("g2D", [I_L1, TPC], BF16),
    ("mg2_in", [1, 1], F32), ("mg2_g", [NCORE, 1], F32),
    ("oD", [I_L1, TPC], BF16), ("mo_in", [1, 1], F32), ("mo_g", [NCORE, 1], F32),
]


def build_full():
    nc = bass.Bass("TRN2", target_bir_lowering=False)
    io, ioh = {}, {}
    for name, shape, dt in _EXT:
        io[name] = nc.dram_tensor(name, shape, dt, kind="ExternalInput").ap()
    io["xoutT"] = nc.dram_tensor("xoutT", [C, TPC], F32, kind="ExternalOutput").ap()
    for name, shape, dt in _INT:
        h = nc.dram_tensor(name, shape, dt)
        ioh[name] = h
        io[name] = h.ap()
    with nc.Block() as blk:
        sq = Seq(nc)
        emit_A(nc, blk, sq, io)
        sq.collective(blk, ioh["kv_loc"], ioh["kv_all"])
        emit_B(nc, blk, sq, io)
        emit_C(nc, blk, sq, io)
        sq.collective(blk, ioh["xmax_in"], ioh["xmax_g"])
        emit_D(nc, blk, sq, io)
        emit_E1(nc, blk, sq, io)
        sq.collective(blk, ioh["m1_in"], ioh["m1_g"])
        emit_E2(nc, blk, sq, io)
        sq.collective(blk, ioh["mg2_in"], ioh["mg2_g"])
        emit_E3(nc, blk, sq, io)
        sq.collective(blk, ioh["mo_in"], ioh["mo_g"])
        emit_E4(nc, blk, sq, io)

        @sq.step(blk, "sync")
        def _(eng, st):
            return eng.nop()

    return nc


def _shard_x_T(x):
    out = []
    for c in range(NCORE):
        sl = x[:, 512 * c:512 * (c + 1), :]
        sl = sl.reshape(TPC, C).T
        out.append(np.ascontiguousarray(sl, dtype=np.float32))
    return out


def kernel(**inputs):
    inputs = {k: np.asarray(v) for k, v in inputs.items()}
    if "full" not in _prog_cache:
        _prog_cache["full"] = build_full()
    nc = _prog_cache["full"]
    f32 = lambda a: np.ascontiguousarray(np.asarray(a, np.float32))
    tr = lambda a, ax: np.ascontiguousarray(
        np.transpose(np.asarray(a, np.float32), ax))
    base = {
        "wq": f32(inputs["L0_attn_Wq"]),
        "wk": f32(inputs["L0_attn_Wk"]),
        "wvv": f32(inputs["L0_attn_Wv"]),
        "wvg": f32(inputs["L0_attn_Wvg"]),
        "wqi": f32(inputs["L0_attn_Wqi"]),
        "wki": f32(inputs["L0_attn_Wki"]),
        "wwi": f32(inputs["L0_attn_Wwi"]),
        "wog": f32(inputs["L0_attn_Wog"]),
        "bvg": f32(inputs["L0_attn_bvg"]).reshape(1, 256),
        "bog": f32(inputs["L0_attn_bog"]).reshape(1, 1024),
        "bi_in": f32(inputs["L0_attn_bi"]).reshape(1, 4),
        "wo": f32(inputs["L0_attn_Wo"]),
        "gwT": tr(inputs["L0_ffn_gate_w"], (1, 0)),
        "ebias": f32(inputs["L0_ffn_expert_bias"]).reshape(1, 16),
        "wgT": tr(inputs["L0_ffn_w_gate"], (0, 2, 1)),
        "wuT": tr(inputs["L0_ffn_w_up"], (0, 2, 1)),
        "wdT": tr(inputs["L0_ffn_w_down"], (0, 2, 1)),
        "sgT": tr(inputs["L0_ffn_shared_gate_w"], (1, 0)),
        "suT": tr(inputs["L0_ffn_shared_up_w"], (1, 0)),
        "sdT": tr(inputs["L0_ffn_shared_down_w"], (1, 0)),
        "wkv": f32(inputs["L1_attn_Wkv"]),
        "bkv": f32(inputs["L1_attn_bkv"]).reshape(1, 2048),
        "wq1": f32(inputs["L1_attn_Wq"]),
        "bq1": f32(inputs["L1_attn_bq"]).reshape(1, 1024),
        "wp1": f32(inputs["L1_attn_Wp"]),
        "bp1": f32(inputs["L1_attn_bp"]).reshape(1, 1024),
        "wgn": f32(inputs["L1_ffn_w_gate"]),
        "wun": f32(inputs["L1_ffn_w_up"]),
        "wdn": f32(inputs["L1_ffn_w_down"]),
    }
    xs = _shard_x_T(inputs["L0_attn_hidden_states"])
    in_maps = [dict(base, xT=xs[c],
                    posoff=np.full((128, 1), 512.0 * c, np.float32))
               for c in range(NCORE)]
    res = run_bass_kernel_spmd(nc, in_maps, core_ids=list(range(NCORE)))
    out = np.empty((B, L, C), np.float32)
    for c in range(NCORE):
        xo = np.asarray(res.results[c]["xoutT"])
        out[:, 512 * c:512 * (c + 1), :] = xo.T.reshape(B, 512, C)
    return out
